# version 01
# baseline (speedup 1.0000x reference)
import numpy as np
import concourse.bass as bass
import concourse.mybir as mybir
from concourse.bass_utils import run_bass_kernel_spmd

F32 = mybir.dt.float32
BF16 = mybir.dt.bfloat16
AF = mybir.ActivationFunctionType
OP = mybir.AluOpType
AX = mybir.AxisListType.X

P = 128
D = 1024
T = 4096
TQ = 2048
H = 16
NE = 4
QB = 512
NQB = TQ // QB
NST = T // P
EPS = 1e-5
QSCALE = 1.0 / 32.0

DEBUG_OUTS = ()


class Prog:
    ENGINES = ("sync", "gpsimd", "vector", "scalar", "tensor")

    def __init__(self, nc):
        self.nc = nc
        self.ops = {e: [] for e in self.ENGINES}
        self.sems = {}
        self.cnt = {}

    def sem(self, name):
        if name not in self.sems:
            self.sems[name] = self.nc.alloc_semaphore(name)
            self.cnt[name] = 0
        return self.sems[name]

    def inc(self, name, k=1):
        h = self.sem(name)
        self.cnt[name] += k
        return h, self.cnt[name]

    def val(self, name):
        self.sem(name)
        return self.cnt[name]

    def emit(self, engine, fn):
        self.ops[engine].append(fn)

    def wait(self, engine, name, value):
        if value <= 0:
            return
        h = self.sem(name)
        self.emit(engine, lambda eng, h=h, v=value: eng.wait_ge(h, v))

    def mm(self, out, lhsT, rhs, start, stop, inc=None, tile_position=None):
        if inc is None:
            self.emit("tensor", lambda t, o=out, l=lhsT, r=rhs, s0=start, s1=stop, tp=tile_position:
                      t.matmul(o, l, r, start=s0, stop=s1, tile_position=tp))
            return None
        h, v = self.inc(inc)
        self.emit("tensor", lambda t, o=out, l=lhsT, r=rhs, s0=start, s1=stop, h=h, tp=tile_position:
                  t.matmul(o, l, r, start=s0, stop=s1, tile_position=tp).then_inc(h, 1))
        return v

    def tr(self, out, in_, ident, inc=None):
        if inc is None:
            self.emit("tensor", lambda t, o=out, i=in_, d=ident: t.transpose(o, i, d))
            return None
        h, v = self.inc(inc)
        self.emit("tensor", lambda t, o=out, i=in_, d=ident, h=h: t.transpose(o, i, d).then_inc(h, 1))
        return v

    def ve(self, fname, *args, inc=None, **kwargs):
        if inc is None:
            self.emit("vector", lambda v, f=fname, a=args, k=kwargs: getattr(v, f)(*a, **k))
            return None
        h, val = self.inc(inc)
        self.emit("vector", lambda v, f=fname, a=args, k=kwargs, h=h:
                  getattr(v, f)(*a, **k).then_inc(h, 1))
        return val

    def se(self, fname, *args, inc=None, **kwargs):
        if inc is None:
            self.emit("scalar", lambda s, f=fname, a=args, k=kwargs: getattr(s, f)(*a, **k))
            return None
        h, val = self.inc(inc)
        self.emit("scalar", lambda s, f=fname, a=args, k=kwargs, h=h:
                  getattr(s, f)(*a, **k).then_inc(h, 1))
        return val

    def drain(self, engine):
        self.emit(engine, lambda eng: eng.drain())

    def dma(self, q, dst, src, sem):
        h, v = self.inc(sem, 16)
        self.emit(q, lambda eng, d=dst, s=src, h=h: eng.dma_start(d, s).then_inc(h, 16))
        return v

    def run(self):
        with self.nc.Block() as blk:
            for e in self.ENGINES:
                getattr(blk, e)(self._runner(e))

    def _runner(self, e):
        def f(eng):
            for op in self.ops[e]:
                op(eng)
        return f


def ceil32(x):
    return (x + 31) & ~31


class Arena:
    def __init__(self, nc, name, nbytes):
        self.nbytes = nbytes
        self.t = nc.alloc_sbuf_tensor(name, [P, nbytes // 2], BF16)
        self.ap = self.t.ap()
        self.off = 0
        self.marks = []
        self.hwm = 0

    def view(self, nelem, dtype, shape=None):
        esz = 2 if dtype == BF16 else 4
        off = ceil32(self.off)
        self.off = off + nelem * esz
        self.hwm = max(self.hwm, self.off)
        assert self.off <= self.nbytes, f"arena overflow {self.off} > {self.nbytes}"
        v = self.ap[:, off // 2: (off + nelem * esz) // 2]
        if dtype != BF16:
            v = v.bitcast(dtype)
        if shape is not None:
            assert int(np.prod(shape)) == nelem
            if len(shape) == 2:
                v = v.rearrange("p (a b) -> p a b", b=shape[1])
        return v

    def push(self):
        self.marks.append(self.off)

    def pop(self):
        self.off = self.marks.pop()


def build(debug_outs=DEBUG_OUTS, phases=("A", "ATT", "FFN")):
    nc = bass.Bass("TRN2", target_bir_lowering=False)
    pr = Prog(nc)

    xkv = nc.dram_tensor("xkv", [T, D], F32, kind="ExternalInput").ap()
    xq = nc.dram_tensor("xq", [TQ, D], F32, kind="ExternalInput").ap()
    wmat = {n: nc.dram_tensor(n, [D, D], F32, kind="ExternalInput").ap()
            for n in ("wq", "wk", "wv", "wu")}
    bu = nc.dram_tensor("bu", [1, D], F32, kind="ExternalInput").ap()
    gw = nc.dram_tensor("gw", [D, NE], F32, kind="ExternalInput").ap()
    gb = nc.dram_tensor("gb", [1, NE], F32, kind="ExternalInput").ap()
    w1 = nc.dram_tensor("w1", [NE, D, D], F32, kind="ExternalInput").ap()
    b1 = nc.dram_tensor("b1", [NE, D], F32, kind="ExternalInput").ap()
    g1 = nc.dram_tensor("g1", [NE, D], F32, kind="ExternalInput").ap()
    be1 = nc.dram_tensor("be1", [NE, D], F32, kind="ExternalInput").ap()
    w2 = nc.dram_tensor("w2", [NE, D, D], F32, kind="ExternalInput").ap()
    b2 = nc.dram_tensor("b2", [NE, D], F32, kind="ExternalInput").ap()
    g2 = nc.dram_tensor("g2", [NE, D], F32, kind="ExternalInput").ap()
    be2 = nc.dram_tensor("be2", [NE, D], F32, kind="ExternalInput").ap()
    ident_in = nc.dram_tensor("ident_in", [P, P], F32, kind="ExternalInput").ap()

    out = nc.dram_tensor("out", [TQ, D], F32, kind="ExternalOutput").ap()

    def scratch(name, shape, dtype):
        kind = "ExternalOutput" if name in debug_outs else "Internal"
        return nc.dram_tensor(name, shape, dtype, kind=kind).ap()

    kT_s = scratch("kT_s", [D, T], BF16)
    qT_s = scratch("qT_s", [D, TQ], BF16)
    v_s = scratch("v_s", [H // 2, T, P], BF16)
    xf_s = scratch("xf_s", [TQ, D], F32)
    dbg_rs = scratch("dbg_rs", [NQB, H, QB], F32)
    dbg_o = scratch("dbg_o", [NQB, P, 8 * QB], BF16)
    dbg_u = scratch("dbg_u", [NQB, 4, P, D], F32)
    dbg_st = scratch("dbg_st", [NQB, 4, P, 24], F32)
    dbg_w = scratch("dbg_w", [2, P, 32], F32)
    dbg_h = scratch("dbg_h", [2, P, D], BF16)
    dbg_acc = scratch("dbg_acc", [5, P, 8 * D], F32)
    dbg_z = scratch("dbg_z", [2, P, D], F32)
    dbg_wb = scratch("dbg_wb", [P, 8 * D], BF16)

    ident = nc.alloc_sbuf_tensor("ident", [P, P], BF16).ap()
    ones1 = nc.alloc_sbuf_tensor("ones1", [1, P], BF16).ap()
    wuT = nc.alloc_sbuf_tensor("wuT", [P, 8 * D], BF16).ap()
    bu_bf = nc.alloc_sbuf_tensor("bu_bf", [1, D], BF16).ap()
    xfT = nc.alloc_sbuf_tensor("xfT", [P, 8 * TQ], BF16).ap()
    gw_bf = nc.alloc_sbuf_tensor("gw_bf", [P, 8 * NE], BF16).ap()
    gb_bf = nc.alloc_sbuf_tensor("gb_bf", [1, NE], BF16).ap()
    wsum_f = nc.alloc_sbuf_tensor("wsum_f", [P, 16], F32).ap()
    wsum_bf = nc.alloc_sbuf_tensor("wsum_bf", [P, 16], BF16).ap()
    bsum_bf = nc.alloc_sbuf_tensor("bsum_bf", [1, 16], BF16).ap()
    bsum_f = nc.alloc_sbuf_tensor("bsum_f", [1, 16], F32).ap()
    st_negm = nc.alloc_sbuf_tensor("st_negm", [P, 4], F32).ap()
    st_tmp = nc.alloc_sbuf_tensor("st_tmp", [P, 8], F32).ap()
    st_sq = nc.alloc_sbuf_tensor("st_sq", [P, 4], F32).ap()
    st_std = nc.alloc_sbuf_tensor("st_std", [P, 4], F32).ap()
    st_rstd = nc.alloc_sbuf_tensor("st_rstd", [P, 4], F32).ap()
    wgate = nc.alloc_sbuf_tensor("wgate", [P, 32], F32).ap()
    wgate_bf = nc.alloc_sbuf_tensor("wgate_bf", [P, 32], BF16).ap()
    wTg = [nc.alloc_sbuf_tensor(f"wTg{i}", [NE, P], BF16).ap() for i in range(8)]
    gm1 = nc.alloc_sbuf_tensor("gm1", [P, 8], F32).ap()
    gm2 = nc.alloc_sbuf_tensor("gm2", [P, 8], F32).ap()
    gden = nc.alloc_sbuf_tensor("gden", [P, 8], F32).ap()
    gtmp = nc.alloc_sbuf_tensor("gtmp", [P, 32], F32).ap()

    psum = nc.alloc_psum_tensor("psum", [P, 4096], F32).ap()

    def psv(c0, c1, r0=0, r1=P):
        return psum[r0:r1, c0:c1]

    def psbf(c0, c1):
        return psum[:, c0:c1].bitcast(BF16)

    AR = Arena(nc, "arena", 140 * 1024)

    pr.dma("gpsimd", ident, ident_in, "g_ld")
    pr.ve("memset", ones1, 1.0, inc="v_misc")

    AR.push()
    wnat = AR.view(8 * D, BF16)
    wTq = AR.view(8 * D, BF16)
    wTk = AR.view(8 * D, BF16)
    wTv = AR.view(8 * D, BF16)
    xnat = [AR.view(4 * D, BF16) for _ in range(2)]
    xT = [AR.view(8 * QB, BF16) for _ in range(2)]
    stg = [AR.view(QB, BF16) for _ in range(4)]

    def transpose_block(srcs, dst_fn):
        pr.wait("tensor", "v_tr", pr.val("v_tr"))
        for i, s in enumerate(srcs):
            pr.tr(psbf(3072, 3584)[:, i * P:(i + 1) * P], s, ident,
                  inc="t_tr" if i == 7 else None)
        pr.wait("vector", "t_tr", pr.val("t_tr"))
        dst_fn(psbf(3072, 3584))

    for wname, dst in (("wq", wTq), ("wk", wTk), ("wv", wTv), ("wu", wuT)):
        pr.wait("gpsimd", "t_tr", pr.val("t_tr"))
        ldv = pr.dma("gpsimd", wnat.rearrange("p (a d) -> p a d", d=D),
                     wmat[wname].rearrange("(a p) d -> p a d", p=P), "g_ld")
        pr.wait("tensor", "g_ld", ldv)
        for j in range(8):
            srcs = [wnat[:, i * D + j * P: i * D + (j + 1) * P] for i in range(8)]
            def ev(pv, j=j, dst=dst):
                pr.ve("tensor_copy", dst[:, j * D:(j + 1) * D], pv, inc="v_tr")
            transpose_block(srcs, ev)

    pr.dma("gpsimd", gw_bf.rearrange("p (a e) -> p a e", e=NE),
           gw.rearrange("(a p) e -> p a e", p=P), "g_ld")
    pr.dma("gpsimd", gb_bf, gb, "g_ld")

    projps = [(i * 512, (i + 1) * 512) for i in range(4)]
    pp = {"i": 0}
    evac_cnt = {i: 0 for i in range(4)}
    chunk_end_tpj = []

    def proj_tile(lhs_rhs_fn, stgbuf, dsts, scale=None):
        bank = pp["i"] % 4
        pp["i"] += 1
        c0, c1 = projps[bank]
        pr.wait("tensor", "v_pa", evac_cnt[bank])
        for c in range(8):
            lhsT, rhs = lhs_rhs_fn(c)
            pr.mm(psv(c0, c1), lhsT, rhs, c == 0, c == 7,
                  inc="t_pj" if c == 7 else None)
        pr.wait("vector", "t_pj", pr.val("t_pj"))
        if scale is None:
            v2 = pr.ve("tensor_copy", stgbuf, psv(c0, c1), inc="v_pa")
        else:
            v2 = pr.ve("tensor_scalar_mul", stgbuf, psv(c0, c1), scale, inc="v_pa")
        evac_cnt[bank] = v2
        pr.wait("sync", "v_pa", v2)
        for dst_ap, src_ap in dsts:
            pr.dma("sync", dst_ap, src_ap, "s_st")

    def do_chunk(src_rows, ct0, emit_k, emit_v, emit_q, buf_i):
        ct_cols = slice(ct0, ct0 + 512)
        xb, xtb = xnat[buf_i], xT[buf_i]
        if len(chunk_end_tpj) >= 2:
            pr.wait("gpsimd", "t_pj", chunk_end_tpj[-2])
        ldv = pr.dma("gpsimd", xb.rearrange("p (a d) -> p a d", d=D),
                     src_rows.rearrange("(a p) d -> p a d", p=P), "g_ld")
        pr.wait("tensor", "g_ld", ldv)
        for tt in range(4):
            srcs = [xb[:, tt * D + j * P: tt * D + (j + 1) * P] for j in range(8)]
            def ev(pv, tt=tt, xtb=xtb):
                xv = xtb.rearrange("p (j c) -> p j c", c=512)[:, :, tt * P:(tt + 1) * P]
                pvv = pv.rearrange("p (j c) -> p j c", c=P)
                pr.ve("tensor_copy", xv, pvv, inc="v_tr")
            transpose_block(srcs, ev)
        pr.wait("tensor", "v_tr", pr.val("v_tr"))
        if emit_k:
            for j in range(8):
                sb = stg[j % 4]
                pr.wait("vector", "s_st", pr.val("s_st"))
                proj_tile(lambda c, j=j, xtb=xtb: (wTk[:, c * D + j * P:c * D + (j + 1) * P],
                                                   xtb[:, c * 512:(c + 1) * 512]),
                          sb, [(kT_s[j * P:(j + 1) * P, ct_cols], sb)])
        if emit_v:
            for tt in range(4):
                for nb in range(2):
                    sb = stg[(tt * 2 + nb) % 4]
                    pr.wait("vector", "s_st", pr.val("s_st"))
                    tok0 = ct0 + tt * P
                    dsts = []
                    for pq in range(4):
                        pair = nb * 4 + pq
                        dsts.append((v_s[pair, tok0:tok0 + P, :],
                                     sb.rearrange("p (a c) -> p a c", c=P)[:, pq, :]))
                    proj_tile(lambda c, tt=tt, nb=nb, xtb=xtb: (
                        xtb[:, c * 512 + tt * P:c * 512 + (tt + 1) * P],
                        wTv[:, c * D + nb * 512:c * D + (nb + 1) * 512]),
                        sb, dsts)
        if emit_q:
            for j in range(8):
                sb = stg[j % 4]
                pr.wait("vector", "s_st", pr.val("s_st"))
                proj_tile(lambda c, j=j, xtb=xtb: (wTq[:, c * D + j * P:c * D + (j + 1) * P],
                                                   xtb[:, c * 512:(c + 1) * 512]),
                          sb, [(qT_s[j * P:(j + 1) * P, ct_cols], sb)],
                          scale=QSCALE)
        chunk_end_tpj.append(pr.val("t_pj"))

    if "A" in phases:
        for ct in range(8):
            do_chunk(xkv[ct * 512:(ct + 1) * 512, :], ct * 512,
                     emit_k=True, emit_v=True, emit_q=False, buf_i=ct % 2)
        for ct in range(4):
            do_chunk(xq[ct * 512:(ct + 1) * 512, :], ct * 512,
                     emit_k=False, emit_v=False, emit_q=True, buf_i=ct % 2)

    AR.pop()
    phaseA_store = pr.val("s_st")

    AR.push()
    KT = [AR.view(T, BF16) for _ in range(2)]
    Vq = [AR.view(32 * 130, BF16) for _ in range(2)]
    QTb = [AR.view(8 * QB, BF16) for _ in range(2)]
    attA = AR.view(16 * QB, BF16)
    attB = AR.view(16 * QB, BF16)
    O_sbU = AR.view(8 * QB, BF16)
    tmpB = AR.view(QB, BF16)
    rs_st = AR.view(2 * QB, F32)
    coll = AR.view(QB, F32)
    rcoll = AR.view(QB, F32)
    rstbf = AR.view(2 * QB, BF16)
    xres = [AR.view(D, F32) for _ in range(2)]
    xfst = [AR.view(D, F32) for _ in range(2)]
    xfn = [AR.view(D, F32) for _ in range(2)]
    xfbf = [AR.view(D, BF16) for _ in range(2)]
    sqd = AR.view(D, BF16)

    SCA = (0, 1536)
    SCB = (1536, 3072)
    POA = (3072, 3584)
    POB = (3584, 4096)
    BCA = (0, 512)
    BCB = (512, 1024)
    PU = (1536, 2560)

    if "ATT" in phases:
        ldv = pr.dma("gpsimd", bu_bf, bu, "g_ld")
        for i in range(2):
            pr.ve("memset", Vq[i], 1.0, inc="v_misc")
        vq_ready = pr.val("v_misc")

        GROUPS = [3, 3, 3, 3, 2, 2]
        pair_end_tav = []
        qb_end_marks = []
        stt_marks = []
        sxf_marks = []
        xfb_marks = []
        sq_marks = []

        for qb in range(NQB):
            qtb = QTb[qb % 2]
            if qb >= 2:
                pr.wait("sync", "t_sc", qb_end_marks[qb - 2]["t_sc"])
            pr.dma("sync",
                   qtb.rearrange("p (j c) -> p j c", c=QB),
                   qT_s.rearrange("(j p) q -> p j q", p=P)[:, :, qb * QB:(qb + 1) * QB],
                   "a_q")
            qtb_ld = pr.val("a_q")

            for p in range(8):
                gi = len(pair_end_tav)
                kt, vq = KT[gi % 2], Vq[gi % 2]
                if gi >= 2:
                    pr.wait("sync", "t_avh", pair_end_tav[gi - 2])
                pr.wait("sync", "v_misc", vq_ready)
                pr.dma("sync", kt, kT_s[p * P:(p + 1) * P, :], "a_kv")
                vsrc = v_s[p].rearrange("(s q) c -> q s c", q=P)
                vdst = vq.rearrange("p (s c) -> p s c", c=130)
                pr.dma("sync", vdst[:, :, 0:64], vsrc[:, :, 0:64], "a_kv")
                pr.dma("sync", vdst[:, :, 65:129], vsrc[:, :, 64:128], "a_kv")
                kv_ld = pr.val("a_kv")

                def emit_attv(item, kt=kt, vq=vq):
                    st0, glen, sexv = item
                    pr.wait("tensor", "s_ex", sexv)
                    for k in range(glen):
                        st = st0 + k
                        slot = (st % 16) * QB
                        if st == 0:
                            pr.wait("tensor", "v_oe", pr.val("v_oe"))
                            pr.wait("tensor", "v_tr", pr.val("v_tr"))
                        fin = (st % 16 == 15) or (st == NST - 1)
                        pr.mm(psv(*POA, 0, 65), vq[:, st * 130: st * 130 + 65],
                              attA[:, slot:slot + QB], st == 0, st == NST - 1)
                        pr.mm(psv(*POB, 0, 65), vq[:, st * 130 + 65: st * 130 + 130],
                              attB[:, slot:slot + QB], st == 0, st == NST - 1,
                              inc="t_avh" if fin else None)

                pend = []
                for half in range(2):
                    for g, glen in enumerate(GROUPS):
                        st0 = half * 16 + sum(GROUPS[:g])
                        pr.wait("tensor", "s_ex", pr.val("s_ex"))
                        if g == 0 and half == 0:
                            pr.wait("tensor", "a_kv", kv_ld)
                            pr.wait("tensor", "a_q", qtb_ld)
                            pr.wait("tensor", "v_no", pr.val("v_no"))
                            pr.wait("tensor", "v_u", pr.val("v_u"))
                        for k in range(glen):
                            st = st0 + k
                            last = (k == glen - 1)
                            pr.mm(psv(*SCA)[:, k * QB:(k + 1) * QB],
                                  kt[0:64, st * P:(st + 1) * P],
                                  qtb[0:64, p * QB:(p + 1) * QB],
                                  True, True, tile_position=(0, 0))
                            pr.mm(psv(*SCB)[:, k * QB:(k + 1) * QB],
                                  kt[64:128, st * P:(st + 1) * P],
                                  qtb[64:128, p * QB:(p + 1) * QB],
                                  True, True, inc="t_sc" if last else None,
                                  tile_position=(64, 0))
                        pr.wait("scalar", "t_sc", pr.val("t_sc"))
                        if g == 0:
                            pr.wait("scalar", "t_avh", pr.val("t_avh"))
                        sl0 = (st0 % 16) * QB
                        pr.se("activation", attA[:, sl0:sl0 + glen * QB],
                              psv(*SCA)[:, 0:glen * QB], AF.Exp, inc="s_ex")
                        pr.se("activation", attB[:, sl0:sl0 + glen * QB],
                              psv(*SCB)[:, 0:glen * QB], AF.Exp, inc="s_ex")
                        pend.append((st0, glen, pr.val("s_ex")))
                        if len(pend) >= 2:
                            emit_attv(pend.pop(0))
                    while pend:
                        emit_attv(pend.pop(0))
                pair_end_tav.append(pr.val("t_avh"))

                pr.wait("vector", "t_avh", pair_end_tav[-1])
                if gi >= 1:
                    pr.wait("vector", "g_rs", pr.val("g_rs"))
                    pr.wait("vector", "g_mb", pr.val("g_mb"))
                pr.ve("tensor_copy", O_sbU[0:64, p * QB:(p + 1) * QB],
                      psv(*POA, 0, 65)[0:64, :], inc="v_oe")
                pr.ve("tensor_copy", rs_st[64:65, 0:QB], psv(*POA, 64, 65), inc="v_oe")
                pr.ve("tensor_copy", tmpB[0:64, :], psv(*POB, 0, 65)[0:64, :], inc="v_oe")
                pr.ve("tensor_copy", rs_st[64:65, QB:2 * QB], psv(*POB, 64, 65), inc="v_oe")
                voe = pr.val("v_oe")
                pr.wait("gpsimd", "v_oe", voe)
                pr.dma("gpsimd", coll[2 * p:2 * p + 2, :],
                       rs_st[64:65, :].rearrange("p (a c) -> p a c", c=QB), "g_rs")
                pr.dma("gpsimd", O_sbU[64:128, p * QB:(p + 1) * QB], tmpB[0:64, :], "g_mb")

            pr.wait("vector", "g_rs", pr.val("g_rs"))
            pr.ve("reciprocal", rcoll[0:16, :], coll[0:16, :], inc="v_rc")
            for p in range(8):
                pr.wait("gpsimd", "v_rc", pr.val("v_rc"))
                if p >= 1:
                    pr.wait("gpsimd", "t_bc", pr.val("t_bc"))
                pr.dma("gpsimd", rstbf[0:1, :].rearrange("p (a c) -> p a c", c=QB),
                       rcoll[2 * p:2 * p + 2, :], "g_sc")
                pr.wait("tensor", "g_sc", pr.val("g_sc"))
                pr.wait("tensor", "s_ex", pr.val("s_ex"))
                pr.wait("tensor", "v_no", pr.val("v_no"))
                pr.mm(psv(*BCA, 0, 64), ones1[0:1, 0:64], rstbf[0:1, 0:QB],
                      True, True, inc="t_bc", tile_position=(0, 0))
                pr.mm(psv(*BCB, 64, 128), ones1[0:1, 0:64], rstbf[0:1, QB:2 * QB],
                      True, True, inc="t_bc", tile_position=(0, 64))
                pr.wait("vector", "t_bc", pr.val("t_bc"))
                pr.wait("vector", "g_mb", pr.val("g_mb"))
                blk_ = O_sbU[0:64, p * QB:(p + 1) * QB]
                pr.ve("tensor_mul", blk_, blk_, psv(*BCA, 0, 64), inc="v_no")
                blk2 = O_sbU[64:128, p * QB:(p + 1) * QB]
                pr.ve("tensor_mul", blk2, blk2, psv(*BCB, 64, 128), inc="v_no")

            if "dbg_rs" in debug_outs:
                pr.wait("sync", "v_rc", pr.val("v_rc"))
                pr.dma("sync", dbg_rs[qb], coll[0:16, :], "s_dbg")
            if "dbg_o" in debug_outs:
                pr.wait("sync", "v_no", pr.val("v_no"))
                pr.dma("sync", dbg_o[qb], O_sbU, "s_dbg")

            vno = pr.val("v_no")
            for tt in range(4):
                xr = xres[tt % 2]
                if len(stt_marks) >= 2:
                    pr.wait("sync", "v_u", stt_marks[-2])
                pr.dma("sync", xr, xq[qb * QB + tt * P: qb * QB + (tt + 1) * P, :], "s_xr")
                sxr = pr.val("s_xr")
                if tt == 0:
                    pr.wait("tensor", "v_no", vno)
                    if qb == 0:
                        pr.wait("tensor", "g_ld", ldv)
                if stt_marks:
                    pr.wait("tensor", "v_u", stt_marks[-1])
                for nb in range(2):
                    c0 = PU[0] + nb * QB
                    for j in range(8):
                        pr.mm(psv(c0, c0 + QB), O_sbU[:, j * QB + tt * P:j * QB + (tt + 1) * P],
                              wuT[:, j * D + nb * QB:j * D + (nb + 1) * QB],
                              j == 0, False)
                    pr.mm(psv(c0, c0 + QB), ones1[0:1, 0:P], bu_bf[0:1, nb * QB:(nb + 1) * QB],
                          False, True, inc="t_u" if nb == 1 else None)
                s_ = tt % 2
                pr.wait("vector", "t_u", pr.val("t_u"))
                pr.wait("vector", "s_xr", sxr)
                if len(sq_marks) >= 2:
                    pr.wait("vector", "s_u", sq_marks[-2])
                vstt = pr.ve("scalar_tensor_tensor", xfst[s_], psv(PU[0], PU[0] + D), 1.0, xr,
                             OP.mult, OP.add, accum_out=st_tmp[:, s_:s_ + 1], inc="v_u")
                stt_marks.append(vstt)
                if "dbg_u" in debug_outs:
                    pr.wait("sync", "v_u", pr.val("v_u"))
                    pr.dma("sync", dbg_u[qb, tt], xfst[s_], "s_dbg")
                pr.wait("scalar", "v_u", pr.val("v_u"))
                pr.se("activation", sqd, xfst[s_], AF.Square,
                      accum_out=st_sq[:, s_:s_ + 1])
                pr.se("activation", st_negm[:, s_:s_ + 1], st_tmp[:, s_:s_ + 1],
                      AF.Identity, scale=-1.0 / D)
                pr.se("activation", st_tmp[:, 2 + s_:3 + s_], st_negm[:, s_:s_ + 1],
                      AF.Square, inc="s_u")
                pr.wait("vector", "s_u", pr.val("s_u"))
                pr.ve("tensor_copy", st_tmp[:, 6 + s_:7 + s_], st_sq[:, s_:s_ + 1])
                pr.ve("tensor_scalar", st_tmp[:, 4 + s_:5 + s_], st_tmp[:, 2 + s_:3 + s_],
                      -1.0, EPS, OP.mult, OP.add, inc="v_u")
                pr.wait("scalar", "v_u", pr.val("v_u"))
                sq_marks.append(pr.se("activation", st_std[:, s_:s_ + 1],
                                      st_tmp[:, 6 + s_:7 + s_],
                                      AF.Sqrt, bias=st_tmp[:, 4 + s_:5 + s_], scale=1.0 / D,
                                      inc="s_u"))
                pr.wait("vector", "s_u", pr.val("s_u"))
                pr.ve("reciprocal", st_rstd[:, s_:s_ + 1], st_std[:, s_:s_ + 1])
                pr.drain("vector")
                if len(sxf_marks) >= 2:
                    pr.wait("vector", "s_xf", sxf_marks[-2])
                if len(xfb_marks) >= 2:
                    pr.wait("vector", "t_tr", xfb_marks[-2])
                pr.ve("tensor_scalar", xfn[s_], xfst[s_], st_negm[:, s_:s_ + 1],
                      st_rstd[:, s_:s_ + 1], OP.add, OP.mult, inc="v_u")
                pr.ve("tensor_copy", xfbf[s_], xfn[s_], inc="v_u")
                vu = pr.val("v_u")
                pr.wait("sync", "v_u", vu)
                pr.dma("sync", xf_s[qb * QB + tt * P: qb * QB + (tt + 1) * P, :],
                       xfn[s_], "s_xf")
                if "dbg_st" in debug_outs:
                    pr.dma("sync", dbg_st[qb, tt, :, 0:8], st_tmp, "s_dbg")
                    pr.dma("sync", dbg_st[qb, tt, :, 8:12], st_sq, "s_dbg")
                    pr.dma("sync", dbg_st[qb, tt, :, 12:16], st_std, "s_dbg")
                    pr.dma("sync", dbg_st[qb, tt, :, 16:20], st_rstd, "s_dbg")
                    pr.dma("sync", dbg_st[qb, tt, :, 20:24], st_negm, "s_dbg")
                pr.wait("tensor", "v_u", vu)
                srcs = [xfbf[s_][:, j * P:(j + 1) * P] for j in range(8)]
                def evx(pv, tt=tt, qb=qb):
                    xview = xfT.rearrange("p (j c) -> p j c", c=TQ)[
                        :, :, qb * QB + tt * P: qb * QB + (tt + 1) * P]
                    pvv = pv.rearrange("p (j c) -> p j c", c=P)
                    pr.ve("tensor_copy", xview, pvv, inc="v_tr")
                transpose_block(srcs, evx)
                sxf_marks.append(pr.val("s_xf"))
                xfb_marks.append(pr.val("t_tr"))
            qb_end_marks.append({"t_sc": pr.val("t_sc")})

    att_end = {"s_xf": pr.val("s_xf") if "ATT" in phases else 0}
    AR.pop()

    AR.push()
    wbuf = [AR.view(8 * D, BF16) for _ in range(3)]
    g1t = AR.view(D, F32)
    be1t = AR.view(D, F32)
    g2t = AR.view(D, F32)
    hbf = [AR.view(D, BF16) for _ in range(2)]
    hTb = [AR.view(8 * P, BF16) for _ in range(2)]
    acc = AR.view(8 * D, F32)
    xfrd = [AR.view(D, F32) for _ in range(2)]
    t1b = [AR.view(D, F32) for _ in range(2)]
    u1b = [AR.view(D, F32)]
    outst = [AR.view(D, F32) for _ in range(2)]
    sqd2 = AR.view(D, BF16)
    b1r = [AR.view(D, BF16)]
    b2r = [AR.view(D, BF16)]
    be2sb = AR.view(D, BF16)
    gsc = AR.view(32, F32)
    gts = AR.view(6 * 8, F32)
    gtmp = AR.view(32, F32)
    st_sq2 = AR.view(8, F32)
    dbgz_sb = outst[0]
    st_s2 = AR.view(8, F32)

    PZ = [(0, 1024), (1024, 2048)]
    PO = (2048, 3072)
    SUMC = 3584
    GATC = 3592

    if "FFN" in phases:
        att_done = {n: pr.val(n) for n in ("t_u", "t_tr", "v_u", "s_xf", "v_tr")}
        for q_ in ("gpsimd", "sync"):
            pr.wait(q_, "t_tr", att_done["t_tr"])
            pr.wait(q_, "t_u", att_done["t_u"])
            pr.wait(q_, "s_xf", att_done["s_xf"])

        def load_expert_w(e, which, buf):
            srct = w1 if which == 0 else w2
            return pr.dma("gpsimd", wbuf[buf].rearrange("p (a d) -> p a d", d=D),
                          srct[e].rearrange("(a p) d -> p a d", p=P), "f_w")

        pr.dma("gpsimd", be2sb[0:4, :], be2, "f_w")
        ldw = {}
        z1_end = {}
        z2_end = {}
        ldw[0] = [load_expert_w(0, 0, 0), load_expert_w(0, 1, 1)]

        def load_gb(e, slot):
            m = pr.dma("gpsimd", g1t, g1[e:e + 1, :].partition_broadcast(P), "f_g")
            m = pr.dma("gpsimd", be1t, be1[e:e + 1, :].partition_broadcast(P), "f_g")
            m = pr.dma("gpsimd", g2t, g2[e:e + 1, :].partition_broadcast(P), "f_g")
            m = pr.dma("gpsimd", b1r[slot][0:1, :], b1[e:e + 1, :], "f_g")
            m = pr.dma("gpsimd", b2r[slot][0:1, :], b2[e:e + 1, :], "f_g")
            return m

        stt1_marks = {0: [], 1: []}
        fstats_mark = [0]
        o_free_mark = [0]

        def ln_chain(zsrc, k, sumsrc, g_t, be_t, out_apply):
            pr.se("activation", sqd2, zsrc, AF.Square, accum_out=st_sq[:, k:k + 1])
            pr.se("activation", st_negm[:, k:k + 1], sumsrc, AF.Identity, scale=-1.0 / D)
            pr.se("activation", st_tmp[:, k:k + 1], st_negm[:, k:k + 1], AF.Square,
                  inc="s_f")
            pr.wait("vector", "s_f", pr.val("s_f"))
            pr.ve("tensor_copy", st_sq2[:, k:k + 1], st_sq[:, k:k + 1])
            pr.ve("tensor_scalar", st_tmp[:, 4 + k:5 + k], st_tmp[:, k:k + 1],
                  -1.0, EPS, OP.mult, OP.add, inc="v_f")
            pr.wait("scalar", "v_f", pr.val("v_f"))
            pr.se("activation", st_std[:, k:k + 1], st_sq2[:, k:k + 1], AF.Sqrt,
                  bias=st_tmp[:, 4 + k:5 + k], scale=1.0 / D, inc="s_f")
            fstats_mark[0] = pr.val("s_f")
            pr.wait("vector", "s_f", pr.val("s_f"))
            pr.ve("reciprocal", st_rstd[:, k:k + 1], st_std[:, k:k + 1])
            pr.drain("vector")
            out_apply(st_negm[:, k:k + 1], st_rstd[:, k:k + 1])

        pr.wait("tensor", "v_u", pr.val("v_u"))
        pr.wait("tensor", "v_no", pr.val("v_no"))
        pr.wait("tensor", "v_oe", pr.val("v_oe"))
        pr.wait("tensor", "s_ex", pr.val("s_ex"))

        for ch in range(2):
            ch0 = ch * 8 * P

            pr.wait("tensor", "v_tr", pr.val("v_tr"))
            pr.wait("tensor", "g_ld", pr.val("g_ld"))
            pr.wait("tensor", "s_f", fstats_mark[0])
            for tt in range(8):
                for c in range(8):
                    pr.mm(psv(GATC + 4 * tt, GATC + 4 * tt + 4),
                          xfT[:, c * TQ + ch0 + tt * P: c * TQ + ch0 + (tt + 1) * P],
                          gw_bf[:, c * NE:(c + 1) * NE], c == 0, False)
                pr.mm(psv(GATC + 4 * tt, GATC + 4 * tt + 4), ones1[0:1, 0:P],
                      gb_bf[0:1, :], False, True, inc="t_g" if tt == 7 else None)
            pr.wait("vector", "t_g", pr.val("t_g"))
            pr.ve("tensor_copy", gsc, psv(GATC, GATC + 32))
            pr.drain("vector")
            g3 = gsc.rearrange("p (a e) -> p a e", e=NE)
            ga, gb_, gc, gd = (gts[:, 0:8], gts[:, 8:16], gts[:, 16:24], gts[:, 24:32])
            gm1v, gm2v = gts[:, 32:40], gts[:, 40:48]
            pr.ve("tensor_max", ga, g3[:, :, 0], g3[:, :, 1])
            pr.ve("tensor_tensor", gb_, g3[:, :, 0], g3[:, :, 1], OP.min)
            pr.ve("tensor_max", gc, g3[:, :, 2], g3[:, :, 3])
            pr.ve("tensor_tensor", gd, g3[:, :, 2], g3[:, :, 3], OP.min)
            pr.drain("vector")
            pr.ve("tensor_max", gm1v, ga, gc)
            pr.ve("tensor_tensor", ga, ga, gc, OP.min)
            pr.ve("tensor_max", gb_, gb_, gd)
            pr.drain("vector")
            pr.ve("tensor_max", gm2v, ga, gb_)
            pr.drain("vector")
            m1b = gm1v.unsqueeze(2).broadcast_to([P, 8, NE])
            m2b = gm2v.unsqueeze(2).broadcast_to([P, 8, NE])
            gt3 = gtmp.rearrange("p (a e) -> p a e", e=NE)
            pr.ve("tensor_sub", gt3, g3, m1b)
            pr.ve("tensor_sub", gd, gm2v, gm1v, inc="v_g")
            pr.drain("vector")
            pr.wait("scalar", "v_g", pr.val("v_g"))
            pr.se("activation", wgate, gtmp, AF.Exp)
            pr.se("activation", gts[:, 0:8], gd, AF.Exp, inc="s_g")
            pr.wait("vector", "s_g", pr.val("s_g"))
            pr.ve("tensor_scalar_add", gts[:, 16:24], gts[:, 0:8], 1.0)
            pr.drain("vector")
            pr.ve("reciprocal", gts[:, 8:16], gts[:, 16:24])
            pr.drain("vector")
            pr.ve("tensor_tensor", gt3, g3, m2b, OP.is_ge)
            pr.drain("vector")
            w3 = wgate.rearrange("p (a e) -> p a e", e=NE)
            pr.ve("tensor_mul", w3, w3, gt3)
            pr.drain("vector")
            rdb = gts[:, 8:16].unsqueeze(2).broadcast_to([P, 8, NE])
            pr.ve("tensor_mul", w3, w3, rdb, inc="v_g")
            pr.drain("vector")
            pr.ve("tensor_copy", wgate_bf, wgate, inc="v_g")
            if "dbg_w" in debug_outs:
                pr.wait("sync", "v_g", pr.val("v_g"))
                pr.dma("sync", dbg_w[ch], wgate, "s_dbg")
            for tt in range(8):
                pr.wait("tensor", "v_g", pr.val("v_g"))
                pr.wait("tensor", "v_tr", pr.val("v_tr"))
                pr.tr(psbf(3072, 3584)[0:4, tt * P:(tt + 1) * P],
                      wgate_bf[:, tt * NE:(tt + 1) * NE], ident,
                      inc="t_tr" if tt == 7 else None)
            pr.wait("vector", "t_tr", pr.val("t_tr"))
            for tt in range(8):
                pr.ve("tensor_copy", wTg[tt][0:4, :],
                      psbf(3072, 3584)[0:4, tt * P:(tt + 1) * P],
                      inc="v_tr" if tt == 7 else None)
            pr.wait("tensor", "f_w", 16)
            for tt in range(8):
                pr.wait("tensor", "v_tr", pr.val("v_tr"))
                if tt > 0:
                    pr.wait("tensor", "v_f", pr.val("v_f"))
                for nb in range(2):
                    pr.mm(psv(PO[0] + nb * QB, PO[0] + (nb + 1) * QB),
                          wTg[tt][0:4, :], be2sb[0:4, nb * QB:(nb + 1) * QB],
                          True, True, inc="t_z" if nb == 1 else None)
                pr.wait("vector", "t_z", pr.val("t_z"))
                pr.ve("tensor_copy", acc[:, tt * D:(tt + 1) * D], psv(*PO), inc="v_f")
            o_free_mark[0] = pr.val("v_f")
            if "dbg_acc" in debug_outs and ch == 0:
                pr.wait("sync", "v_f", pr.val("v_f"))
                pr.dma("sync", dbg_acc[0], acc, "s_dbg")

            for e in range(NE):
                ge = ch * NE + e
                wa = (2 * ge) % 3
                wb = (2 * ge + 1) % 3
                gslot = 0
                if ge >= 1:
                    pr.wait("gpsimd", "v_f", pr.val("v_f"))
                    pr.wait("gpsimd", "t_z", pr.val("t_z"))
                ldg = load_gb(e, gslot)
                gn = ge + 1
                if gn < 8 and gn not in ldw:
                    if ge - 1 in z2_end:
                        pr.wait("gpsimd", "t_z", z2_end[ge - 1])
                    ldw[gn] = [load_expert_w(gn % NE, 0, (2 * gn) % 3), None]
                pr.wait("scalar", "f_g", ldg)
                pr.se("activation", sqd2[0:1, :], b1r[gslot][0:1, :], AF.Identity,
                      accum_out=bsum_f[0:1, e:e + 1], inc="s_f")
                pr.wait("vector", "s_f", pr.val("s_f"))
                pr.ve("tensor_copy", bsum_bf[0:1, e:e + 1], bsum_f[0:1, e:e + 1],
                      inc="v_f")
                pr.wait("scalar", "f_w", ldw[ge][0])
                for j in range(8):
                    pr.se("activation", sqd2, wbuf[wa][:, j * D:(j + 1) * D], AF.Identity,
                          accum_out=wsum_f[:, j:j + 1], inc="s_f" if j == 7 else None)
                pr.wait("vector", "s_f", pr.val("s_f"))
                pr.wait("vector", "t_z", pr.val("t_z"))
                pr.ve("tensor_copy", wsum_bf[:, 0:8], wsum_f[:, 0:8], inc="v_f")
                wsum_ready = pr.val("v_f")

                pr.wait("tensor", "s_f", fstats_mark[0])
                pr.wait("tensor", "v_f", wsum_ready)
                pr.wait("tensor", "f_g", ldg)
                for tt in range(8):
                    for c in range(8):
                        pr.mm(psv(SUMC + tt, SUMC + tt + 1),
                              xfT[:, c * TQ + ch0 + tt * P: c * TQ + ch0 + (tt + 1) * P],
                              wsum_bf[:, c:c + 1], c == 0, False)
                    pr.mm(psv(SUMC + tt, SUMC + tt + 1), ones1[0:1, 0:P],
                          bsum_bf[0:1, e:e + 1], False, True,
                          inc="t_z" if tt == 7 else None)
                pr.wait("tensor", "f_w", ldw[ge][0])
                if "dbg_wb" in debug_outs and ch == 0 and e == 0:
                    pr.wait("sync", "f_w", ldw[ge][0])
                    pr.dma("sync", dbg_wb, wbuf[wa], "s_dbg")

                for tt in range(8):
                    zb = PZ[tt % 2]
                    k1, k2 = tt % 2, 2 + tt % 2
                    if len(stt1_marks[tt % 2]) >= 1:
                        pr.wait("tensor", "v_f", stt1_marks[tt % 2][-1])
                    pr.wait("tensor", "s_f", fstats_mark[0])
                    for nb in range(2):
                        c0 = zb[0] + nb * QB
                        for c in range(8):
                            pr.mm(psv(c0, c0 + QB),
                                  xfT[:, c * TQ + ch0 + tt * P: c * TQ + ch0 + (tt + 1) * P],
                                  wbuf[wa][:, c * D + nb * QB:c * D + (nb + 1) * QB],
                                  c == 0, False)
                        pr.mm(psv(c0, c0 + QB), ones1[0:1, 0:P],
                              b1r[gslot][0:1, nb * QB:(nb + 1) * QB], False, True,
                              inc="t_z" if nb == 1 else None)
                    if tt == 7:
                        z1_end[ge] = pr.val("t_z")
                        gn = ge + 1
                        if gn < 8:
                            pr.wait("gpsimd", "t_z", z1_end[ge])
                            ldw[gn][1] = load_expert_w(gn % NE, 1, (2 * gn + 1) % 3)
                    pr.wait("scalar", "t_z", pr.val("t_z"))

                    def apply1(negm, rstd, tt=tt, zb=zb, ch=ch, e=e):
                        t1 = t1b[tt % 2]
                        if "dbg_z" in debug_outs and ch == 0 and e == 0 and tt == 0:
                            pr.ve("tensor_copy", dbgz_sb, psv(*zb), inc="v_f")
                            pr.wait("sync", "v_f", pr.val("v_f"))
                            pr.dma("sync", dbg_z[0], dbgz_sb, "s_dbg")
                        pr.ve("scalar_tensor_tensor", t1, psv(*zb), negm, g1t,
                              OP.add, OP.mult, inc="v_f")
                        stt1_marks[tt % 2].append(pr.val("v_f"))
                        pr.ve("scalar_tensor_tensor", t1, t1, rstd, be1t,
                              OP.mult, OP.add)
                        pr.ve("tensor_scalar_max", hbf[tt % 2], t1, 0.0, inc="v_f")
                        if "dbg_h" in debug_outs and ch == 0 and e == 0 and tt == 0:
                            pr.ve("tensor_copy", outst[1].bitcast(BF16)[:, 0:D], hbf[0], inc="v_f")
                            pr.wait("sync", "v_f", pr.val("v_f"))
                            pr.dma("sync", dbg_h[0], outst[1].bitcast(BF16)[:, 0:D], "s_dbg")

                    ln_chain(psv(*zb), k1, psv(SUMC + tt, SUMC + tt + 1), g1t, be1t,
                             apply1)
                    h_ready = pr.val("v_f")

                    pr.wait("tensor", "v_f", h_ready)
                    pr.wait("tensor", "v_tr", pr.val("v_tr"))
                    for j in range(8):
                        pr.tr(psbf(3072, 3584)[:, j * P:(j + 1) * P],
                              hbf[tt % 2][:, j * P:(j + 1) * P], ident,
                              inc="t_tr" if j == 7 else None)
                    pr.wait("vector", "t_tr", pr.val("t_tr"))
                    pr.ve("tensor_copy", hTb[tt % 2], psbf(3072, 3584), inc="v_tr")
                    ht_ready = pr.val("v_tr")

                    pr.wait("tensor", "v_tr", ht_ready)
                    pr.wait("tensor", "v_f", o_free_mark[0])
                    pr.wait("tensor", "s_f", fstats_mark[0])
                    pr.wait("tensor", "f_w", ldw[ge][1])
                    for nb in range(2):
                        c0 = PO[0] + nb * QB
                        for j in range(8):
                            pr.mm(psv(c0, c0 + QB),
                                  hTb[tt % 2][:, j * P:(j + 1) * P],
                                  wbuf[wb][:, j * D + nb * QB:j * D + (nb + 1) * QB],
                                  j == 0, False)
                        pr.mm(psv(c0, c0 + QB), ones1[0:1, 0:P],
                              b2r[gslot][0:1, nb * QB:(nb + 1) * QB], False, True,
                              inc="t_z" if nb == 1 else None)
                    pr.wait("scalar", "t_z", pr.val("t_z"))
                    pr.se("activation", sqd2, psv(*PO), AF.Identity,
                          accum_out=st_sq[:, k2:k2 + 1], inc="s_f")
                    pr.wait("vector", "s_f", pr.val("s_f"))
                    pr.ve("tensor_copy", st_s2[:, k2:k2 + 1], st_sq[:, k2:k2 + 1],
                          inc="v_f")
                    pr.wait("scalar", "v_f", pr.val("v_f"))

                    def apply2(negm, rstd, tt=tt, e=e):
                        u1 = u1b[0]
                        pr.ve("scalar_tensor_tensor", u1, psv(*PO), negm, g2t,
                              OP.add, OP.mult, inc="v_f")
                        o_free_mark[0] = pr.val("v_f")
                        pr.ve("tensor_scalar_mul", st_rstd[:, 3:4], rstd,
                              wgate[:, tt * NE + e: tt * NE + e + 1])
                        pr.drain("vector")
                        ab_ = acc[:, tt * D:(tt + 1) * D]
                        pr.ve("scalar_tensor_tensor", ab_, u1, st_rstd[:, 3:4], ab_,
                              OP.mult, OP.add, inc="v_f")

                    ln_chain(psv(*PO), k2, st_s2[:, k2:k2 + 1], g2t, None, apply2)
                    if tt == 7:
                        z2_end[ge] = pr.val("t_z")
                        if "dbg_acc" in debug_outs and ch == 0:
                            pr.wait("sync", "v_f", pr.val("v_f"))
                            pr.dma("sync", dbg_acc[1 + e], acc, "s_dbg")

            fin_marks = []
            for tt in range(8):
                xr2 = xfrd[tt % 2]
                if len(fin_marks) >= 2:
                    pr.wait("sync", "v_f", fin_marks[-2])
                pr.dma("sync", xr2, xf_s[ch0 + tt * P: ch0 + (tt + 1) * P, :], "s_x2")
                sx2 = pr.val("s_x2")
                pr.wait("vector", "s_x2", sx2)
                pr.wait("vector", "s_f", fstats_mark[0])
                k = tt % 2
                fm = pr.ve("scalar_tensor_tensor", t1b[k], xr2, 2.0, acc[:, tt * D:(tt + 1) * D],
                           OP.mult, OP.add, accum_out=st_tmp[:, k:k + 1], inc="v_f")
                fin_marks.append(fm)
                pr.wait("scalar", "v_f", pr.val("v_f"))

                def applyf(negm, rstd, tt=tt, k=k):
                    pr.wait("vector", "s_out", max(0, pr.val("s_out") - 16))
                    pr.ve("tensor_scalar", outst[k], t1b[k], negm, rstd,
                          OP.add, OP.mult, inc="v_f")
                    vo = pr.val("v_f")
                    pr.wait("sync", "v_f", vo)
                    pr.dma("sync", out[ch0 + tt * P: ch0 + (tt + 1) * P, :], outst[k],
                           "s_out")

                ln_chain(t1b[k], k, st_tmp[:, k:k + 1], None, None, applyf)

        pr.wait("sync", "s_out", pr.val("s_out"))

    AR.pop()

    if "FFN" not in phases:
        AR.push()
        ostub = AR.view(D, F32)
        pr.wait("vector", "s_st", phaseA_store)
        pr.ve("memset", ostub, 0.0, inc="v_misc")
        pr.wait("sync", "v_misc", pr.val("v_misc"))
        for i in range(TQ // P):
            pr.dma("sync", out[i * P:(i + 1) * P, :], ostub, "s_out")
        pr.wait("sync", "s_out", pr.val("s_out"))
        AR.pop()

    pr.run()
    return nc


def kernel(L0_attn_x, L0_attn_Wq, L0_attn_Wk, L0_attn_Wv, L0_attn_Wu, L0_attn_bu,
           L0_ffn_gate_W, L0_ffn_gate_b, L0_ffn_W1, L0_ffn_b1, L0_ffn_g1, L0_ffn_be1,
           L0_ffn_W2, L0_ffn_b2, L0_ffn_g2, L0_ffn_be2):
    x = np.ascontiguousarray(np.asarray(L0_attn_x, dtype=np.float32))
    B = x.shape[0]
    nc = build()
    res = run_bass_kernel_spmd(nc, make_in_maps(x, locals()), core_ids=list(range(8)))
    outp = np.empty((B, T, D), dtype=np.float32)
    for c in range(8):
        b, qh = c // 2, c % 2
        outp[b, qh * TQ:(qh + 1) * TQ] = res.results[c]["out"]
    return outp


def make_in_maps(x, env):
    eye = np.eye(P, dtype=np.float32)
    f = np.float32
    maps = []
    for c in range(8):
        b, qh = c // 2, c % 2
        maps.append({
            "xkv": x[b],
            "xq": np.ascontiguousarray(x[b, qh * TQ:(qh + 1) * TQ]),
            "wq": np.asarray(env["L0_attn_Wq"], f),
            "wk": np.asarray(env["L0_attn_Wk"], f),
            "wv": np.asarray(env["L0_attn_Wv"], f),
            "wu": np.asarray(env["L0_attn_Wu"], f),
            "bu": np.asarray(env["L0_attn_bu"], f).reshape(1, D),
            "gw": np.asarray(env["L0_ffn_gate_W"], f),
            "gb": np.asarray(env["L0_ffn_gate_b"], f).reshape(1, NE),
            "w1": np.asarray(env["L0_ffn_W1"], f),
            "b1": np.asarray(env["L0_ffn_b1"], f),
            "g1": np.asarray(env["L0_ffn_g1"], f),
            "be1": np.asarray(env["L0_ffn_be1"], f),
            "w2": np.asarray(env["L0_ffn_W2"], f),
            "b2": np.asarray(env["L0_ffn_b2"], f),
            "g2": np.asarray(env["L0_ffn_g2"], f),
            "be2": np.asarray(env["L0_ffn_be2"], f),
            "ident_in": eye,
        })
    return maps


# version 02
# speedup vs baseline: 1.0971x; 1.0971x over previous
import numpy as np
import concourse.bass as bass
import concourse.mybir as mybir
from concourse.bass_utils import run_bass_kernel_spmd

F32 = mybir.dt.float32
BF16 = mybir.dt.bfloat16
AF = mybir.ActivationFunctionType
OP = mybir.AluOpType
AX = mybir.AxisListType.X

P = 128
D = 1024
T = 4096
TQ = 2048
H = 16
NE = 4
QB = 512
NQB = TQ // QB
NST = T // P
EPS = 1e-5
QSCALE = 1.0 / 32.0

DEBUG_OUTS = ()


class Prog:
    ENGINES = ("sync", "gpsimd", "vector", "scalar", "tensor")

    def __init__(self, nc):
        self.nc = nc
        self.ops = {e: [] for e in self.ENGINES}
        self.sems = {}
        self.cnt = {}

    def sem(self, name):
        if name not in self.sems:
            self.sems[name] = self.nc.alloc_semaphore(name)
            self.cnt[name] = 0
        return self.sems[name]

    def inc(self, name, k=1):
        h = self.sem(name)
        self.cnt[name] += k
        return h, self.cnt[name]

    def val(self, name):
        self.sem(name)
        return self.cnt[name]

    def emit(self, engine, fn):
        self.ops[engine].append(fn)

    def wait(self, engine, name, value):
        if value <= 0:
            return
        h = self.sem(name)
        self.emit(engine, lambda eng, h=h, v=value: eng.wait_ge(h, v))

    def mm(self, out, lhsT, rhs, start, stop, inc=None, tile_position=None):
        if inc is None:
            self.emit("tensor", lambda t, o=out, l=lhsT, r=rhs, s0=start, s1=stop, tp=tile_position:
                      t.matmul(o, l, r, start=s0, stop=s1, tile_position=tp))
            return None
        h, v = self.inc(inc)
        self.emit("tensor", lambda t, o=out, l=lhsT, r=rhs, s0=start, s1=stop, h=h, tp=tile_position:
                  t.matmul(o, l, r, start=s0, stop=s1, tile_position=tp).then_inc(h, 1))
        return v

    def tr(self, out, in_, ident, inc=None):
        if inc is None:
            self.emit("tensor", lambda t, o=out, i=in_, d=ident: t.transpose(o, i, d))
            return None
        h, v = self.inc(inc)
        self.emit("tensor", lambda t, o=out, i=in_, d=ident, h=h: t.transpose(o, i, d).then_inc(h, 1))
        return v

    def ve(self, fname, *args, inc=None, **kwargs):
        if inc is None:
            self.emit("vector", lambda v, f=fname, a=args, k=kwargs: getattr(v, f)(*a, **k))
            return None
        h, val = self.inc(inc)
        self.emit("vector", lambda v, f=fname, a=args, k=kwargs, h=h:
                  getattr(v, f)(*a, **k).then_inc(h, 1))
        return val

    def se(self, fname, *args, inc=None, **kwargs):
        if inc is None:
            self.emit("scalar", lambda s, f=fname, a=args, k=kwargs: getattr(s, f)(*a, **k))
            return None
        h, val = self.inc(inc)
        self.emit("scalar", lambda s, f=fname, a=args, k=kwargs, h=h:
                  getattr(s, f)(*a, **k).then_inc(h, 1))
        return val

    def drain(self, engine):
        self.emit(engine, lambda eng: eng.drain())

    def dma(self, q, dst, src, sem):
        h, v = self.inc(sem, 16)
        self.emit(q, lambda eng, d=dst, s=src, h=h: eng.dma_start(d, s).then_inc(h, 16))
        return v

    def run(self):
        with self.nc.Block() as blk:
            for e in self.ENGINES:
                getattr(blk, e)(self._runner(e))

    def _runner(self, e):
        def f(eng):
            for op in self.ops[e]:
                op(eng)
        return f


def ceil32(x):
    return (x + 31) & ~31


class Arena:
    def __init__(self, nc, name, nbytes):
        self.nbytes = nbytes
        self.t = nc.alloc_sbuf_tensor(name, [P, nbytes // 2], BF16)
        self.ap = self.t.ap()
        self.off = 0
        self.marks = []
        self.hwm = 0

    def view(self, nelem, dtype, shape=None):
        esz = 2 if dtype == BF16 else 4
        off = ceil32(self.off)
        self.off = off + nelem * esz
        self.hwm = max(self.hwm, self.off)
        assert self.off <= self.nbytes, f"arena overflow {self.off} > {self.nbytes}"
        v = self.ap[:, off // 2: (off + nelem * esz) // 2]
        if dtype != BF16:
            v = v.bitcast(dtype)
        if shape is not None:
            assert int(np.prod(shape)) == nelem
            if len(shape) == 2:
                v = v.rearrange("p (a b) -> p a b", b=shape[1])
        return v

    def push(self):
        self.marks.append(self.off)

    def pop(self):
        self.off = self.marks.pop()


def build(debug_outs=DEBUG_OUTS, phases=("A", "ATT", "FFN")):
    nc = bass.Bass("TRN2", target_bir_lowering=False)
    pr = Prog(nc)

    xkv = nc.dram_tensor("xkv", [T, D], F32, kind="ExternalInput").ap()
    xq = nc.dram_tensor("xq", [TQ, D], F32, kind="ExternalInput").ap()
    wmat = {n: nc.dram_tensor(n, [D, D], F32, kind="ExternalInput").ap()
            for n in ("wq", "wk", "wv", "wu")}
    bu = nc.dram_tensor("bu", [1, D], F32, kind="ExternalInput").ap()
    gw = nc.dram_tensor("gw", [D, NE], F32, kind="ExternalInput").ap()
    gb = nc.dram_tensor("gb", [1, NE], F32, kind="ExternalInput").ap()
    w1 = nc.dram_tensor("w1", [NE, D, D], F32, kind="ExternalInput").ap()
    b1 = nc.dram_tensor("b1", [NE, D], F32, kind="ExternalInput").ap()
    g1 = nc.dram_tensor("g1", [NE, D], F32, kind="ExternalInput").ap()
    be1 = nc.dram_tensor("be1", [NE, D], F32, kind="ExternalInput").ap()
    w2 = nc.dram_tensor("w2", [NE, D, D], F32, kind="ExternalInput").ap()
    b2 = nc.dram_tensor("b2", [NE, D], F32, kind="ExternalInput").ap()
    g2 = nc.dram_tensor("g2", [NE, D], F32, kind="ExternalInput").ap()
    be2 = nc.dram_tensor("be2", [NE, D], F32, kind="ExternalInput").ap()
    ident_in = nc.dram_tensor("ident_in", [P, P], F32, kind="ExternalInput").ap()

    out = nc.dram_tensor("out", [TQ, D], F32, kind="ExternalOutput").ap()

    def scratch(name, shape, dtype):
        kind = "ExternalOutput" if name in debug_outs else "Internal"
        return nc.dram_tensor(name, shape, dtype, kind=kind).ap()

    kT_s = scratch("kT_s", [D, T], BF16)
    qT_s = scratch("qT_s", [D, TQ], BF16)
    v_s = scratch("v_s", [H // 2, T, P], BF16)
    xf_s = scratch("xf_s", [TQ, D], F32)
    dbg_rs = scratch("dbg_rs", [NQB, H, QB], F32)
    dbg_o = scratch("dbg_o", [NQB, P, 8 * QB], BF16)
    dbg_u = scratch("dbg_u", [NQB, 4, P, D], F32)
    dbg_st = scratch("dbg_st", [NQB, 4, P, 24], F32)
    dbg_w = scratch("dbg_w", [2, P, 32], F32)
    dbg_h = scratch("dbg_h", [2, P, D], BF16)
    dbg_acc = scratch("dbg_acc", [5, P, 8 * D], F32)
    dbg_z = scratch("dbg_z", [2, P, D], F32)
    dbg_wb = scratch("dbg_wb", [P, 8 * D], BF16)

    ident = nc.alloc_sbuf_tensor("ident", [P, P], BF16).ap()
    ones1 = nc.alloc_sbuf_tensor("ones1", [1, P], BF16).ap()
    wuT = nc.alloc_sbuf_tensor("wuT", [P, 8 * D], BF16).ap()
    bu_bf = nc.alloc_sbuf_tensor("bu_bf", [1, D], BF16).ap()
    xfT = nc.alloc_sbuf_tensor("xfT", [P, 8 * TQ], BF16).ap()
    gw_bf = nc.alloc_sbuf_tensor("gw_bf", [P, 8 * NE], BF16).ap()
    gb_bf = nc.alloc_sbuf_tensor("gb_bf", [1, NE], BF16).ap()
    wsum_f = nc.alloc_sbuf_tensor("wsum_f", [P, 16], F32).ap()
    wsum_bf = nc.alloc_sbuf_tensor("wsum_bf", [P, 16], BF16).ap()
    bsum_bf = nc.alloc_sbuf_tensor("bsum_bf", [1, 16], BF16).ap()
    bsum_f = nc.alloc_sbuf_tensor("bsum_f", [1, 16], F32).ap()
    st_negm = nc.alloc_sbuf_tensor("st_negm", [P, 4], F32).ap()
    st_tmp = nc.alloc_sbuf_tensor("st_tmp", [P, 8], F32).ap()
    st_sq = nc.alloc_sbuf_tensor("st_sq", [P, 4], F32).ap()
    st_std = nc.alloc_sbuf_tensor("st_std", [P, 4], F32).ap()
    st_rstd = nc.alloc_sbuf_tensor("st_rstd", [P, 4], F32).ap()
    wgate = nc.alloc_sbuf_tensor("wgate", [P, 32], F32).ap()
    wgate_bf = nc.alloc_sbuf_tensor("wgate_bf", [P, 32], BF16).ap()
    wTg = [nc.alloc_sbuf_tensor(f"wTg{i}", [NE, P], BF16).ap() for i in range(8)]
    gm1 = nc.alloc_sbuf_tensor("gm1", [P, 8], F32).ap()
    gm2 = nc.alloc_sbuf_tensor("gm2", [P, 8], F32).ap()
    gden = nc.alloc_sbuf_tensor("gden", [P, 8], F32).ap()
    gtmp = nc.alloc_sbuf_tensor("gtmp", [P, 32], F32).ap()

    psum = nc.alloc_psum_tensor("psum", [P, 4096], F32).ap()

    def psv(c0, c1, r0=0, r1=P):
        return psum[r0:r1, c0:c1]

    def psbf(c0, c1):
        return psum[:, c0:c1].bitcast(BF16)

    AR = Arena(nc, "arena", 140 * 1024)

    pr.dma("gpsimd", ident, ident_in, "g_ld")
    pr.ve("memset", ones1, 1.0, inc="v_misc")

    AR.push()
    wnat = AR.view(8 * D, BF16)
    wTq = AR.view(8 * D, BF16)
    wTk = AR.view(8 * D, BF16)
    wTv = AR.view(8 * D, BF16)
    xnat = [AR.view(4 * D, BF16) for _ in range(2)]
    xT = [AR.view(8 * QB, BF16) for _ in range(2)]
    stg = [AR.view(QB, BF16) for _ in range(4)]

    def transpose_block(srcs, dst_fn):
        pr.wait("tensor", "v_tr", pr.val("v_tr"))
        for i, s in enumerate(srcs):
            pr.tr(psbf(3072, 3584)[:, i * P:(i + 1) * P], s, ident,
                  inc="t_tr" if i == 7 else None)
        pr.wait("vector", "t_tr", pr.val("t_tr"))
        dst_fn(psbf(3072, 3584))

    for wname, dst in (("wq", wTq), ("wk", wTk), ("wv", wTv), ("wu", wuT)):
        pr.wait("gpsimd", "t_tr", pr.val("t_tr"))
        ldv = pr.dma("gpsimd", wnat.rearrange("p (a d) -> p a d", d=D),
                     wmat[wname].rearrange("(a p) d -> p a d", p=P), "g_ld")
        pr.wait("tensor", "g_ld", ldv)
        for j in range(8):
            srcs = [wnat[:, i * D + j * P: i * D + (j + 1) * P] for i in range(8)]
            def ev(pv, j=j, dst=dst):
                pr.ve("tensor_copy", dst[:, j * D:(j + 1) * D], pv, inc="v_tr")
            transpose_block(srcs, ev)

    pr.dma("gpsimd", gw_bf.rearrange("p (a e) -> p a e", e=NE),
           gw.rearrange("(a p) e -> p a e", p=P), "g_ld")
    pr.dma("gpsimd", gb_bf, gb, "g_ld")

    projps = [(i * 512, (i + 1) * 512) for i in range(4)]
    pp = {"i": 0}
    evac_cnt = {i: 0 for i in range(4)}
    chunk_end_tpj = []
    stg_store_mark = {i: 0 for i in range(4)}

    def proj_tile(lhs_rhs_fn, stgbuf, dsts, scale=None):
        bank = pp["i"] % 4
        pp["i"] += 1
        c0, c1 = projps[bank]
        pr.wait("tensor", "v_pa", evac_cnt[bank])
        for c in range(8):
            lhsT, rhs = lhs_rhs_fn(c)
            pr.mm(psv(c0, c1), lhsT, rhs, c == 0, c == 7,
                  inc="t_pj" if c == 7 else None)
        pr.wait("vector", "t_pj", pr.val("t_pj"))
        if scale is None:
            v2 = pr.ve("tensor_copy", stgbuf, psv(c0, c1), inc="v_pa")
        else:
            v2 = pr.ve("tensor_scalar_mul", stgbuf, psv(c0, c1), scale, inc="v_pa")
        evac_cnt[bank] = v2
        pr.wait("sync", "v_pa", v2)
        for dst_ap, src_ap in dsts:
            pr.dma("sync", dst_ap, src_ap, "s_st")

    def do_chunk(src_rows, ct0, emit_k, emit_v, emit_q, buf_i):
        ct_cols = slice(ct0, ct0 + 512)
        xb, xtb = xnat[buf_i], xT[buf_i]
        if len(chunk_end_tpj) >= 2:
            pr.wait("gpsimd", "t_pj", chunk_end_tpj[-2])
        ldv = pr.dma("gpsimd", xb.rearrange("p (a d) -> p a d", d=D),
                     src_rows.rearrange("(a p) d -> p a d", p=P), "g_ld")
        pr.wait("tensor", "g_ld", ldv)
        for tt in range(4):
            srcs = [xb[:, tt * D + j * P: tt * D + (j + 1) * P] for j in range(8)]
            def ev(pv, tt=tt, xtb=xtb):
                xv = xtb.rearrange("p (j c) -> p j c", c=512)[:, :, tt * P:(tt + 1) * P]
                pvv = pv.rearrange("p (j c) -> p j c", c=P)
                pr.ve("tensor_copy", xv, pvv, inc="v_tr")
            transpose_block(srcs, ev)
        pr.wait("tensor", "v_tr", pr.val("v_tr"))
        if emit_k:
            for j in range(8):
                si = j % 4
                sb = stg[si]
                pr.wait("vector", "s_st", stg_store_mark[si])
                proj_tile(lambda c, j=j, xtb=xtb: (wTk[:, c * D + j * P:c * D + (j + 1) * P],
                                                   xtb[:, c * 512:(c + 1) * 512]),
                          sb, [(kT_s[j * P:(j + 1) * P, ct_cols], sb)])
                stg_store_mark[si] = pr.val("s_st")
        if emit_v:
            for tt in range(4):
                for nb in range(2):
                    si = (tt * 2 + nb) % 4
                    sb = stg[si]
                    pr.wait("vector", "s_st", stg_store_mark[si])
                    tok0 = ct0 + tt * P
                    dsts = []
                    for pq in range(4):
                        pair = nb * 4 + pq
                        dsts.append((v_s[pair, tok0:tok0 + P, :],
                                     sb.rearrange("p (a c) -> p a c", c=P)[:, pq, :]))
                    proj_tile(lambda c, tt=tt, nb=nb, xtb=xtb: (
                        xtb[:, c * 512 + tt * P:c * 512 + (tt + 1) * P],
                        wTv[:, c * D + nb * 512:c * D + (nb + 1) * 512]),
                        sb, dsts)
                    stg_store_mark[si] = pr.val("s_st")
        if emit_q:
            for j in range(8):
                si = j % 4
                sb = stg[si]
                pr.wait("vector", "s_st", stg_store_mark[si])
                proj_tile(lambda c, j=j, xtb=xtb: (wTq[:, c * D + j * P:c * D + (j + 1) * P],
                                                   xtb[:, c * 512:(c + 1) * 512]),
                          sb, [(qT_s[j * P:(j + 1) * P, ct_cols], sb)],
                          scale=QSCALE)
                stg_store_mark[si] = pr.val("s_st")
        chunk_end_tpj.append(pr.val("t_pj"))

    if "A" in phases:
        for ct in range(8):
            do_chunk(xkv[ct * 512:(ct + 1) * 512, :], ct * 512,
                     emit_k=True, emit_v=True, emit_q=False, buf_i=ct % 2)
        for ct in range(4):
            do_chunk(xq[ct * 512:(ct + 1) * 512, :], ct * 512,
                     emit_k=False, emit_v=False, emit_q=True, buf_i=ct % 2)

    AR.pop()
    phaseA_store = pr.val("s_st")

    AR.push()
    KT = [AR.view(T, BF16) for _ in range(2)]
    Vq = [AR.view(32 * 130, BF16) for _ in range(2)]
    QTb = [AR.view(8 * QB, BF16) for _ in range(2)]
    attA = AR.view(16 * QB, BF16)
    attB = AR.view(16 * QB, BF16)
    O_sbU = AR.view(8 * QB, BF16)
    tmpB = AR.view(QB, BF16)
    rs_st = AR.view(2 * QB, F32)
    coll = AR.view(QB, F32)
    rcoll = AR.view(QB, F32)
    rstbf = AR.view(2 * QB, BF16)
    xres = [AR.view(D, F32) for _ in range(2)]
    xfst = [AR.view(D, F32) for _ in range(2)]
    xfn = [AR.view(D, F32) for _ in range(2)]
    xfbf = [AR.view(D, BF16) for _ in range(2)]
    sqd = AR.view(D, BF16)

    SCA = (0, 1536)
    SCB = (1536, 3072)
    POA = (3072, 3584)
    POB = (3584, 4096)
    BCA = (0, 512)
    BCB = (512, 1024)
    PU = (1536, 2560)

    if "ATT" in phases:
        ldv = pr.dma("gpsimd", bu_bf, bu, "g_ld")
        for i in range(2):
            pr.ve("memset", Vq[i], 1.0, inc="v_misc")
        vq_ready = pr.val("v_misc")

        GROUPS = [3, 3, 3, 3, 2, 2]
        reg_mark = {0: 0, 1: 0}
        pair_end_tav = []
        qb_end_marks = []
        stt_marks = []
        sxf_marks = []
        xfb_marks = []
        sq_marks = []

        for qb in range(NQB):
            qtb = QTb[qb % 2]
            if qb >= 2:
                pr.wait("sync", "t_sc", qb_end_marks[qb - 2]["t_sc"])
            pr.dma("sync",
                   qtb.rearrange("p (j c) -> p j c", c=QB),
                   qT_s.rearrange("(j p) q -> p j q", p=P)[:, :, qb * QB:(qb + 1) * QB],
                   "a_q")
            qtb_ld = pr.val("a_q")

            for p in range(8):
                gi = len(pair_end_tav)
                kt, vq = KT[gi % 2], Vq[gi % 2]
                if gi >= 2:
                    pr.wait("sync", "t_avh", pair_end_tav[gi - 2])
                pr.wait("sync", "v_misc", vq_ready)
                pr.dma("sync", kt, kT_s[p * P:(p + 1) * P, :], "a_kv")
                vsrc = v_s[p].rearrange("(s q) c -> q s c", q=P)
                vdst = vq.rearrange("p (s c) -> p s c", c=130)
                pr.dma("sync", vdst[:, :, 0:64], vsrc[:, :, 0:64], "a_kv")
                pr.dma("sync", vdst[:, :, 65:129], vsrc[:, :, 64:128], "a_kv")
                kv_ld = pr.val("a_kv")

                def emit_attv(item, kt=kt, vq=vq):
                    st0, glen, sexv = item
                    pr.wait("tensor", "s_ex", sexv)
                    for k in range(glen):
                        st = st0 + k
                        slot = (st % 16) * QB
                        if st == 0:
                            pr.wait("tensor", "v_oe", pr.val("v_oe"))
                            pr.wait("tensor", "v_tr", pr.val("v_tr"))
                        fin = (st % 16 == 15) or (st == NST - 1)
                        pr.mm(psv(*POA, 0, 65), vq[:, st * 130: st * 130 + 65],
                              attA[:, slot:slot + QB], st == 0, st == NST - 1)
                        pr.mm(psv(*POB, 0, 65), vq[:, st * 130 + 65: st * 130 + 130],
                              attB[:, slot:slot + QB], st == 0, st == NST - 1,
                              inc="t_avh" if fin else None)

                pend = []
                for half in range(2):
                    for g, glen in enumerate(GROUPS):
                        st0 = half * 16 + sum(GROUPS[:g])
                        if g == 0 and half == 0:
                            pr.wait("tensor", "a_kv", kv_ld)
                            pr.wait("tensor", "a_q", qtb_ld)
                            pr.wait("tensor", "v_no", pr.val("v_no"))
                            pr.wait("tensor", "v_u", pr.val("v_u"))
                        pr.wait("tensor", "s_ex", reg_mark[0])
                        tscA = tscB = None
                        for k in range(glen):
                            st = st0 + k
                            last = (k == glen - 1)
                            if k == 0:
                                pass
                            pr.mm(psv(*SCA)[:, k * QB:(k + 1) * QB],
                                  kt[0:64, st * P:(st + 1) * P],
                                  qtb[0:64, p * QB:(p + 1) * QB],
                                  True, True, inc="t_sc" if last else None,
                                  tile_position=(0, 0))
                            if k == 0:
                                tscA = pr.val("t_sc") + (0 if last else glen - 1)
                                pr.wait("tensor", "s_ex", reg_mark[1])
                            pr.mm(psv(*SCB)[:, k * QB:(k + 1) * QB],
                                  kt[64:128, st * P:(st + 1) * P],
                                  qtb[64:128, p * QB:(p + 1) * QB],
                                  True, True, inc="t_sc" if last else None,
                                  tile_position=(64, 0))
                        tscA = pr.val("t_sc") - 1
                        tscB = pr.val("t_sc")
                        if g == 0:
                            pr.wait("scalar", "t_avh", pr.val("t_avh"))
                        sl0 = (st0 % 16) * QB
                        pr.wait("scalar", "t_sc", tscA)
                        pr.se("activation", attA[:, sl0:sl0 + glen * QB],
                              psv(*SCA)[:, 0:glen * QB], AF.Exp, inc="s_ex")
                        reg_mark[0] = pr.val("s_ex")
                        pr.wait("scalar", "t_sc", tscB)
                        pr.se("activation", attB[:, sl0:sl0 + glen * QB],
                              psv(*SCB)[:, 0:glen * QB], AF.Exp, inc="s_ex")
                        reg_mark[1] = pr.val("s_ex")
                        pend.append((st0, glen, pr.val("s_ex")))
                        if len(pend) >= 2:
                            emit_attv(pend.pop(0))
                    while pend:
                        emit_attv(pend.pop(0))
                pair_end_tav.append(pr.val("t_avh"))

                pr.wait("vector", "t_avh", pair_end_tav[-1])
                if gi >= 1:
                    pr.wait("vector", "g_rs", pr.val("g_rs"))
                    pr.wait("vector", "g_mb", pr.val("g_mb"))
                pr.ve("tensor_copy", O_sbU[0:64, p * QB:(p + 1) * QB],
                      psv(*POA, 0, 65)[0:64, :], inc="v_oe")
                pr.ve("tensor_copy", rs_st[64:65, 0:QB], psv(*POA, 64, 65), inc="v_oe")
                pr.ve("tensor_copy", tmpB[0:64, :], psv(*POB, 0, 65)[0:64, :], inc="v_oe")
                pr.ve("tensor_copy", rs_st[64:65, QB:2 * QB], psv(*POB, 64, 65), inc="v_oe")
                voe = pr.val("v_oe")
                pr.wait("gpsimd", "v_oe", voe)
                pr.dma("gpsimd", coll[2 * p:2 * p + 2, :],
                       rs_st[64:65, :].rearrange("p (a c) -> p a c", c=QB), "g_rs")
                pr.dma("gpsimd", O_sbU[64:128, p * QB:(p + 1) * QB], tmpB[0:64, :], "g_mb")

            pr.wait("vector", "g_rs", pr.val("g_rs"))
            pr.ve("reciprocal", rcoll[0:16, :], coll[0:16, :], inc="v_rc")
            for p in range(8):
                pr.wait("gpsimd", "v_rc", pr.val("v_rc"))
                if p >= 1:
                    pr.wait("gpsimd", "t_bc", pr.val("t_bc"))
                pr.dma("gpsimd", rstbf[0:1, :].rearrange("p (a c) -> p a c", c=QB),
                       rcoll[2 * p:2 * p + 2, :], "g_sc")
                pr.wait("tensor", "g_sc", pr.val("g_sc"))
                pr.wait("tensor", "s_ex", pr.val("s_ex"))
                pr.wait("tensor", "v_no", pr.val("v_no"))
                pr.mm(psv(*BCA, 0, 64), ones1[0:1, 0:64], rstbf[0:1, 0:QB],
                      True, True, inc="t_bc", tile_position=(0, 0))
                pr.mm(psv(*BCB, 64, 128), ones1[0:1, 0:64], rstbf[0:1, QB:2 * QB],
                      True, True, inc="t_bc", tile_position=(0, 64))
                pr.wait("vector", "t_bc", pr.val("t_bc"))
                pr.wait("vector", "g_mb", pr.val("g_mb"))
                blk_ = O_sbU[0:64, p * QB:(p + 1) * QB]
                pr.ve("tensor_mul", blk_, blk_, psv(*BCA, 0, 64), inc="v_no")
                blk2 = O_sbU[64:128, p * QB:(p + 1) * QB]
                pr.ve("tensor_mul", blk2, blk2, psv(*BCB, 64, 128), inc="v_no")

            if "dbg_rs" in debug_outs:
                pr.wait("sync", "v_rc", pr.val("v_rc"))
                pr.dma("sync", dbg_rs[qb], coll[0:16, :], "s_dbg")
            if "dbg_o" in debug_outs:
                pr.wait("sync", "v_no", pr.val("v_no"))
                pr.dma("sync", dbg_o[qb], O_sbU, "s_dbg")

            vno = pr.val("v_no")
            for tt in range(4):
                xr = xres[tt % 2]
                if len(stt_marks) >= 2:
                    pr.wait("sync", "v_u", stt_marks[-2])
                pr.dma("sync", xr, xq[qb * QB + tt * P: qb * QB + (tt + 1) * P, :], "s_xr")
                sxr = pr.val("s_xr")
                if tt == 0:
                    pr.wait("tensor", "v_no", vno)
                    if qb == 0:
                        pr.wait("tensor", "g_ld", ldv)
                if stt_marks:
                    pr.wait("tensor", "v_u", stt_marks[-1])
                for nb in range(2):
                    c0 = PU[0] + nb * QB
                    for j in range(8):
                        pr.mm(psv(c0, c0 + QB), O_sbU[:, j * QB + tt * P:j * QB + (tt + 1) * P],
                              wuT[:, j * D + nb * QB:j * D + (nb + 1) * QB],
                              j == 0, False)
                    pr.mm(psv(c0, c0 + QB), ones1[0:1, 0:P], bu_bf[0:1, nb * QB:(nb + 1) * QB],
                          False, True, inc="t_u" if nb == 1 else None)
                s_ = tt % 2
                pr.wait("vector", "t_u", pr.val("t_u"))
                pr.wait("vector", "s_xr", sxr)
                if len(sq_marks) >= 2:
                    pr.wait("vector", "s_u", sq_marks[-2])
                vstt = pr.ve("scalar_tensor_tensor", xfst[s_], psv(PU[0], PU[0] + D), 1.0, xr,
                             OP.mult, OP.add, accum_out=st_tmp[:, s_:s_ + 1], inc="v_u")
                stt_marks.append(vstt)
                if "dbg_u" in debug_outs:
                    pr.wait("sync", "v_u", pr.val("v_u"))
                    pr.dma("sync", dbg_u[qb, tt], xfst[s_], "s_dbg")
                pr.wait("scalar", "v_u", pr.val("v_u"))
                pr.se("activation", sqd, xfst[s_], AF.Square,
                      accum_out=st_sq[:, s_:s_ + 1])
                pr.se("activation", st_negm[:, s_:s_ + 1], st_tmp[:, s_:s_ + 1],
                      AF.Identity, scale=-1.0 / D)
                pr.se("activation", st_tmp[:, 2 + s_:3 + s_], st_negm[:, s_:s_ + 1],
                      AF.Square, inc="s_u")
                pr.wait("vector", "s_u", pr.val("s_u"))
                pr.ve("tensor_copy", st_tmp[:, 6 + s_:7 + s_], st_sq[:, s_:s_ + 1])
                pr.ve("tensor_scalar", st_tmp[:, 4 + s_:5 + s_], st_tmp[:, 2 + s_:3 + s_],
                      -1.0, EPS, OP.mult, OP.add, inc="v_u")
                pr.wait("scalar", "v_u", pr.val("v_u"))
                sq_marks.append(pr.se("activation", st_std[:, s_:s_ + 1],
                                      st_tmp[:, 6 + s_:7 + s_],
                                      AF.Sqrt, bias=st_tmp[:, 4 + s_:5 + s_], scale=1.0 / D,
                                      inc="s_u"))
                pr.wait("vector", "s_u", pr.val("s_u"))
                pr.ve("reciprocal", st_rstd[:, s_:s_ + 1], st_std[:, s_:s_ + 1])
                pr.drain("vector")
                if len(sxf_marks) >= 2:
                    pr.wait("vector", "s_xf", sxf_marks[-2])
                if len(xfb_marks) >= 2:
                    pr.wait("vector", "t_tr", xfb_marks[-2])
                pr.ve("tensor_scalar", xfn[s_], xfst[s_], st_negm[:, s_:s_ + 1],
                      st_rstd[:, s_:s_ + 1], OP.add, OP.mult, inc="v_u")
                pr.ve("tensor_copy", xfbf[s_], xfn[s_], inc="v_u")
                vu = pr.val("v_u")
                pr.wait("sync", "v_u", vu)
                pr.dma("sync", xf_s[qb * QB + tt * P: qb * QB + (tt + 1) * P, :],
                       xfn[s_], "s_xf")
                if "dbg_st" in debug_outs:
                    pr.dma("sync", dbg_st[qb, tt, :, 0:8], st_tmp, "s_dbg")
                    pr.dma("sync", dbg_st[qb, tt, :, 8:12], st_sq, "s_dbg")
                    pr.dma("sync", dbg_st[qb, tt, :, 12:16], st_std, "s_dbg")
                    pr.dma("sync", dbg_st[qb, tt, :, 16:20], st_rstd, "s_dbg")
                    pr.dma("sync", dbg_st[qb, tt, :, 20:24], st_negm, "s_dbg")
                pr.wait("tensor", "v_u", vu)
                srcs = [xfbf[s_][:, j * P:(j + 1) * P] for j in range(8)]
                def evx(pv, tt=tt, qb=qb):
                    xview = xfT.rearrange("p (j c) -> p j c", c=TQ)[
                        :, :, qb * QB + tt * P: qb * QB + (tt + 1) * P]
                    pvv = pv.rearrange("p (j c) -> p j c", c=P)
                    pr.ve("tensor_copy", xview, pvv, inc="v_tr")
                transpose_block(srcs, evx)
                sxf_marks.append(pr.val("s_xf"))
                xfb_marks.append(pr.val("t_tr"))
            qb_end_marks.append({"t_sc": pr.val("t_sc")})

    att_end = {"s_xf": pr.val("s_xf") if "ATT" in phases else 0}
    AR.pop()

    AR.push()
    wbuf = [AR.view(8 * D, BF16) for _ in range(3)]
    g1t = AR.view(D, F32)
    be1t = AR.view(D, F32)
    g2t = AR.view(D, F32)
    hbf = [AR.view(D, BF16) for _ in range(2)]
    hTb = [AR.view(8 * P, BF16) for _ in range(2)]
    acc = AR.view(8 * D, F32)
    xfrd = [AR.view(D, F32) for _ in range(2)]
    t1b = [AR.view(D, F32) for _ in range(2)]
    u1b = [AR.view(D, F32)]
    outst = [AR.view(D, F32) for _ in range(2)]
    sqd2 = AR.view(D, BF16)
    b1r = [AR.view(D, BF16)]
    b2r = [AR.view(D, BF16)]
    be2sb = AR.view(D, BF16)
    gsc = AR.view(32, F32)
    gts = AR.view(6 * 8, F32)
    gtmp = AR.view(32, F32)
    st_sq2 = AR.view(8, F32)
    dbgz_sb = outst[0]
    st_s2 = AR.view(8, F32)

    PZ = [(0, 1024), (1024, 2048)]
    PO = (2048, 3072)
    SUMC = 3584
    GATC = 3592

    if "FFN" in phases:
        att_done = {n: pr.val(n) for n in ("t_u", "t_tr", "v_u", "s_xf", "v_tr")}
        for q_ in ("gpsimd", "sync"):
            pr.wait(q_, "t_tr", att_done["t_tr"])
            pr.wait(q_, "t_u", att_done["t_u"])
            pr.wait(q_, "s_xf", att_done["s_xf"])

        def load_expert_w(e, which, buf):
            srct = w1 if which == 0 else w2
            return pr.dma("gpsimd", wbuf[buf].rearrange("p (a d) -> p a d", d=D),
                          srct[e].rearrange("(a p) d -> p a d", p=P), "f_w")

        pr.dma("gpsimd", be2sb[0:4, :], be2, "f_w")
        ldw = {}
        z1_end = {}
        z2_end = {}
        ldw[0] = [load_expert_w(0, 0, 0), load_expert_w(0, 1, 1)]

        def load_gb(e, slot):
            m = pr.dma("gpsimd", g1t, g1[e:e + 1, :].partition_broadcast(P), "f_g")
            m = pr.dma("gpsimd", be1t, be1[e:e + 1, :].partition_broadcast(P), "f_g")
            m = pr.dma("gpsimd", g2t, g2[e:e + 1, :].partition_broadcast(P), "f_g")
            m = pr.dma("gpsimd", b1r[slot][0:1, :], b1[e:e + 1, :], "f_g")
            m = pr.dma("gpsimd", b2r[slot][0:1, :], b2[e:e + 1, :], "f_g")
            return m

        stt1_marks = {0: [], 1: []}
        z1sq_mark = {}
        z2sq_mark = [0]
        fstats_mark = [0]
        o_free_mark = [0]

        def ln_chain(zsrc, k, sumsrc, g_t, be_t, out_apply):
            pr.se("activation", sqd2, zsrc, AF.Square, accum_out=st_sq[:, k:k + 1])
            pr.se("activation", st_negm[:, k:k + 1], sumsrc, AF.Identity, scale=-1.0 / D)
            pr.se("activation", st_tmp[:, k:k + 1], st_negm[:, k:k + 1], AF.Square,
                  inc="s_f")
            m0 = pr.val("s_f")
            pr.wait("vector", "s_f", m0)
            pr.ve("tensor_copy", st_sq2[:, k:k + 1], st_sq[:, k:k + 1])
            pr.ve("tensor_scalar", st_tmp[:, 4 + k:5 + k], st_tmp[:, k:k + 1],
                  -1.0, EPS, OP.mult, OP.add, inc="v_f")
            pr.wait("scalar", "v_f", pr.val("v_f"))
            pr.se("activation", st_std[:, k:k + 1], st_sq2[:, k:k + 1], AF.Sqrt,
                  bias=st_tmp[:, 4 + k:5 + k], scale=1.0 / D, inc="s_f")
            fstats_mark[0] = pr.val("s_f")
            pr.wait("vector", "s_f", pr.val("s_f"))
            pr.ve("reciprocal", st_rstd[:, k:k + 1], st_std[:, k:k + 1])
            pr.drain("vector")
            out_apply(st_negm[:, k:k + 1], st_rstd[:, k:k + 1])
            return m0

        pr.wait("tensor", "v_u", pr.val("v_u"))
        pr.wait("tensor", "v_no", pr.val("v_no"))
        pr.wait("tensor", "v_oe", pr.val("v_oe"))
        pr.wait("tensor", "s_ex", pr.val("s_ex"))

        for ch in range(2):
            ch0 = ch * 8 * P

            pr.wait("tensor", "v_tr", pr.val("v_tr"))
            pr.wait("tensor", "g_ld", pr.val("g_ld"))
            pr.wait("tensor", "s_f", fstats_mark[0])
            for tt in range(8):
                for c in range(8):
                    pr.mm(psv(GATC + 4 * tt, GATC + 4 * tt + 4),
                          xfT[:, c * TQ + ch0 + tt * P: c * TQ + ch0 + (tt + 1) * P],
                          gw_bf[:, c * NE:(c + 1) * NE], c == 0, False)
                pr.mm(psv(GATC + 4 * tt, GATC + 4 * tt + 4), ones1[0:1, 0:P],
                      gb_bf[0:1, :], False, True, inc="t_g" if tt == 7 else None)
            pr.wait("vector", "t_g", pr.val("t_g"))
            pr.ve("tensor_copy", gsc, psv(GATC, GATC + 32))
            pr.drain("vector")
            g3 = gsc.rearrange("p (a e) -> p a e", e=NE)
            ga, gb_, gc, gd = (gts[:, 0:8], gts[:, 8:16], gts[:, 16:24], gts[:, 24:32])
            gm1v, gm2v = gts[:, 32:40], gts[:, 40:48]
            pr.ve("tensor_max", ga, g3[:, :, 0], g3[:, :, 1])
            pr.ve("tensor_tensor", gb_, g3[:, :, 0], g3[:, :, 1], OP.min)
            pr.ve("tensor_max", gc, g3[:, :, 2], g3[:, :, 3])
            pr.ve("tensor_tensor", gd, g3[:, :, 2], g3[:, :, 3], OP.min)
            pr.drain("vector")
            pr.ve("tensor_max", gm1v, ga, gc)
            pr.ve("tensor_tensor", ga, ga, gc, OP.min)
            pr.ve("tensor_max", gb_, gb_, gd)
            pr.drain("vector")
            pr.ve("tensor_max", gm2v, ga, gb_)
            pr.drain("vector")
            m1b = gm1v.unsqueeze(2).broadcast_to([P, 8, NE])
            m2b = gm2v.unsqueeze(2).broadcast_to([P, 8, NE])
            gt3 = gtmp.rearrange("p (a e) -> p a e", e=NE)
            pr.ve("tensor_sub", gt3, g3, m1b)
            pr.ve("tensor_sub", gd, gm2v, gm1v, inc="v_g")
            pr.drain("vector")
            pr.wait("scalar", "v_g", pr.val("v_g"))
            pr.se("activation", wgate, gtmp, AF.Exp)
            pr.se("activation", gts[:, 0:8], gd, AF.Exp, inc="s_g")
            pr.wait("vector", "s_g", pr.val("s_g"))
            pr.ve("tensor_scalar_add", gts[:, 16:24], gts[:, 0:8], 1.0)
            pr.drain("vector")
            pr.ve("reciprocal", gts[:, 8:16], gts[:, 16:24])
            pr.drain("vector")
            pr.ve("tensor_tensor", gt3, g3, m2b, OP.is_ge)
            pr.drain("vector")
            w3 = wgate.rearrange("p (a e) -> p a e", e=NE)
            pr.ve("tensor_mul", w3, w3, gt3)
            pr.drain("vector")
            rdb = gts[:, 8:16].unsqueeze(2).broadcast_to([P, 8, NE])
            pr.ve("tensor_mul", w3, w3, rdb, inc="v_g")
            pr.drain("vector")
            pr.ve("tensor_copy", wgate_bf, wgate, inc="v_g")
            if "dbg_w" in debug_outs:
                pr.wait("sync", "v_g", pr.val("v_g"))
                pr.dma("sync", dbg_w[ch], wgate, "s_dbg")
            for tt in range(8):
                pr.wait("tensor", "v_g", pr.val("v_g"))
                pr.wait("tensor", "v_tr", pr.val("v_tr"))
                pr.tr(psbf(3072, 3584)[0:4, tt * P:(tt + 1) * P],
                      wgate_bf[:, tt * NE:(tt + 1) * NE], ident,
                      inc="t_tr" if tt == 7 else None)
            pr.wait("vector", "t_tr", pr.val("t_tr"))
            for tt in range(8):
                pr.ve("tensor_copy", wTg[tt][0:4, :],
                      psbf(3072, 3584)[0:4, tt * P:(tt + 1) * P],
                      inc="v_tr" if tt == 7 else None)
            pr.wait("tensor", "f_w", 16)
            for tt in range(8):
                pr.wait("tensor", "v_tr", pr.val("v_tr"))
                if tt > 0:
                    pr.wait("tensor", "v_f", pr.val("v_f"))
                for nb in range(2):
                    pr.mm(psv(PO[0] + nb * QB, PO[0] + (nb + 1) * QB),
                          wTg[tt][0:4, :], be2sb[0:4, nb * QB:(nb + 1) * QB],
                          True, True, inc="t_z" if nb == 1 else None)
                pr.wait("vector", "t_z", pr.val("t_z"))
                pr.ve("tensor_copy", acc[:, tt * D:(tt + 1) * D], psv(*PO), inc="v_f")
            o_free_mark[0] = pr.val("v_f")
            if "dbg_acc" in debug_outs and ch == 0:
                pr.wait("sync", "v_f", pr.val("v_f"))
                pr.dma("sync", dbg_acc[0], acc, "s_dbg")

            for e in range(NE):
                ge = ch * NE + e
                wa = (2 * ge) % 3
                wb = (2 * ge + 1) % 3
                gslot = 0
                if ge >= 1:
                    pr.wait("gpsimd", "v_f", pr.val("v_f"))
                    pr.wait("gpsimd", "t_z", pr.val("t_z"))
                ldg = load_gb(e, gslot)
                gn = ge + 1
                if gn < 8 and gn not in ldw:
                    if ge - 1 in z2_end:
                        pr.wait("gpsimd", "t_z", z2_end[ge - 1])
                    ldw[gn] = [load_expert_w(gn % NE, 0, (2 * gn) % 3), None]
                pr.wait("scalar", "f_g", ldg)
                pr.se("activation", sqd2[0:1, :], b1r[gslot][0:1, :], AF.Identity,
                      accum_out=bsum_f[0:1, e:e + 1], inc="s_f")
                pr.wait("vector", "s_f", pr.val("s_f"))
                pr.ve("tensor_copy", bsum_bf[0:1, e:e + 1], bsum_f[0:1, e:e + 1],
                      inc="v_f")
                pr.wait("scalar", "f_w", ldw[ge][0])
                for j in range(8):
                    pr.se("activation", sqd2, wbuf[wa][:, j * D:(j + 1) * D], AF.Identity,
                          accum_out=wsum_f[:, j:j + 1], inc="s_f" if j == 7 else None)
                pr.wait("vector", "s_f", pr.val("s_f"))
                pr.wait("vector", "t_z", pr.val("t_z"))
                pr.ve("tensor_copy", wsum_bf[:, 0:8], wsum_f[:, 0:8], inc="v_f")
                wsum_ready = pr.val("v_f")

                pr.wait("tensor", "s_f", max(z1sq_mark.values()) if z1sq_mark else 0)
                pr.wait("tensor", "v_f", wsum_ready)
                pr.wait("tensor", "f_g", ldg)
                for tt in range(8):
                    for c in range(8):
                        pr.mm(psv(SUMC + tt, SUMC + tt + 1),
                              xfT[:, c * TQ + ch0 + tt * P: c * TQ + ch0 + (tt + 1) * P],
                              wsum_bf[:, c:c + 1], c == 0, False)
                    pr.mm(psv(SUMC + tt, SUMC + tt + 1), ones1[0:1, 0:P],
                          bsum_bf[0:1, e:e + 1], False, True,
                          inc="t_z" if tt == 7 else None)
                pr.wait("tensor", "f_w", ldw[ge][0])
                if "dbg_wb" in debug_outs and ch == 0 and e == 0:
                    pr.wait("sync", "f_w", ldw[ge][0])
                    pr.dma("sync", dbg_wb, wbuf[wa], "s_dbg")

                for tt in range(8):
                    zb = PZ[tt % 2]
                    k1, k2 = tt % 2, 2 + tt % 2
                    if len(stt1_marks[tt % 2]) >= 1:
                        pr.wait("tensor", "v_f", stt1_marks[tt % 2][-1])
                    pr.wait("tensor", "s_f", fstats_mark[0])
                    for nb in range(2):
                        c0 = zb[0] + nb * QB
                        for c in range(8):
                            pr.mm(psv(c0, c0 + QB),
                                  xfT[:, c * TQ + ch0 + tt * P: c * TQ + ch0 + (tt + 1) * P],
                                  wbuf[wa][:, c * D + nb * QB:c * D + (nb + 1) * QB],
                                  c == 0, False)
                        pr.mm(psv(c0, c0 + QB), ones1[0:1, 0:P],
                              b1r[gslot][0:1, nb * QB:(nb + 1) * QB], False, True,
                              inc="t_z" if nb == 1 else None)
                    if tt == 7:
                        z1_end[ge] = pr.val("t_z")
                        gn = ge + 1
                        if gn < 8:
                            pr.wait("gpsimd", "t_z", z1_end[ge])
                            ldw[gn][1] = load_expert_w(gn % NE, 1, (2 * gn + 1) % 3)
                    pr.wait("scalar", "t_z", pr.val("t_z"))

                    def apply1(negm, rstd, tt=tt, zb=zb, ch=ch, e=e):
                        t1 = t1b[tt % 2]
                        if "dbg_z" in debug_outs and ch == 0 and e == 0 and tt == 0:
                            pr.ve("tensor_copy", dbgz_sb, psv(*zb), inc="v_f")
                            pr.wait("sync", "v_f", pr.val("v_f"))
                            pr.dma("sync", dbg_z[0], dbgz_sb, "s_dbg")
                        pr.ve("scalar_tensor_tensor", t1, psv(*zb), negm, g1t,
                              OP.add, OP.mult, inc="v_f")
                        stt1_marks[tt % 2].append(pr.val("v_f"))
                        pr.ve("scalar_tensor_tensor", t1, t1, rstd, be1t,
                              OP.mult, OP.add)
                        pr.ve("tensor_scalar_max", hbf[tt % 2], t1, 0.0, inc="v_f")
                        if "dbg_h" in debug_outs and ch == 0 and e == 0 and tt == 0:
                            pr.ve("tensor_copy", outst[1].bitcast(BF16)[:, 0:D], hbf[0], inc="v_f")
                            pr.wait("sync", "v_f", pr.val("v_f"))
                            pr.dma("sync", dbg_h[0], outst[1].bitcast(BF16)[:, 0:D], "s_dbg")

                    z1sq_mark[tt % 2] = ln_chain(psv(*zb), k1,
                                                 psv(SUMC + tt, SUMC + tt + 1),
                                                 g1t, be1t, apply1)
                    h_ready = pr.val("v_f")

                    pr.wait("tensor", "v_f", h_ready)
                    pr.wait("tensor", "v_tr", pr.val("v_tr"))
                    for j in range(8):
                        pr.tr(psbf(3072, 3584)[:, j * P:(j + 1) * P],
                              hbf[tt % 2][:, j * P:(j + 1) * P], ident,
                              inc="t_tr" if j == 7 else None)
                    pr.wait("vector", "t_tr", pr.val("t_tr"))
                    pr.ve("tensor_copy", hTb[tt % 2], psbf(3072, 3584), inc="v_tr")
                    ht_ready = pr.val("v_tr")

                    pr.wait("tensor", "v_tr", ht_ready)
                    pr.wait("tensor", "v_f", o_free_mark[0])
                    pr.wait("tensor", "s_f", z2sq_mark[0])
                    pr.wait("tensor", "f_w", ldw[ge][1])
                    for nb in range(2):
                        c0 = PO[0] + nb * QB
                        for j in range(8):
                            pr.mm(psv(c0, c0 + QB),
                                  hTb[tt % 2][:, j * P:(j + 1) * P],
                                  wbuf[wb][:, j * D + nb * QB:j * D + (nb + 1) * QB],
                                  j == 0, False)
                        pr.mm(psv(c0, c0 + QB), ones1[0:1, 0:P],
                              b2r[gslot][0:1, nb * QB:(nb + 1) * QB], False, True,
                              inc="t_z" if nb == 1 else None)
                    pr.wait("scalar", "t_z", pr.val("t_z"))
                    pr.se("activation", sqd2, psv(*PO), AF.Identity,
                          accum_out=st_sq[:, k2:k2 + 1], inc="s_f")
                    pr.wait("vector", "s_f", pr.val("s_f"))
                    pr.ve("tensor_copy", st_s2[:, k2:k2 + 1], st_sq[:, k2:k2 + 1],
                          inc="v_f")
                    pr.wait("scalar", "v_f", pr.val("v_f"))

                    def apply2(negm, rstd, tt=tt, e=e):
                        u1 = u1b[0]
                        pr.ve("scalar_tensor_tensor", u1, psv(*PO), negm, g2t,
                              OP.add, OP.mult, inc="v_f")
                        o_free_mark[0] = pr.val("v_f")
                        pr.ve("tensor_scalar_mul", st_rstd[:, 3:4], rstd,
                              wgate[:, tt * NE + e: tt * NE + e + 1])
                        pr.drain("vector")
                        ab_ = acc[:, tt * D:(tt + 1) * D]
                        pr.ve("scalar_tensor_tensor", ab_, u1, st_rstd[:, 3:4], ab_,
                              OP.mult, OP.add, inc="v_f")

                    z2sq_mark[0] = ln_chain(psv(*PO), k2, st_s2[:, k2:k2 + 1], g2t,
                                            None, apply2)
                    if tt == 7:
                        z2_end[ge] = pr.val("t_z")
                        if "dbg_acc" in debug_outs and ch == 0:
                            pr.wait("sync", "v_f", pr.val("v_f"))
                            pr.dma("sync", dbg_acc[1 + e], acc, "s_dbg")

            fin_marks = []
            for tt in range(8):
                xr2 = xfrd[tt % 2]
                if len(fin_marks) >= 2:
                    pr.wait("sync", "v_f", fin_marks[-2])
                pr.dma("sync", xr2, xf_s[ch0 + tt * P: ch0 + (tt + 1) * P, :], "s_x2")
                sx2 = pr.val("s_x2")
                pr.wait("vector", "s_x2", sx2)
                pr.wait("vector", "s_f", fstats_mark[0])
                k = tt % 2
                fm = pr.ve("scalar_tensor_tensor", t1b[k], xr2, 2.0, acc[:, tt * D:(tt + 1) * D],
                           OP.mult, OP.add, accum_out=st_tmp[:, k:k + 1], inc="v_f")
                fin_marks.append(fm)
                pr.wait("scalar", "v_f", pr.val("v_f"))

                def applyf(negm, rstd, tt=tt, k=k):
                    pr.wait("vector", "s_out", max(0, pr.val("s_out") - 16))
                    pr.ve("tensor_scalar", outst[k], t1b[k], negm, rstd,
                          OP.add, OP.mult, inc="v_f")
                    vo = pr.val("v_f")
                    pr.wait("sync", "v_f", vo)
                    pr.dma("sync", out[ch0 + tt * P: ch0 + (tt + 1) * P, :], outst[k],
                           "s_out")

                ln_chain(t1b[k], k, st_tmp[:, k:k + 1], None, None, applyf)

        pr.wait("sync", "s_out", pr.val("s_out"))

    AR.pop()

    if "FFN" not in phases:
        AR.push()
        ostub = AR.view(D, F32)
        pr.wait("vector", "s_st", phaseA_store)
        pr.ve("memset", ostub, 0.0, inc="v_misc")
        pr.wait("sync", "v_misc", pr.val("v_misc"))
        for i in range(TQ // P):
            pr.dma("sync", out[i * P:(i + 1) * P, :], ostub, "s_out")
        pr.wait("sync", "s_out", pr.val("s_out"))
        AR.pop()

    pr.run()
    return nc


def kernel(L0_attn_x, L0_attn_Wq, L0_attn_Wk, L0_attn_Wv, L0_attn_Wu, L0_attn_bu,
           L0_ffn_gate_W, L0_ffn_gate_b, L0_ffn_W1, L0_ffn_b1, L0_ffn_g1, L0_ffn_be1,
           L0_ffn_W2, L0_ffn_b2, L0_ffn_g2, L0_ffn_be2):
    x = np.ascontiguousarray(np.asarray(L0_attn_x, dtype=np.float32))
    B = x.shape[0]
    nc = build()
    res = run_bass_kernel_spmd(nc, make_in_maps(x, locals()), core_ids=list(range(8)))
    outp = np.empty((B, T, D), dtype=np.float32)
    for c in range(8):
        b, qh = c // 2, c % 2
        outp[b, qh * TQ:(qh + 1) * TQ] = res.results[c]["out"]
    return outp


def make_in_maps(x, env):
    eye = np.eye(P, dtype=np.float32)
    f = np.float32
    maps = []
    for c in range(8):
        b, qh = c // 2, c % 2
        maps.append({
            "xkv": x[b],
            "xq": np.ascontiguousarray(x[b, qh * TQ:(qh + 1) * TQ]),
            "wq": np.asarray(env["L0_attn_Wq"], f),
            "wk": np.asarray(env["L0_attn_Wk"], f),
            "wv": np.asarray(env["L0_attn_Wv"], f),
            "wu": np.asarray(env["L0_attn_Wu"], f),
            "bu": np.asarray(env["L0_attn_bu"], f).reshape(1, D),
            "gw": np.asarray(env["L0_ffn_gate_W"], f),
            "gb": np.asarray(env["L0_ffn_gate_b"], f).reshape(1, NE),
            "w1": np.asarray(env["L0_ffn_W1"], f),
            "b1": np.asarray(env["L0_ffn_b1"], f),
            "g1": np.asarray(env["L0_ffn_g1"], f),
            "be1": np.asarray(env["L0_ffn_be1"], f),
            "w2": np.asarray(env["L0_ffn_W2"], f),
            "b2": np.asarray(env["L0_ffn_b2"], f),
            "g2": np.asarray(env["L0_ffn_g2"], f),
            "be2": np.asarray(env["L0_ffn_be2"], f),
            "ident_in": eye,
        })
    return maps


# version 04
# speedup vs baseline: 1.2243x; 1.1159x over previous
import numpy as np
import concourse.bass as bass
import concourse.mybir as mybir
from concourse.bass_utils import run_bass_kernel_spmd

F32 = mybir.dt.float32
BF16 = mybir.dt.bfloat16
AF = mybir.ActivationFunctionType
OP = mybir.AluOpType
AX = mybir.AxisListType.X

P = 128
D = 1024
T = 4096
TQ = 2048
H = 16
NE = 4
QB = 512
NQB = TQ // QB
NST = T // P
EPS = 1e-5
QSCALE = 1.0 / 32.0

DEBUG_OUTS = ()


class Prog:
    ENGINES = ("sync", "gpsimd", "vector", "scalar", "tensor")

    def __init__(self, nc):
        self.nc = nc
        self.ops = {e: [] for e in self.ENGINES}
        self.sems = {}
        self.cnt = {}

    def sem(self, name):
        if name not in self.sems:
            self.sems[name] = self.nc.alloc_semaphore(name)
            self.cnt[name] = 0
        return self.sems[name]

    def inc(self, name, k=1):
        h = self.sem(name)
        self.cnt[name] += k
        return h, self.cnt[name]

    def val(self, name):
        self.sem(name)
        return self.cnt[name]

    def emit(self, engine, fn):
        self.ops[engine].append(fn)

    def wait(self, engine, name, value):
        if value <= 0:
            return
        h = self.sem(name)
        self.emit(engine, lambda eng, h=h, v=value: eng.wait_ge(h, v))

    def mm(self, out, lhsT, rhs, start, stop, inc=None, tile_position=None):
        if inc is None:
            self.emit("tensor", lambda t, o=out, l=lhsT, r=rhs, s0=start, s1=stop, tp=tile_position:
                      t.matmul(o, l, r, start=s0, stop=s1, tile_position=tp))
            return None
        h, v = self.inc(inc)
        self.emit("tensor", lambda t, o=out, l=lhsT, r=rhs, s0=start, s1=stop, h=h, tp=tile_position:
                  t.matmul(o, l, r, start=s0, stop=s1, tile_position=tp).then_inc(h, 1))
        return v

    def tr(self, out, in_, ident, inc=None):
        if inc is None:
            self.emit("tensor", lambda t, o=out, i=in_, d=ident: t.transpose(o, i, d))
            return None
        h, v = self.inc(inc)
        self.emit("tensor", lambda t, o=out, i=in_, d=ident, h=h: t.transpose(o, i, d).then_inc(h, 1))
        return v

    def ve(self, fname, *args, inc=None, **kwargs):
        if inc is None:
            self.emit("vector", lambda v, f=fname, a=args, k=kwargs: getattr(v, f)(*a, **k))
            return None
        h, val = self.inc(inc)
        self.emit("vector", lambda v, f=fname, a=args, k=kwargs, h=h:
                  getattr(v, f)(*a, **k).then_inc(h, 1))
        return val

    def se(self, fname, *args, inc=None, **kwargs):
        if inc is None:
            self.emit("scalar", lambda s, f=fname, a=args, k=kwargs: getattr(s, f)(*a, **k))
            return None
        h, val = self.inc(inc)
        self.emit("scalar", lambda s, f=fname, a=args, k=kwargs, h=h:
                  getattr(s, f)(*a, **k).then_inc(h, 1))
        return val

    def drain(self, engine):
        self.emit(engine, lambda eng: eng.drain())

    def dma(self, q, dst, src, sem):
        h, v = self.inc(sem, 16)
        self.emit(q, lambda eng, d=dst, s=src, h=h: eng.dma_start(d, s).then_inc(h, 16))
        return v

    def run(self):
        with self.nc.Block() as blk:
            for e in self.ENGINES:
                getattr(blk, e)(self._runner(e))

    def _runner(self, e):
        def f(eng):
            for op in self.ops[e]:
                op(eng)
        return f


def ceil32(x):
    return (x + 31) & ~31


class Arena:
    def __init__(self, nc, name, nbytes):
        self.nbytes = nbytes
        self.t = nc.alloc_sbuf_tensor(name, [P, nbytes // 2], BF16)
        self.ap = self.t.ap()
        self.off = 0
        self.marks = []
        self.hwm = 0

    def view(self, nelem, dtype, shape=None):
        esz = 2 if dtype == BF16 else 4
        off = ceil32(self.off)
        self.off = off + nelem * esz
        self.hwm = max(self.hwm, self.off)
        assert self.off <= self.nbytes, f"arena overflow {self.off} > {self.nbytes}"
        v = self.ap[:, off // 2: (off + nelem * esz) // 2]
        if dtype != BF16:
            v = v.bitcast(dtype)
        if shape is not None:
            assert int(np.prod(shape)) == nelem
            if len(shape) == 2:
                v = v.rearrange("p (a b) -> p a b", b=shape[1])
        return v

    def push(self):
        self.marks.append(self.off)

    def pop(self):
        self.off = self.marks.pop()


def build(debug_outs=DEBUG_OUTS, phases=("A", "ATT", "FFN")):
    nc = bass.Bass("TRN2", target_bir_lowering=False)
    pr = Prog(nc)

    xkv = nc.dram_tensor("xkv", [T, D], F32, kind="ExternalInput").ap()
    xq = nc.dram_tensor("xq", [TQ, D], F32, kind="ExternalInput").ap()
    wmat = {n: nc.dram_tensor(n, [D, D], F32, kind="ExternalInput").ap()
            for n in ("wq", "wk", "wv", "wu")}
    bu = nc.dram_tensor("bu", [1, D], F32, kind="ExternalInput").ap()
    gw = nc.dram_tensor("gw", [D, NE], F32, kind="ExternalInput").ap()
    gb = nc.dram_tensor("gb", [1, NE], F32, kind="ExternalInput").ap()
    w1 = nc.dram_tensor("w1", [NE, D, D], F32, kind="ExternalInput").ap()
    b1 = nc.dram_tensor("b1", [NE, D], F32, kind="ExternalInput").ap()
    g1 = nc.dram_tensor("g1", [NE, D], F32, kind="ExternalInput").ap()
    be1 = nc.dram_tensor("be1", [NE, D], F32, kind="ExternalInput").ap()
    w2 = nc.dram_tensor("w2", [NE, D, D], F32, kind="ExternalInput").ap()
    b2 = nc.dram_tensor("b2", [NE, D], F32, kind="ExternalInput").ap()
    g2 = nc.dram_tensor("g2", [NE, D], F32, kind="ExternalInput").ap()
    be2 = nc.dram_tensor("be2", [NE, D], F32, kind="ExternalInput").ap()
    ident_in = nc.dram_tensor("ident_in", [P, P], F32, kind="ExternalInput").ap()

    out = nc.dram_tensor("out", [TQ, D], F32, kind="ExternalOutput").ap()

    def scratch(name, shape, dtype):
        kind = "ExternalOutput" if name in debug_outs else "Internal"
        return nc.dram_tensor(name, shape, dtype, kind=kind).ap()

    kT_s = scratch("kT_s", [D, T], BF16)
    qT_s = scratch("qT_s", [D, TQ], BF16)
    v_s = scratch("v_s", [H // 2, T, P], BF16)
    xf_s = scratch("xf_s", [TQ, D], F32)
    dbg_rs = scratch("dbg_rs", [NQB, H, QB], F32)
    dbg_o = scratch("dbg_o", [NQB, P, 8 * QB], BF16)
    dbg_u = scratch("dbg_u", [NQB, 4, P, D], F32)
    dbg_st = scratch("dbg_st", [NQB, 4, P, 24], F32)
    dbg_w = scratch("dbg_w", [2, P, 32], F32)
    dbg_h = scratch("dbg_h", [2, P, D], BF16)
    dbg_acc = scratch("dbg_acc", [5, P, 8 * D], F32)
    dbg_z = scratch("dbg_z", [2, P, D], F32)
    dbg_wb = scratch("dbg_wb", [P, 8 * D], BF16)

    ident = nc.alloc_sbuf_tensor("ident", [P, P], BF16).ap()
    ones1 = nc.alloc_sbuf_tensor("ones1", [1, P], BF16).ap()
    wuT = nc.alloc_sbuf_tensor("wuT", [P, 8 * D], BF16).ap()
    bu_bf = nc.alloc_sbuf_tensor("bu_bf", [1, D], BF16).ap()
    xfT = nc.alloc_sbuf_tensor("xfT", [P, 8 * TQ], BF16).ap()
    gw_bf = nc.alloc_sbuf_tensor("gw_bf", [P, 8 * NE], BF16).ap()
    gb_bf = nc.alloc_sbuf_tensor("gb_bf", [1, NE], BF16).ap()
    wsum_f = nc.alloc_sbuf_tensor("wsum_f", [P, 16], F32).ap()
    wsum_bf = nc.alloc_sbuf_tensor("wsum_bf", [P, 16], BF16).ap()
    bsum_bf = nc.alloc_sbuf_tensor("bsum_bf", [1, 16], BF16).ap()
    bsum_f = nc.alloc_sbuf_tensor("bsum_f", [1, 16], F32).ap()
    st_negm = nc.alloc_sbuf_tensor("st_negm", [P, 4], F32).ap()
    st_tmp = nc.alloc_sbuf_tensor("st_tmp", [P, 8], F32).ap()
    st_sq = nc.alloc_sbuf_tensor("st_sq", [P, 4], F32).ap()
    st_std = nc.alloc_sbuf_tensor("st_std", [P, 4], F32).ap()
    st_rstd = nc.alloc_sbuf_tensor("st_rstd", [P, 4], F32).ap()
    wgate = nc.alloc_sbuf_tensor("wgate", [P, 32], F32).ap()
    wgate_bf = nc.alloc_sbuf_tensor("wgate_bf", [P, 32], BF16).ap()
    wTg = [nc.alloc_sbuf_tensor(f"wTg{i}", [NE, P], BF16).ap() for i in range(8)]
    gm1 = nc.alloc_sbuf_tensor("gm1", [P, 8], F32).ap()
    gm2 = nc.alloc_sbuf_tensor("gm2", [P, 8], F32).ap()
    gden = nc.alloc_sbuf_tensor("gden", [P, 8], F32).ap()
    gtmp = nc.alloc_sbuf_tensor("gtmp", [P, 32], F32).ap()

    psum = nc.alloc_psum_tensor("psum", [P, 4096], F32).ap()

    def psv(c0, c1, r0=0, r1=P):
        return psum[r0:r1, c0:c1]

    def psbf(c0, c1):
        return psum[:, c0:c1].bitcast(BF16)

    AR = Arena(nc, "arena", 140 * 1024)

    pr.dma("gpsimd", ident, ident_in, "g_ld")
    pr.ve("memset", ones1, 1.0, inc="v_misc")

    AR.push()
    wnat2 = [AR.view(8 * D, BF16) for _ in range(2)]
    wTq = AR.view(8 * D, BF16)
    wTk = AR.view(8 * D, BF16)
    wTv = AR.view(8 * D, BF16)
    xnat = [AR.view(4 * D, BF16) for _ in range(2)]
    xT = [AR.view(8 * QB, BF16) for _ in range(2)]
    stg = [AR.view(QB, BF16) for _ in range(4)]

    def transpose_block(srcs, dst_fn):
        pr.wait("tensor", "v_tr", pr.val("v_tr"))
        for i, s in enumerate(srcs):
            pr.tr(psbf(3072, 3584)[:, i * P:(i + 1) * P], s, ident,
                  inc="t_tr" if i == 7 else None)
        pr.wait("vector", "t_tr", pr.val("t_tr"))
        dst_fn(psbf(3072, 3584))

    wprep_marks = [0, 0]
    for wi, (wname, dst) in enumerate((("wq", wTq), ("wk", wTk), ("wv", wTv), ("wu", wuT))):
        wnat = wnat2[wi % 2]
        pr.wait("gpsimd", "t_tr", wprep_marks[wi % 2])
        ldv = pr.dma("gpsimd", wnat.rearrange("p (a d) -> p a d", d=D),
                     wmat[wname].rearrange("(a p) d -> p a d", p=P), "g_ld")
        pr.wait("tensor", "g_ld", ldv)
        for j in range(8):
            srcs = [wnat[:, i * D + j * P: i * D + (j + 1) * P] for i in range(8)]
            def ev(pv, j=j, dst=dst):
                pr.ve("tensor_copy", dst[:, j * D:(j + 1) * D], pv, inc="v_tr")
            transpose_block(srcs, ev)
        wprep_marks[wi % 2] = pr.val("t_tr")

    pr.dma("gpsimd", gw_bf.rearrange("p (a e) -> p a e", e=NE),
           gw.rearrange("(a p) e -> p a e", p=P), "g_ld")
    pr.dma("gpsimd", gb_bf, gb, "g_ld")

    projps = [(i * 512, (i + 1) * 512) for i in range(4)]
    pp = {"i": 0}
    evac_cnt = {i: 0 for i in range(4)}
    chunk_end_tpj = []
    stg_store_mark = {i: 0 for i in range(4)}

    def proj_tile(lhs_rhs_fn, stgbuf, dsts, scale=None):
        bank = pp["i"] % 4
        pp["i"] += 1
        c0, c1 = projps[bank]
        pr.wait("tensor", "v_pa", evac_cnt[bank])
        for c in range(8):
            lhsT, rhs = lhs_rhs_fn(c)
            pr.mm(psv(c0, c1), lhsT, rhs, c == 0, c == 7,
                  inc="t_pj" if c == 7 else None)
        pr.wait("vector", "t_pj", pr.val("t_pj"))
        if scale is None:
            v2 = pr.ve("tensor_copy", stgbuf, psv(c0, c1), inc="v_pa")
        else:
            v2 = pr.ve("tensor_scalar_mul", stgbuf, psv(c0, c1), scale, inc="v_pa")
        evac_cnt[bank] = v2
        pr.wait("sync", "v_pa", v2)
        for dst_ap, src_ap in dsts:
            pr.dma("sync", dst_ap, src_ap, "s_st")

    def do_chunk(src_rows, ct0, emit_k, emit_v, emit_q, buf_i):
        ct_cols = slice(ct0, ct0 + 512)
        xb, xtb = xnat[buf_i], xT[buf_i]
        if len(chunk_end_tpj) >= 2:
            pr.wait("gpsimd", "t_pj", chunk_end_tpj[-2])
        ldv = pr.dma("gpsimd", xb.rearrange("p (a d) -> p a d", d=D),
                     src_rows.rearrange("(a p) d -> p a d", p=P), "g_ld")
        pr.wait("tensor", "g_ld", ldv)
        for tt in range(4):
            srcs = [xb[:, tt * D + j * P: tt * D + (j + 1) * P] for j in range(8)]
            def ev(pv, tt=tt, xtb=xtb):
                xv = xtb.rearrange("p (j c) -> p j c", c=512)[:, :, tt * P:(tt + 1) * P]
                pvv = pv.rearrange("p (j c) -> p j c", c=P)
                pr.ve("tensor_copy", xv, pvv, inc="v_tr")
            transpose_block(srcs, ev)
        pr.wait("tensor", "v_tr", pr.val("v_tr"))
        if emit_k:
            for j in range(8):
                si = j % 4
                sb = stg[si]
                pr.wait("vector", "s_st", stg_store_mark[si])
                proj_tile(lambda c, j=j, xtb=xtb: (wTk[:, c * D + j * P:c * D + (j + 1) * P],
                                                   xtb[:, c * 512:(c + 1) * 512]),
                          sb, [(kT_s[j * P:(j + 1) * P, ct_cols], sb)])
                stg_store_mark[si] = pr.val("s_st")
        if emit_v:
            for tt in range(4):
                for nb in range(2):
                    si = (tt * 2 + nb) % 4
                    sb = stg[si]
                    pr.wait("vector", "s_st", stg_store_mark[si])
                    tok0 = ct0 + tt * P
                    dsts = []
                    for pq in range(4):
                        pair = nb * 4 + pq
                        dsts.append((v_s[pair, tok0:tok0 + P, :],
                                     sb.rearrange("p (a c) -> p a c", c=P)[:, pq, :]))
                    proj_tile(lambda c, tt=tt, nb=nb, xtb=xtb: (
                        xtb[:, c * 512 + tt * P:c * 512 + (tt + 1) * P],
                        wTv[:, c * D + nb * 512:c * D + (nb + 1) * 512]),
                        sb, dsts)
                    stg_store_mark[si] = pr.val("s_st")
        if emit_q:
            for j in range(8):
                si = j % 4
                sb = stg[si]
                pr.wait("vector", "s_st", stg_store_mark[si])
                proj_tile(lambda c, j=j, xtb=xtb: (wTq[:, c * D + j * P:c * D + (j + 1) * P],
                                                   xtb[:, c * 512:(c + 1) * 512]),
                          sb, [(qT_s[j * P:(j + 1) * P, ct_cols], sb)],
                          scale=QSCALE)
                stg_store_mark[si] = pr.val("s_st")
        chunk_end_tpj.append(pr.val("t_pj"))

    if "A" in phases:
        for ct in range(8):
            do_chunk(xkv[ct * 512:(ct + 1) * 512, :], ct * 512,
                     emit_k=True, emit_v=True, emit_q=False, buf_i=ct % 2)
        for ct in range(4):
            do_chunk(xq[ct * 512:(ct + 1) * 512, :], ct * 512,
                     emit_k=False, emit_v=False, emit_q=True, buf_i=ct % 2)

    AR.pop()
    phaseA_store = pr.val("s_st")

    AR.push()
    KT = [AR.view(T, BF16) for _ in range(2)]
    Vq = [AR.view(32 * 130, BF16) for _ in range(2)]
    QTb = [AR.view(8 * QB, BF16) for _ in range(2)]
    attA = AR.view(16 * QB, BF16)
    attB = AR.view(16 * QB, BF16)
    O_sbU = AR.view(8 * QB, BF16)
    tmpB = AR.view(QB, BF16)
    rs_st = AR.view(2 * QB, F32)
    coll = AR.view(QB, F32)
    rcoll = AR.view(QB, F32)
    rstbf = AR.view(8 * QB, BF16)
    xres = [AR.view(D, F32) for _ in range(2)]
    xfst = [AR.view(D, F32) for _ in range(2)]
    xfn = [AR.view(D, F32) for _ in range(2)]
    xfbf = [AR.view(D, BF16) for _ in range(2)]
    sqd = AR.view(D, BF16)

    SCA = (0, 1536)
    SCB = (1536, 3072)
    POA = (3072, 3584)
    POB = (3584, 4096)
    BCA = (0, 512)
    BCB = (512, 1024)
    PU = (1536, 2560)

    if "ATT" in phases:
        ldv = pr.dma("gpsimd", bu_bf, bu, "g_ld")
        for i in range(2):
            pr.ve("memset", Vq[i], 1.0, inc="v_misc")
        vq_ready = pr.val("v_misc")

        GROUPS = [3, 3, 3, 3, 2, 2]
        reg_mark = {0: 0, 1: 0}
        pair_end_tav = []
        qb_end_marks = []
        stt_marks = []
        sxf_marks = []
        xfb_marks = []
        sq_marks = []

        for qb in range(NQB):
            qtb = QTb[qb % 2]
            if qb >= 2:
                pr.wait("sync", "t_sc", qb_end_marks[qb - 2]["t_sc"])
            pr.dma("sync",
                   qtb.rearrange("p (j c) -> p j c", c=QB),
                   qT_s.rearrange("(j p) q -> p j q", p=P)[:, :, qb * QB:(qb + 1) * QB],
                   "a_q")
            qtb_ld = pr.val("a_q")

            for p in range(8):
                gi = len(pair_end_tav)
                kt, vq = KT[gi % 2], Vq[gi % 2]
                if gi >= 2:
                    pr.wait("sync", "t_avh", pair_end_tav[gi - 2])
                pr.wait("sync", "v_misc", vq_ready)
                pr.dma("sync", kt, kT_s[p * P:(p + 1) * P, :], "a_kv")
                vsrc = v_s[p].rearrange("(s q) c -> q s c", q=P)
                vdst = vq.rearrange("p (s c) -> p s c", c=130)
                pr.dma("sync", vdst[:, :, 0:64], vsrc[:, :, 0:64], "a_kv")
                pr.dma("sync", vdst[:, :, 65:129], vsrc[:, :, 64:128], "a_kv")
                kv_ld = pr.val("a_kv")

                def emit_attv(item, kt=kt, vq=vq):
                    st0, glen, sexv = item
                    pr.wait("tensor", "s_ex", sexv)
                    for k in range(glen):
                        st = st0 + k
                        slot = (st % 16) * QB
                        if st == 0:
                            pr.wait("tensor", "v_oe", pr.val("v_oe"))
                            pr.wait("tensor", "v_tr", pr.val("v_tr"))
                        fin = (st % 16 == 15) or (st == NST - 1)
                        pr.mm(psv(*POA, 0, 65), vq[:, st * 130: st * 130 + 65],
                              attA[:, slot:slot + QB], st == 0, st == NST - 1)
                        pr.mm(psv(*POB, 0, 65), vq[:, st * 130 + 65: st * 130 + 130],
                              attB[:, slot:slot + QB], st == 0, st == NST - 1,
                              inc="t_avh" if fin else None)

                pend = []
                for half in range(2):
                    for g, glen in enumerate(GROUPS):
                        st0 = half * 16 + sum(GROUPS[:g])
                        if g == 0 and half == 0:
                            pr.wait("tensor", "a_kv", kv_ld)
                            pr.wait("tensor", "a_q", qtb_ld)
                            pr.wait("tensor", "v_no", pr.val("v_no"))
                            pr.wait("tensor", "v_u", pr.val("v_u"))
                        pr.wait("tensor", "s_ex", reg_mark[0])
                        tscA = tscB = None
                        for k in range(glen):
                            st = st0 + k
                            last = (k == glen - 1)
                            if k == 0:
                                pass
                            pr.mm(psv(*SCA)[:, k * QB:(k + 1) * QB],
                                  kt[0:64, st * P:(st + 1) * P],
                                  qtb[0:64, p * QB:(p + 1) * QB],
                                  True, True, inc="t_sc" if last else None,
                                  tile_position=(0, 0))
                            if k == 0:
                                tscA = pr.val("t_sc") + (0 if last else glen - 1)
                                pr.wait("tensor", "s_ex", reg_mark[1])
                            pr.mm(psv(*SCB)[:, k * QB:(k + 1) * QB],
                                  kt[64:128, st * P:(st + 1) * P],
                                  qtb[64:128, p * QB:(p + 1) * QB],
                                  True, True, inc="t_sc" if last else None,
                                  tile_position=(64, 0))
                        tscA = pr.val("t_sc") - 1
                        tscB = pr.val("t_sc")
                        if g == 0:
                            pr.wait("scalar", "t_avh", pr.val("t_avh"))
                        sl0 = (st0 % 16) * QB
                        pr.wait("scalar", "t_sc", tscA)
                        pr.se("activation", attA[:, sl0:sl0 + glen * QB],
                              psv(*SCA)[:, 0:glen * QB], AF.Exp, inc="s_ex")
                        reg_mark[0] = pr.val("s_ex")
                        pr.wait("scalar", "t_sc", tscB)
                        pr.se("activation", attB[:, sl0:sl0 + glen * QB],
                              psv(*SCB)[:, 0:glen * QB], AF.Exp, inc="s_ex")
                        reg_mark[1] = pr.val("s_ex")
                        pend.append((st0, glen, pr.val("s_ex")))
                        if len(pend) >= 2:
                            emit_attv(pend.pop(0))
                    while pend:
                        emit_attv(pend.pop(0))
                pair_end_tav.append(pr.val("t_avh"))

                pr.wait("vector", "t_avh", pair_end_tav[-1])
                if gi >= 1:
                    pr.wait("vector", "g_rs", pr.val("g_rs"))
                    pr.wait("vector", "g_mb", pr.val("g_mb"))
                pr.ve("tensor_copy", O_sbU[0:64, p * QB:(p + 1) * QB],
                      psv(*POA, 0, 65)[0:64, :], inc="v_oe")
                pr.ve("tensor_copy", rs_st[64:65, 0:QB], psv(*POA, 64, 65), inc="v_oe")
                pr.ve("tensor_copy", tmpB[0:64, :], psv(*POB, 0, 65)[0:64, :], inc="v_oe")
                pr.ve("tensor_copy", rs_st[64:65, QB:2 * QB], psv(*POB, 64, 65), inc="v_oe")
                voe = pr.val("v_oe")
                pr.wait("gpsimd", "v_oe", voe)
                pr.dma("gpsimd", coll[2 * p:2 * p + 2, :],
                       rs_st[64:65, :].rearrange("p (a c) -> p a c", c=QB), "g_rs")
                pr.dma("gpsimd", O_sbU[64:128, p * QB:(p + 1) * QB], tmpB[0:64, :], "g_mb")

            pr.wait("vector", "g_rs", pr.val("g_rs"))
            pr.ve("reciprocal", rcoll[0:16, :], coll[0:16, :], inc="v_rc")
            for hf in range(2):
                pr.wait("gpsimd", "v_rc", pr.val("v_rc"))
                pr.wait("gpsimd", "t_bc", pr.val("t_bc"))
                pr.dma("gpsimd",
                       rstbf[0:1, :].rearrange("p (a c) -> p a c", c=QB),
                       rcoll[8 * hf:8 * hf + 8, :], "g_sc")
                gscv = pr.val("g_sc")
                for p4 in range(4):
                    p = hf * 4 + p4
                    pr.wait("tensor", "g_sc", gscv)
                    if p == 0:
                        pr.wait("tensor", "s_ex", pr.val("s_ex"))
                    pr.wait("tensor", "v_no", pr.val("v_no"))
                    r0 = 2 * p4 * QB
                    pr.mm(psv(*BCA, 0, 64), ones1[0:1, 0:64], rstbf[0:1, r0:r0 + QB],
                          True, True, inc="t_bc", tile_position=(0, 0))
                    pr.mm(psv(*BCB, 64, 128), ones1[0:1, 0:64],
                          rstbf[0:1, r0 + QB:r0 + 2 * QB],
                          True, True, inc="t_bc", tile_position=(0, 64))
                    pr.wait("vector", "t_bc", pr.val("t_bc"))
                    pr.wait("vector", "g_mb", pr.val("g_mb"))
                    blk_ = O_sbU[0:64, p * QB:(p + 1) * QB]
                    pr.ve("tensor_mul", blk_, blk_, psv(*BCA, 0, 64), inc="v_no")
                    blk2 = O_sbU[64:128, p * QB:(p + 1) * QB]
                    pr.ve("tensor_mul", blk2, blk2, psv(*BCB, 64, 128), inc="v_no")

            if "dbg_rs" in debug_outs:
                pr.wait("sync", "v_rc", pr.val("v_rc"))
                pr.dma("sync", dbg_rs[qb], coll[0:16, :], "s_dbg")
            if "dbg_o" in debug_outs:
                pr.wait("sync", "v_no", pr.val("v_no"))
                pr.dma("sync", dbg_o[qb], O_sbU, "s_dbg")

            vno = pr.val("v_no")
            for tt in range(4):
                xr = xres[tt % 2]
                if len(stt_marks) >= 2:
                    pr.wait("sync", "v_u", stt_marks[-2])
                pr.dma("sync", xr, xq[qb * QB + tt * P: qb * QB + (tt + 1) * P, :], "s_xr")
                sxr = pr.val("s_xr")
                if tt == 0:
                    pr.wait("tensor", "v_no", vno)
                    if qb == 0:
                        pr.wait("tensor", "g_ld", ldv)
                if stt_marks:
                    pr.wait("tensor", "v_u", stt_marks[-1])
                for nb in range(2):
                    c0 = PU[0] + nb * QB
                    for j in range(8):
                        pr.mm(psv(c0, c0 + QB), O_sbU[:, j * QB + tt * P:j * QB + (tt + 1) * P],
                              wuT[:, j * D + nb * QB:j * D + (nb + 1) * QB],
                              j == 0, False)
                    pr.mm(psv(c0, c0 + QB), ones1[0:1, 0:P], bu_bf[0:1, nb * QB:(nb + 1) * QB],
                          False, True, inc="t_u" if nb == 1 else None)
                s_ = tt % 2
                pr.wait("vector", "t_u", pr.val("t_u"))
                pr.wait("vector", "s_xr", sxr)
                if len(sq_marks) >= 2:
                    pr.wait("vector", "s_u", sq_marks[-2])
                vstt = pr.ve("scalar_tensor_tensor", xfst[s_], psv(PU[0], PU[0] + D), 1.0, xr,
                             OP.mult, OP.add, accum_out=st_tmp[:, s_:s_ + 1], inc="v_u")
                stt_marks.append(vstt)
                if "dbg_u" in debug_outs:
                    pr.wait("sync", "v_u", pr.val("v_u"))
                    pr.dma("sync", dbg_u[qb, tt], xfst[s_], "s_dbg")
                pr.wait("scalar", "v_u", pr.val("v_u"))
                pr.se("activation", sqd, xfst[s_], AF.Square,
                      accum_out=st_sq[:, s_:s_ + 1])
                pr.se("activation", st_negm[:, s_:s_ + 1], st_tmp[:, s_:s_ + 1],
                      AF.Identity, scale=-1.0 / D)
                pr.se("activation", st_tmp[:, 2 + s_:3 + s_], st_negm[:, s_:s_ + 1],
                      AF.Square, inc="s_u")
                pr.wait("vector", "s_u", pr.val("s_u"))
                pr.ve("tensor_copy", st_tmp[:, 6 + s_:7 + s_], st_sq[:, s_:s_ + 1])
                pr.ve("tensor_scalar", st_tmp[:, 4 + s_:5 + s_], st_tmp[:, 2 + s_:3 + s_],
                      -1.0, EPS, OP.mult, OP.add, inc="v_u")
                pr.wait("scalar", "v_u", pr.val("v_u"))
                sq_marks.append(pr.se("activation", st_std[:, s_:s_ + 1],
                                      st_tmp[:, 6 + s_:7 + s_],
                                      AF.Sqrt, bias=st_tmp[:, 4 + s_:5 + s_], scale=1.0 / D,
                                      inc="s_u"))
                pr.wait("vector", "s_u", pr.val("s_u"))
                pr.ve("reciprocal", st_rstd[:, s_:s_ + 1], st_std[:, s_:s_ + 1])
                pr.drain("vector")
                if len(sxf_marks) >= 2:
                    pr.wait("vector", "s_xf", sxf_marks[-2])
                if len(xfb_marks) >= 2:
                    pr.wait("vector", "t_tr", xfb_marks[-2])
                pr.ve("tensor_scalar", xfn[s_], xfst[s_], st_negm[:, s_:s_ + 1],
                      st_rstd[:, s_:s_ + 1], OP.add, OP.mult, inc="v_u")
                pr.ve("tensor_copy", xfbf[s_], xfn[s_], inc="v_u")
                vu = pr.val("v_u")
                pr.wait("sync", "v_u", vu)
                pr.dma("sync", xf_s[qb * QB + tt * P: qb * QB + (tt + 1) * P, :],
                       xfn[s_], "s_xf")
                if "dbg_st" in debug_outs:
                    pr.dma("sync", dbg_st[qb, tt, :, 0:8], st_tmp, "s_dbg")
                    pr.dma("sync", dbg_st[qb, tt, :, 8:12], st_sq, "s_dbg")
                    pr.dma("sync", dbg_st[qb, tt, :, 12:16], st_std, "s_dbg")
                    pr.dma("sync", dbg_st[qb, tt, :, 16:20], st_rstd, "s_dbg")
                    pr.dma("sync", dbg_st[qb, tt, :, 20:24], st_negm, "s_dbg")
                pr.wait("tensor", "v_u", vu)
                srcs = [xfbf[s_][:, j * P:(j + 1) * P] for j in range(8)]
                def evx(pv, tt=tt, qb=qb):
                    xview = xfT.rearrange("p (j c) -> p j c", c=TQ)[
                        :, :, qb * QB + tt * P: qb * QB + (tt + 1) * P]
                    pvv = pv.rearrange("p (j c) -> p j c", c=P)
                    pr.ve("tensor_copy", xview, pvv, inc="v_tr")
                transpose_block(srcs, evx)
                sxf_marks.append(pr.val("s_xf"))
                xfb_marks.append(pr.val("t_tr"))
            qb_end_marks.append({"t_sc": pr.val("t_sc")})

    att_end = {"s_xf": pr.val("s_xf") if "ATT" in phases else 0}
    AR.pop()

    AR.push()
    wbuf = [AR.view(8 * D, BF16) for _ in range(3)]
    g1t = AR.view(D, F32)
    be1t = AR.view(D, F32)
    g2t = AR.view(D, F32)
    hbf = [AR.view(D, BF16) for _ in range(2)]
    hTb = [AR.view(8 * P, BF16) for _ in range(2)]
    acc = AR.view(8 * D, F32)
    xfrd = [AR.view(D, F32) for _ in range(2)]
    t1b = [AR.view(D, F32) for _ in range(2)]
    u1b = [AR.view(D, F32)]
    outst = [AR.view(D, F32) for _ in range(2)]
    sqd2 = AR.view(D, BF16)
    b1r = [AR.view(D, BF16)]
    b2r = [AR.view(D, BF16)]
    be2sb = AR.view(D, BF16)
    gsc = AR.view(32, F32)
    gts = AR.view(6 * 8, F32)
    gtmp = AR.view(32, F32)
    st_sq2 = AR.view(8, F32)
    dbgz_sb = outst[0]
    st_s2 = AR.view(8, F32)

    PZ = [(0, 1024), (1024, 2048)]
    PO = (2048, 3072)
    SUMC = 3584
    GATC = 3592

    if "FFN" in phases:
        att_done = {n: pr.val(n) for n in ("t_u", "t_tr", "v_u", "s_xf", "v_tr")}
        for q_ in ("gpsimd", "sync"):
            pr.wait(q_, "t_tr", att_done["t_tr"])
            pr.wait(q_, "t_u", att_done["t_u"])
            pr.wait(q_, "s_xf", att_done["s_xf"])

        def load_expert_w(e, which, buf):
            srct = w1 if which == 0 else w2
            return pr.dma("gpsimd", wbuf[buf].rearrange("p (a d) -> p a d", d=D),
                          srct[e].rearrange("(a p) d -> p a d", p=P), "f_w")

        pr.dma("gpsimd", be2sb[0:4, :], be2, "f_w")
        ldw = {}
        z1_end = {}
        z2_end = {}
        ldw[0] = [load_expert_w(0, 0, 0), load_expert_w(0, 1, 1)]

        def load_gb(e, slot):
            m = pr.dma("gpsimd", g1t, g1[e:e + 1, :].partition_broadcast(P), "f_g")
            m = pr.dma("gpsimd", be1t, be1[e:e + 1, :].partition_broadcast(P), "f_g")
            m = pr.dma("gpsimd", g2t, g2[e:e + 1, :].partition_broadcast(P), "f_g")
            m = pr.dma("gpsimd", b1r[slot][0:1, :], b1[e:e + 1, :], "f_g")
            m = pr.dma("gpsimd", b2r[slot][0:1, :], b2[e:e + 1, :], "f_g")
            return m

        stt1_marks = {0: [], 1: []}
        z1sq_mark = {}
        z2sq_mark = [0]
        fstats_mark = [0]
        o_free_mark = [0]

        def ln_chain(zsrc, k, sumsrc, g_t, be_t, out_apply):
            pr.se("activation", sqd2, zsrc, AF.Square, accum_out=st_sq[:, k:k + 1])
            pr.se("activation", st_negm[:, k:k + 1], sumsrc, AF.Identity, scale=-1.0 / D)
            pr.se("activation", st_tmp[:, k:k + 1], st_negm[:, k:k + 1], AF.Square,
                  inc="s_f")
            m0 = pr.val("s_f")
            pr.wait("vector", "s_f", m0)
            pr.ve("tensor_copy", st_sq2[:, k:k + 1], st_sq[:, k:k + 1])
            pr.ve("tensor_scalar", st_tmp[:, 4 + k:5 + k], st_tmp[:, k:k + 1],
                  -1.0, EPS, OP.mult, OP.add, inc="v_f")
            pr.wait("scalar", "v_f", pr.val("v_f"))
            pr.se("activation", st_std[:, k:k + 1], st_sq2[:, k:k + 1], AF.Sqrt,
                  bias=st_tmp[:, 4 + k:5 + k], scale=1.0 / D, inc="s_f")
            fstats_mark[0] = pr.val("s_f")
            pr.wait("vector", "s_f", pr.val("s_f"))
            pr.ve("reciprocal", st_rstd[:, k:k + 1], st_std[:, k:k + 1])
            pr.drain("vector")
            out_apply(st_negm[:, k:k + 1], st_rstd[:, k:k + 1])
            return m0

        pr.wait("tensor", "v_u", pr.val("v_u"))
        pr.wait("tensor", "v_no", pr.val("v_no"))
        pr.wait("tensor", "v_oe", pr.val("v_oe"))
        pr.wait("tensor", "s_ex", pr.val("s_ex"))

        for ch in range(2):
            ch0 = ch * 8 * P

            pr.wait("tensor", "v_tr", pr.val("v_tr"))
            pr.wait("tensor", "g_ld", pr.val("g_ld"))
            pr.wait("tensor", "s_f", fstats_mark[0])
            for tt in range(8):
                for c in range(8):
                    pr.mm(psv(GATC + 4 * tt, GATC + 4 * tt + 4),
                          xfT[:, c * TQ + ch0 + tt * P: c * TQ + ch0 + (tt + 1) * P],
                          gw_bf[:, c * NE:(c + 1) * NE], c == 0, False)
                pr.mm(psv(GATC + 4 * tt, GATC + 4 * tt + 4), ones1[0:1, 0:P],
                      gb_bf[0:1, :], False, True, inc="t_g" if tt == 7 else None)
            pr.wait("vector", "t_g", pr.val("t_g"))
            pr.ve("tensor_copy", gsc, psv(GATC, GATC + 32))
            pr.drain("vector")
            g3 = gsc.rearrange("p (a e) -> p a e", e=NE)
            ga, gb_, gc, gd = (gts[:, 0:8], gts[:, 8:16], gts[:, 16:24], gts[:, 24:32])
            gm1v, gm2v = gts[:, 32:40], gts[:, 40:48]
            pr.ve("tensor_max", ga, g3[:, :, 0], g3[:, :, 1])
            pr.ve("tensor_tensor", gb_, g3[:, :, 0], g3[:, :, 1], OP.min)
            pr.ve("tensor_max", gc, g3[:, :, 2], g3[:, :, 3])
            pr.ve("tensor_tensor", gd, g3[:, :, 2], g3[:, :, 3], OP.min)
            pr.drain("vector")
            pr.ve("tensor_max", gm1v, ga, gc)
            pr.ve("tensor_tensor", ga, ga, gc, OP.min)
            pr.ve("tensor_max", gb_, gb_, gd)
            pr.drain("vector")
            pr.ve("tensor_max", gm2v, ga, gb_)
            pr.drain("vector")
            m1b = gm1v.unsqueeze(2).broadcast_to([P, 8, NE])
            m2b = gm2v.unsqueeze(2).broadcast_to([P, 8, NE])
            gt3 = gtmp.rearrange("p (a e) -> p a e", e=NE)
            pr.ve("tensor_sub", gt3, g3, m1b)
            pr.ve("tensor_sub", gd, gm2v, gm1v, inc="v_g")
            pr.drain("vector")
            pr.wait("scalar", "v_g", pr.val("v_g"))
            pr.se("activation", wgate, gtmp, AF.Exp)
            pr.se("activation", gts[:, 0:8], gd, AF.Exp, inc="s_g")
            pr.wait("vector", "s_g", pr.val("s_g"))
            pr.ve("tensor_scalar_add", gts[:, 16:24], gts[:, 0:8], 1.0)
            pr.drain("vector")
            pr.ve("reciprocal", gts[:, 8:16], gts[:, 16:24])
            pr.drain("vector")
            pr.ve("tensor_tensor", gt3, g3, m2b, OP.is_ge)
            pr.drain("vector")
            w3 = wgate.rearrange("p (a e) -> p a e", e=NE)
            pr.ve("tensor_mul", w3, w3, gt3)
            pr.drain("vector")
            rdb = gts[:, 8:16].unsqueeze(2).broadcast_to([P, 8, NE])
            pr.ve("tensor_mul", w3, w3, rdb, inc="v_g")
            pr.drain("vector")
            pr.ve("tensor_copy", wgate_bf, wgate, inc="v_g")
            if "dbg_w" in debug_outs:
                pr.wait("sync", "v_g", pr.val("v_g"))
                pr.dma("sync", dbg_w[ch], wgate, "s_dbg")
            for tt in range(8):
                pr.wait("tensor", "v_g", pr.val("v_g"))
                pr.wait("tensor", "v_tr", pr.val("v_tr"))
                pr.tr(psbf(3072, 3584)[0:4, tt * P:(tt + 1) * P],
                      wgate_bf[:, tt * NE:(tt + 1) * NE], ident,
                      inc="t_tr" if tt == 7 else None)
            pr.wait("vector", "t_tr", pr.val("t_tr"))
            for tt in range(8):
                pr.ve("tensor_copy", wTg[tt][0:4, :],
                      psbf(3072, 3584)[0:4, tt * P:(tt + 1) * P],
                      inc="v_tr" if tt == 7 else None)
            pr.wait("tensor", "f_w", 16)
            for tt in range(8):
                pr.wait("tensor", "v_tr", pr.val("v_tr"))
                if tt > 0:
                    pr.wait("tensor", "v_f", pr.val("v_f"))
                for nb in range(2):
                    pr.mm(psv(PO[0] + nb * QB, PO[0] + (nb + 1) * QB),
                          wTg[tt][0:4, :], be2sb[0:4, nb * QB:(nb + 1) * QB],
                          True, True, inc="t_z" if nb == 1 else None)
                pr.wait("vector", "t_z", pr.val("t_z"))
                pr.ve("tensor_copy", acc[:, tt * D:(tt + 1) * D], psv(*PO), inc="v_f")
            o_free_mark[0] = pr.val("v_f")
            if "dbg_acc" in debug_outs and ch == 0:
                pr.wait("sync", "v_f", pr.val("v_f"))
                pr.dma("sync", dbg_acc[0], acc, "s_dbg")

            for e in range(NE):
                ge = ch * NE + e
                wa = (2 * ge) % 3
                wb = (2 * ge + 1) % 3
                gslot = 0
                if ge >= 1:
                    pr.wait("gpsimd", "v_f", pr.val("v_f"))
                    pr.wait("gpsimd", "t_z", pr.val("t_z"))
                ldg = load_gb(e, gslot)
                gn = ge + 1
                if gn < 8 and gn not in ldw:
                    if ge - 1 in z2_end:
                        pr.wait("gpsimd", "t_z", z2_end[ge - 1])
                    ldw[gn] = [load_expert_w(gn % NE, 0, (2 * gn) % 3), None]
                pr.wait("scalar", "f_g", ldg)
                pr.se("activation", sqd2[0:1, :], b1r[gslot][0:1, :], AF.Identity,
                      accum_out=bsum_f[0:1, e:e + 1], inc="s_f")
                pr.wait("vector", "s_f", pr.val("s_f"))
                pr.ve("tensor_copy", bsum_bf[0:1, e:e + 1], bsum_f[0:1, e:e + 1],
                      inc="v_f")
                pr.wait("scalar", "f_w", ldw[ge][0])
                for j in range(8):
                    pr.se("activation", sqd2, wbuf[wa][:, j * D:(j + 1) * D], AF.Identity,
                          accum_out=wsum_f[:, j:j + 1], inc="s_f" if j == 7 else None)
                pr.wait("vector", "s_f", pr.val("s_f"))
                pr.wait("vector", "t_z", pr.val("t_z"))
                pr.ve("tensor_copy", wsum_bf[:, 0:8], wsum_f[:, 0:8], inc="v_f")
                wsum_ready = pr.val("v_f")

                pr.wait("tensor", "s_f", max(z1sq_mark.values()) if z1sq_mark else 0)
                pr.wait("tensor", "v_f", wsum_ready)
                pr.wait("tensor", "f_g", ldg)
                for tt in range(8):
                    for c in range(8):
                        pr.mm(psv(SUMC + tt, SUMC + tt + 1),
                              xfT[:, c * TQ + ch0 + tt * P: c * TQ + ch0 + (tt + 1) * P],
                              wsum_bf[:, c:c + 1], c == 0, False)
                    pr.mm(psv(SUMC + tt, SUMC + tt + 1), ones1[0:1, 0:P],
                          bsum_bf[0:1, e:e + 1], False, True,
                          inc="t_z" if tt == 7 else None)
                pr.wait("tensor", "f_w", ldw[ge][0])
                if "dbg_wb" in debug_outs and ch == 0 and e == 0:
                    pr.wait("sync", "f_w", ldw[ge][0])
                    pr.dma("sync", dbg_wb, wbuf[wa], "s_dbg")

                ht_tr_mark = {0: 0, 1: 0}
                ht_used_mark = {0: 0, 1: 0}
                h_ready = {}

                def z_side(tt, wa=wa, gslot=gslot, ge=ge):
                    zb = PZ[tt % 2]
                    k1 = tt % 2
                    if len(stt1_marks[tt % 2]) >= 1:
                        pr.wait("tensor", "v_f", stt1_marks[tt % 2][-1])
                    pr.wait("tensor", "s_f", z1sq_mark.get(tt % 2, 0))
                    for nb in range(2):
                        c0 = zb[0] + nb * QB
                        for c in range(8):
                            pr.mm(psv(c0, c0 + QB),
                                  xfT[:, c * TQ + ch0 + tt * P: c * TQ + ch0 + (tt + 1) * P],
                                  wbuf[wa][:, c * D + nb * QB:c * D + (nb + 1) * QB],
                                  c == 0, False)
                        pr.mm(psv(c0, c0 + QB), ones1[0:1, 0:P],
                              b1r[gslot][0:1, nb * QB:(nb + 1) * QB], False, True,
                              inc="t_z" if nb == 1 else None)
                    if tt == 7:
                        z1_end[ge] = pr.val("t_z")
                        gn = ge + 1
                        if gn < 8:
                            pr.wait("gpsimd", "t_z", z1_end[ge])
                            ldw[gn][1] = load_expert_w(gn % NE, 1, (2 * gn + 1) % 3)
                    pr.wait("scalar", "t_z", pr.val("t_z"))
                    if len(stt1_marks[tt % 2]) >= 1:
                        pr.wait("scalar", "v_f", stt1_marks[tt % 2][-1])

                    def apply1(negm, rstd, tt=tt, zb=zb):
                        t1 = t1b[tt % 2]
                        pr.ve("scalar_tensor_tensor", t1, psv(*zb), negm, g1t,
                              OP.add, OP.mult, inc="v_f")
                        stt1_marks[tt % 2].append(pr.val("v_f"))
                        pr.ve("scalar_tensor_tensor", t1, t1, rstd, be1t,
                              OP.mult, OP.add)
                        pr.wait("vector", "t_tr", ht_tr_mark[tt % 2])
                        pr.ve("tensor_scalar_max", hbf[tt % 2], t1, 0.0, inc="v_f")

                    z1sq_mark[tt % 2] = ln_chain(psv(*zb), k1,
                                                 psv(SUMC + tt, SUMC + tt + 1),
                                                 g1t, be1t, apply1)
                    h_ready[tt] = pr.val("v_f")

                def o_side(tt, wb=wb, gslot=gslot, ge=ge, e=e):
                    k2 = 2 + tt % 2
                    pr.wait("tensor", "v_f", h_ready[tt])
                    pr.wait("tensor", "v_tr", pr.val("v_tr"))
                    for j in range(8):
                        pr.tr(psbf(3072, 3584)[:, j * P:(j + 1) * P],
                              hbf[tt % 2][:, j * P:(j + 1) * P], ident,
                              inc="t_tr" if j == 7 else None)
                    ht_tr_mark[tt % 2] = pr.val("t_tr")
                    pr.wait("vector", "t_tr", pr.val("t_tr"))
                    pr.wait("vector", "t_z", ht_used_mark[tt % 2])
                    pr.ve("tensor_copy", hTb[tt % 2], psbf(3072, 3584), inc="v_tr")
                    ht_ready = pr.val("v_tr")

                    pr.wait("tensor", "v_tr", ht_ready)
                    pr.wait("tensor", "v_f", o_free_mark[0])
                    pr.wait("tensor", "s_f", z2sq_mark[0])
                    pr.wait("tensor", "f_w", ldw[ge][1])
                    for nb in range(2):
                        c0 = PO[0] + nb * QB
                        for j in range(8):
                            pr.mm(psv(c0, c0 + QB),
                                  hTb[tt % 2][:, j * P:(j + 1) * P],
                                  wbuf[wb][:, j * D + nb * QB:j * D + (nb + 1) * QB],
                                  j == 0, False)
                        pr.mm(psv(c0, c0 + QB), ones1[0:1, 0:P],
                              b2r[gslot][0:1, nb * QB:(nb + 1) * QB], False, True,
                              inc="t_z" if nb == 1 else None)
                    ht_used_mark[tt % 2] = pr.val("t_z")
                    pr.wait("scalar", "t_z", pr.val("t_z"))
                    pr.se("activation", sqd2, psv(*PO), AF.Identity,
                          accum_out=st_sq[:, k2:k2 + 1], inc="s_f")
                    pr.wait("vector", "s_f", pr.val("s_f"))
                    pr.ve("tensor_copy", st_s2[:, k2:k2 + 1], st_sq[:, k2:k2 + 1],
                          inc="v_f")
                    pr.wait("scalar", "v_f", pr.val("v_f"))

                    def apply2(negm, rstd, tt=tt, e=e):
                        u1 = u1b[0]
                        pr.ve("scalar_tensor_tensor", u1, psv(*PO), negm, g2t,
                              OP.add, OP.mult, inc="v_f")
                        o_free_mark[0] = pr.val("v_f")
                        pr.ve("tensor_scalar_mul", st_rstd[:, 3:4], rstd,
                              wgate[:, tt * NE + e: tt * NE + e + 1])
                        pr.drain("vector")
                        ab_ = acc[:, tt * D:(tt + 1) * D]
                        pr.ve("scalar_tensor_tensor", ab_, u1, st_rstd[:, 3:4], ab_,
                              OP.mult, OP.add, inc="v_f")

                    z2sq_mark[0] = ln_chain(psv(*PO), k2, st_s2[:, k2:k2 + 1], g2t,
                                            None, apply2)
                    if tt == 7:
                        z2_end[ge] = pr.val("t_z")
                        if "dbg_acc" in debug_outs and ch == 0:
                            pr.wait("sync", "v_f", pr.val("v_f"))
                            pr.dma("sync", dbg_acc[1 + e], acc, "s_dbg")

                for step in range(9):
                    if step < 8:
                        z_side(step)
                    if step >= 1:
                        o_side(step - 1)

            fin_marks = []
            for tt in range(8):
                xr2 = xfrd[tt % 2]
                if len(fin_marks) >= 2:
                    pr.wait("sync", "v_f", fin_marks[-2])
                pr.dma("sync", xr2, xf_s[ch0 + tt * P: ch0 + (tt + 1) * P, :], "s_x2")
                sx2 = pr.val("s_x2")
                pr.wait("vector", "s_x2", sx2)
                pr.wait("vector", "s_f", fstats_mark[0])
                k = tt % 2
                fm = pr.ve("scalar_tensor_tensor", t1b[k], xr2, 2.0, acc[:, tt * D:(tt + 1) * D],
                           OP.mult, OP.add, accum_out=st_tmp[:, k:k + 1], inc="v_f")
                fin_marks.append(fm)
                pr.wait("scalar", "v_f", pr.val("v_f"))

                def applyf(negm, rstd, tt=tt, k=k):
                    pr.wait("vector", "s_out", max(0, pr.val("s_out") - 16))
                    pr.ve("tensor_scalar", outst[k], t1b[k], negm, rstd,
                          OP.add, OP.mult, inc="v_f")
                    vo = pr.val("v_f")
                    pr.wait("sync", "v_f", vo)
                    pr.dma("sync", out[ch0 + tt * P: ch0 + (tt + 1) * P, :], outst[k],
                           "s_out")

                ln_chain(t1b[k], k, st_tmp[:, k:k + 1], None, None, applyf)

        pr.wait("sync", "s_out", pr.val("s_out"))

    AR.pop()

    if "FFN" not in phases:
        AR.push()
        ostub = AR.view(D, F32)
        pr.wait("vector", "s_st", phaseA_store)
        pr.ve("memset", ostub, 0.0, inc="v_misc")
        pr.wait("sync", "v_misc", pr.val("v_misc"))
        for i in range(TQ // P):
            pr.dma("sync", out[i * P:(i + 1) * P, :], ostub, "s_out")
        pr.wait("sync", "s_out", pr.val("s_out"))
        AR.pop()

    pr.run()
    return nc


def kernel(L0_attn_x, L0_attn_Wq, L0_attn_Wk, L0_attn_Wv, L0_attn_Wu, L0_attn_bu,
           L0_ffn_gate_W, L0_ffn_gate_b, L0_ffn_W1, L0_ffn_b1, L0_ffn_g1, L0_ffn_be1,
           L0_ffn_W2, L0_ffn_b2, L0_ffn_g2, L0_ffn_be2):
    x = np.ascontiguousarray(np.asarray(L0_attn_x, dtype=np.float32))
    B = x.shape[0]
    nc = build()
    res = run_bass_kernel_spmd(nc, make_in_maps(x, locals()), core_ids=list(range(8)))
    outp = np.empty((B, T, D), dtype=np.float32)
    for c in range(8):
        b, qh = c // 2, c % 2
        outp[b, qh * TQ:(qh + 1) * TQ] = res.results[c]["out"]
    return outp


def make_in_maps(x, env):
    eye = np.eye(P, dtype=np.float32)
    f = np.float32
    maps = []
    for c in range(8):
        b, qh = c // 2, c % 2
        maps.append({
            "xkv": x[b],
            "xq": np.ascontiguousarray(x[b, qh * TQ:(qh + 1) * TQ]),
            "wq": np.asarray(env["L0_attn_Wq"], f),
            "wk": np.asarray(env["L0_attn_Wk"], f),
            "wv": np.asarray(env["L0_attn_Wv"], f),
            "wu": np.asarray(env["L0_attn_Wu"], f),
            "bu": np.asarray(env["L0_attn_bu"], f).reshape(1, D),
            "gw": np.asarray(env["L0_ffn_gate_W"], f),
            "gb": np.asarray(env["L0_ffn_gate_b"], f).reshape(1, NE),
            "w1": np.asarray(env["L0_ffn_W1"], f),
            "b1": np.asarray(env["L0_ffn_b1"], f),
            "g1": np.asarray(env["L0_ffn_g1"], f),
            "be1": np.asarray(env["L0_ffn_be1"], f),
            "w2": np.asarray(env["L0_ffn_W2"], f),
            "b2": np.asarray(env["L0_ffn_b2"], f),
            "g2": np.asarray(env["L0_ffn_g2"], f),
            "be2": np.asarray(env["L0_ffn_be2"], f),
            "ident_in": eye,
        })
    return maps


# version 05
# speedup vs baseline: 1.2274x; 1.0026x over previous
import numpy as np
import concourse.bass as bass
import concourse.mybir as mybir
from concourse.bass_utils import run_bass_kernel_spmd

F32 = mybir.dt.float32
BF16 = mybir.dt.bfloat16
AF = mybir.ActivationFunctionType
OP = mybir.AluOpType
AX = mybir.AxisListType.X

P = 128
D = 1024
T = 4096
TQ = 2048
H = 16
NE = 4
QB = 512
NQB = TQ // QB
NST = T // P
EPS = 1e-5
QSCALE = 1.0 / 32.0

DEBUG_OUTS = ()


class Prog:
    ENGINES = ("sync", "gpsimd", "vector", "scalar", "tensor")

    def __init__(self, nc):
        self.nc = nc
        self.ops = {e: [] for e in self.ENGINES}
        self.sems = {}
        self.cnt = {}

    def sem(self, name):
        if name not in self.sems:
            self.sems[name] = self.nc.alloc_semaphore(name)
            self.cnt[name] = 0
        return self.sems[name]

    def inc(self, name, k=1):
        h = self.sem(name)
        self.cnt[name] += k
        return h, self.cnt[name]

    def val(self, name):
        self.sem(name)
        return self.cnt[name]

    def emit(self, engine, fn):
        self.ops[engine].append(fn)

    def wait(self, engine, name, value):
        if value <= 0:
            return
        h = self.sem(name)
        self.emit(engine, lambda eng, h=h, v=value: eng.wait_ge(h, v))

    def mm(self, out, lhsT, rhs, start, stop, inc=None, tile_position=None):
        if inc is None:
            self.emit("tensor", lambda t, o=out, l=lhsT, r=rhs, s0=start, s1=stop, tp=tile_position:
                      t.matmul(o, l, r, start=s0, stop=s1, tile_position=tp))
            return None
        h, v = self.inc(inc)
        self.emit("tensor", lambda t, o=out, l=lhsT, r=rhs, s0=start, s1=stop, h=h, tp=tile_position:
                  t.matmul(o, l, r, start=s0, stop=s1, tile_position=tp).then_inc(h, 1))
        return v

    def tr(self, out, in_, ident, inc=None):
        if inc is None:
            self.emit("tensor", lambda t, o=out, i=in_, d=ident: t.transpose(o, i, d))
            return None
        h, v = self.inc(inc)
        self.emit("tensor", lambda t, o=out, i=in_, d=ident, h=h: t.transpose(o, i, d).then_inc(h, 1))
        return v

    def ve(self, fname, *args, inc=None, **kwargs):
        if inc is None:
            self.emit("vector", lambda v, f=fname, a=args, k=kwargs: getattr(v, f)(*a, **k))
            return None
        h, val = self.inc(inc)
        self.emit("vector", lambda v, f=fname, a=args, k=kwargs, h=h:
                  getattr(v, f)(*a, **k).then_inc(h, 1))
        return val

    def se(self, fname, *args, inc=None, **kwargs):
        if inc is None:
            self.emit("scalar", lambda s, f=fname, a=args, k=kwargs: getattr(s, f)(*a, **k))
            return None
        h, val = self.inc(inc)
        self.emit("scalar", lambda s, f=fname, a=args, k=kwargs, h=h:
                  getattr(s, f)(*a, **k).then_inc(h, 1))
        return val

    def drain(self, engine):
        self.emit(engine, lambda eng: eng.drain())

    def dma(self, q, dst, src, sem):
        h, v = self.inc(sem, 16)
        self.emit(q, lambda eng, d=dst, s=src, h=h: eng.dma_start(d, s).then_inc(h, 16))
        return v

    def run(self):
        with self.nc.Block() as blk:
            for e in self.ENGINES:
                getattr(blk, e)(self._runner(e))

    def _runner(self, e):
        def f(eng):
            for op in self.ops[e]:
                op(eng)
        return f


def ceil32(x):
    return (x + 31) & ~31


class Arena:
    def __init__(self, nc, name, nbytes):
        self.nbytes = nbytes
        self.t = nc.alloc_sbuf_tensor(name, [P, nbytes // 2], BF16)
        self.ap = self.t.ap()
        self.off = 0
        self.marks = []
        self.hwm = 0

    def view(self, nelem, dtype, shape=None):
        esz = 2 if dtype == BF16 else 4
        off = ceil32(self.off)
        self.off = off + nelem * esz
        self.hwm = max(self.hwm, self.off)
        assert self.off <= self.nbytes, f"arena overflow {self.off} > {self.nbytes}"
        v = self.ap[:, off // 2: (off + nelem * esz) // 2]
        if dtype != BF16:
            v = v.bitcast(dtype)
        if shape is not None:
            assert int(np.prod(shape)) == nelem
            if len(shape) == 2:
                v = v.rearrange("p (a b) -> p a b", b=shape[1])
        return v

    def push(self):
        self.marks.append(self.off)

    def pop(self):
        self.off = self.marks.pop()


def build(debug_outs=DEBUG_OUTS, phases=("A", "ATT", "FFN")):
    nc = bass.Bass("TRN2", target_bir_lowering=False)
    pr = Prog(nc)

    xkv = nc.dram_tensor("xkv", [T, D], F32, kind="ExternalInput").ap()
    xq = nc.dram_tensor("xq", [TQ, D], F32, kind="ExternalInput").ap()
    wmat = {n: nc.dram_tensor(n, [D, D], F32, kind="ExternalInput").ap()
            for n in ("wq", "wk", "wv", "wu")}
    bu = nc.dram_tensor("bu", [1, D], F32, kind="ExternalInput").ap()
    gw = nc.dram_tensor("gw", [D, NE], F32, kind="ExternalInput").ap()
    gb = nc.dram_tensor("gb", [1, NE], F32, kind="ExternalInput").ap()
    w1 = nc.dram_tensor("w1", [NE, D, D], F32, kind="ExternalInput").ap()
    b1 = nc.dram_tensor("b1", [NE, D], F32, kind="ExternalInput").ap()
    g1 = nc.dram_tensor("g1", [NE, D], F32, kind="ExternalInput").ap()
    be1 = nc.dram_tensor("be1", [NE, D], F32, kind="ExternalInput").ap()
    w2 = nc.dram_tensor("w2", [NE, D, D], F32, kind="ExternalInput").ap()
    b2 = nc.dram_tensor("b2", [NE, D], F32, kind="ExternalInput").ap()
    g2 = nc.dram_tensor("g2", [NE, D], F32, kind="ExternalInput").ap()
    be2 = nc.dram_tensor("be2", [NE, D], F32, kind="ExternalInput").ap()
    ident_in = nc.dram_tensor("ident_in", [P, P], F32, kind="ExternalInput").ap()

    out = nc.dram_tensor("out", [TQ, D], F32, kind="ExternalOutput").ap()

    def scratch(name, shape, dtype):
        kind = "ExternalOutput" if name in debug_outs else "Internal"
        return nc.dram_tensor(name, shape, dtype, kind=kind).ap()

    kT_s = scratch("kT_s", [D, T], BF16)
    qT_s = scratch("qT_s", [D, TQ], BF16)
    v_s = scratch("v_s", [H // 2, T, P], BF16)
    xf_s = scratch("xf_s", [TQ, D], F32)
    dbg_rs = scratch("dbg_rs", [NQB, H, QB], F32)
    dbg_o = scratch("dbg_o", [NQB, P, 8 * QB], BF16)
    dbg_u = scratch("dbg_u", [NQB, 4, P, D], F32)
    dbg_st = scratch("dbg_st", [NQB, 4, P, 24], F32)
    dbg_w = scratch("dbg_w", [2, P, 32], F32)
    dbg_h = scratch("dbg_h", [2, P, D], BF16)
    dbg_acc = scratch("dbg_acc", [5, P, 8 * D], F32)
    dbg_z = scratch("dbg_z", [2, P, D], F32)
    dbg_wb = scratch("dbg_wb", [P, 8 * D], BF16)

    ident = nc.alloc_sbuf_tensor("ident", [P, P], BF16).ap()
    ones1 = nc.alloc_sbuf_tensor("ones1", [1, P], BF16).ap()
    wuT = nc.alloc_sbuf_tensor("wuT", [P, 8 * D], BF16).ap()
    bu_bf = nc.alloc_sbuf_tensor("bu_bf", [1, D], BF16).ap()
    xfT = nc.alloc_sbuf_tensor("xfT", [P, 8 * TQ], BF16).ap()
    gw_bf = nc.alloc_sbuf_tensor("gw_bf", [P, 8 * NE], BF16).ap()
    gb_bf = nc.alloc_sbuf_tensor("gb_bf", [1, NE], BF16).ap()
    wsum_f = nc.alloc_sbuf_tensor("wsum_f", [P, 16], F32).ap()
    wsum_bf = nc.alloc_sbuf_tensor("wsum_bf", [P, 16], BF16).ap()
    bsum_bf = nc.alloc_sbuf_tensor("bsum_bf", [1, 16], BF16).ap()
    bsum_f = nc.alloc_sbuf_tensor("bsum_f", [1, 16], F32).ap()
    st_negm = nc.alloc_sbuf_tensor("st_negm", [P, 4], F32).ap()
    st_tmp = nc.alloc_sbuf_tensor("st_tmp", [P, 8], F32).ap()
    st_sq = nc.alloc_sbuf_tensor("st_sq", [P, 4], F32).ap()
    st_std = nc.alloc_sbuf_tensor("st_std", [P, 4], F32).ap()
    st_rstd = nc.alloc_sbuf_tensor("st_rstd", [P, 4], F32).ap()
    wgate = nc.alloc_sbuf_tensor("wgate", [P, 32], F32).ap()
    wgate_bf = nc.alloc_sbuf_tensor("wgate_bf", [P, 32], BF16).ap()
    wTg = [nc.alloc_sbuf_tensor(f"wTg{i}", [NE, P], BF16).ap() for i in range(8)]
    gm1 = nc.alloc_sbuf_tensor("gm1", [P, 8], F32).ap()
    gm2 = nc.alloc_sbuf_tensor("gm2", [P, 8], F32).ap()
    gden = nc.alloc_sbuf_tensor("gden", [P, 8], F32).ap()
    gtmp = nc.alloc_sbuf_tensor("gtmp", [P, 32], F32).ap()

    psum = nc.alloc_psum_tensor("psum", [P, 4096], F32).ap()

    def psv(c0, c1, r0=0, r1=P):
        return psum[r0:r1, c0:c1]

    def psbf(c0, c1):
        return psum[:, c0:c1].bitcast(BF16)

    AR = Arena(nc, "arena", 140 * 1024)

    pr.dma("gpsimd", ident, ident_in, "g_ld")
    pr.ve("memset", ones1, 1.0, inc="v_misc")

    AR.push()
    wnat2 = [AR.view(8 * D, BF16) for _ in range(2)]
    wTq = AR.view(8 * D, BF16)
    wTk = AR.view(8 * D, BF16)
    wTv = AR.view(8 * D, BF16)
    xnat = [AR.view(4 * D, BF16) for _ in range(2)]
    xT = [AR.view(8 * QB, BF16) for _ in range(2)]
    stg = [AR.view(QB, BF16) for _ in range(4)]

    def transpose_block(srcs, dst_fn):
        pr.wait("tensor", "v_tr", pr.val("v_tr"))
        for i, s in enumerate(srcs):
            pr.tr(psbf(3072, 3584)[:, i * P:(i + 1) * P], s, ident,
                  inc="t_tr" if i == 7 else None)
        pr.wait("vector", "t_tr", pr.val("t_tr"))
        dst_fn(psbf(3072, 3584))

    wprep_marks = [0, 0]
    for wi, (wname, dst) in enumerate((("wq", wTq), ("wk", wTk), ("wv", wTv), ("wu", wuT))):
        wnat = wnat2[wi % 2]
        pr.wait("gpsimd", "t_tr", wprep_marks[wi % 2])
        ldv = pr.dma("gpsimd", wnat.rearrange("p (a d) -> p a d", d=D),
                     wmat[wname].rearrange("(a p) d -> p a d", p=P), "g_ld")
        pr.wait("tensor", "g_ld", ldv)
        for j in range(8):
            srcs = [wnat[:, i * D + j * P: i * D + (j + 1) * P] for i in range(8)]
            def ev(pv, j=j, dst=dst):
                pr.ve("tensor_copy", dst[:, j * D:(j + 1) * D], pv, inc="v_tr")
            transpose_block(srcs, ev)
        wprep_marks[wi % 2] = pr.val("t_tr")

    pr.dma("gpsimd", gw_bf.rearrange("p (a e) -> p a e", e=NE),
           gw.rearrange("(a p) e -> p a e", p=P), "g_ld")
    pr.dma("gpsimd", gb_bf, gb, "g_ld")

    projps = [(i * 512, (i + 1) * 512) for i in range(4)]
    pp = {"i": 0}
    evac_cnt = {i: 0 for i in range(4)}
    chunk_end_tpj = []
    stg_store_mark = {i: 0 for i in range(4)}

    def proj_tile(lhs_rhs_fn, stgbuf, dsts, scale=None):
        bank = pp["i"] % 4
        pp["i"] += 1
        c0, c1 = projps[bank]
        pr.wait("tensor", "v_pa", evac_cnt[bank])
        for c in range(8):
            lhsT, rhs = lhs_rhs_fn(c)
            pr.mm(psv(c0, c1), lhsT, rhs, c == 0, c == 7,
                  inc="t_pj" if c == 7 else None)
        pr.wait("vector", "t_pj", pr.val("t_pj"))
        if scale is None:
            v2 = pr.ve("tensor_copy", stgbuf, psv(c0, c1), inc="v_pa")
        else:
            v2 = pr.ve("tensor_scalar_mul", stgbuf, psv(c0, c1), scale, inc="v_pa")
        evac_cnt[bank] = v2
        pr.wait("sync", "v_pa", v2)
        for dst_ap, src_ap in dsts:
            pr.dma("sync", dst_ap, src_ap, "s_st")

    def do_chunk(src_rows, ct0, emit_k, emit_v, emit_q, buf_i):
        ct_cols = slice(ct0, ct0 + 512)
        xb, xtb = xnat[buf_i], xT[buf_i]
        if len(chunk_end_tpj) >= 2:
            pr.wait("gpsimd", "t_pj", chunk_end_tpj[-2])
        ldv = pr.dma("gpsimd", xb.rearrange("p (a d) -> p a d", d=D),
                     src_rows.rearrange("(a p) d -> p a d", p=P), "g_ld")
        pr.wait("tensor", "g_ld", ldv)
        for tt in range(4):
            srcs = [xb[:, tt * D + j * P: tt * D + (j + 1) * P] for j in range(8)]
            def ev(pv, tt=tt, xtb=xtb):
                xv = xtb.rearrange("p (j c) -> p j c", c=512)[:, :, tt * P:(tt + 1) * P]
                pvv = pv.rearrange("p (j c) -> p j c", c=P)
                pr.ve("tensor_copy", xv, pvv, inc="v_tr")
            transpose_block(srcs, ev)
        pr.wait("tensor", "v_tr", pr.val("v_tr"))
        if emit_k:
            for j in range(8):
                si = j % 4
                sb = stg[si]
                pr.wait("vector", "s_st", stg_store_mark[si])
                proj_tile(lambda c, j=j, xtb=xtb: (wTk[:, c * D + j * P:c * D + (j + 1) * P],
                                                   xtb[:, c * 512:(c + 1) * 512]),
                          sb, [(kT_s[j * P:(j + 1) * P, ct_cols], sb)])
                stg_store_mark[si] = pr.val("s_st")
        if emit_v:
            for tt in range(4):
                for nb in range(2):
                    si = (tt * 2 + nb) % 4
                    sb = stg[si]
                    pr.wait("vector", "s_st", stg_store_mark[si])
                    tok0 = ct0 + tt * P
                    dsts = []
                    for pq in range(4):
                        pair = nb * 4 + pq
                        dsts.append((v_s[pair, tok0:tok0 + P, :],
                                     sb.rearrange("p (a c) -> p a c", c=P)[:, pq, :]))
                    proj_tile(lambda c, tt=tt, nb=nb, xtb=xtb: (
                        xtb[:, c * 512 + tt * P:c * 512 + (tt + 1) * P],
                        wTv[:, c * D + nb * 512:c * D + (nb + 1) * 512]),
                        sb, dsts)
                    stg_store_mark[si] = pr.val("s_st")
        if emit_q:
            for j in range(8):
                si = j % 4
                sb = stg[si]
                pr.wait("vector", "s_st", stg_store_mark[si])
                proj_tile(lambda c, j=j, xtb=xtb: (wTq[:, c * D + j * P:c * D + (j + 1) * P],
                                                   xtb[:, c * 512:(c + 1) * 512]),
                          sb, [(qT_s[j * P:(j + 1) * P, ct_cols], sb)],
                          scale=QSCALE)
                stg_store_mark[si] = pr.val("s_st")
        chunk_end_tpj.append(pr.val("t_pj"))

    if "A" in phases:
        for ct in range(8):
            do_chunk(xkv[ct * 512:(ct + 1) * 512, :], ct * 512,
                     emit_k=True, emit_v=True, emit_q=False, buf_i=ct % 2)
        for ct in range(4):
            do_chunk(xq[ct * 512:(ct + 1) * 512, :], ct * 512,
                     emit_k=False, emit_v=False, emit_q=True, buf_i=ct % 2)

    AR.pop()
    phaseA_store = pr.val("s_st")

    AR.push()
    KT = [AR.view(T, BF16) for _ in range(2)]
    Vq = [AR.view(32 * 130, BF16) for _ in range(2)]
    QTb = [AR.view(8 * QB, BF16) for _ in range(2)]
    attA = AR.view(16 * QB, BF16)
    attB = AR.view(16 * QB, BF16)
    O_sbU = AR.view(8 * QB, BF16)
    tmpB = AR.view(QB, BF16)
    rs_st = AR.view(2 * QB, F32)
    coll = AR.view(QB, F32)
    rcoll = AR.view(QB, F32)
    rstbf = AR.view(8 * QB, BF16)
    xres = [AR.view(D, F32) for _ in range(2)]
    xfst = [AR.view(D, F32) for _ in range(2)]
    xfn = [AR.view(D, F32) for _ in range(2)]
    xfbf = [AR.view(D, BF16) for _ in range(2)]
    sqd = AR.view(D, BF16)

    SCA = (0, 1536)
    SCB = (1536, 3072)
    POA = (3072, 3584)
    POB = (3584, 4096)
    BCA = (0, 512)
    BCB = (512, 1024)
    PU = (1536, 2560)

    if "ATT" in phases:
        ldv = pr.dma("gpsimd", bu_bf, bu, "g_ld")
        for i in range(2):
            pr.ve("memset", Vq[i], 1.0, inc="v_misc")
        vq_ready = pr.val("v_misc")

        GROUPS = [3, 3, 3, 3, 2, 2]
        reg_mark = {0: 0, 1: 0}
        pair_end_tav = []
        qb_end_marks = []
        stt_marks = []
        sxf_marks = []
        xfb_marks = []
        sq_marks = []

        for qb in range(NQB):
            qtb = QTb[qb % 2]
            if qb >= 2:
                pr.wait("sync", "t_sc", qb_end_marks[qb - 2]["t_sc"])
            pr.dma("sync",
                   qtb.rearrange("p (j c) -> p j c", c=QB),
                   qT_s.rearrange("(j p) q -> p j q", p=P)[:, :, qb * QB:(qb + 1) * QB],
                   "a_q")
            qtb_ld = pr.val("a_q")

            for p in range(8):
                gi = len(pair_end_tav)
                kt, vq = KT[gi % 2], Vq[gi % 2]
                if gi >= 2:
                    pr.wait("sync", "t_avh", pair_end_tav[gi - 2])
                pr.wait("sync", "v_misc", vq_ready)
                pr.dma("sync", kt, kT_s[p * P:(p + 1) * P, :], "a_kv")
                vsrc = v_s[p].rearrange("(s q) c -> q s c", q=P)
                vdst = vq.rearrange("p (s c) -> p s c", c=130)
                pr.dma("sync", vdst[:, :, 0:64], vsrc[:, :, 0:64], "a_kv")
                pr.dma("sync", vdst[:, :, 65:129], vsrc[:, :, 64:128], "a_kv")
                kv_ld = pr.val("a_kv")

                def emit_attv(item, kt=kt, vq=vq):
                    st0, glen, sexv = item
                    pr.wait("tensor", "s_ex", sexv)
                    for k in range(glen):
                        st = st0 + k
                        slot = (st % 16) * QB
                        if st == 0:
                            pr.wait("tensor", "v_oe", pr.val("v_oe"))
                            pr.wait("tensor", "v_tr", pr.val("v_tr"))
                        fin = (st % 16 == 15) or (st == NST - 1)
                        pr.mm(psv(*POA, 0, 65), vq[:, st * 130: st * 130 + 65],
                              attA[:, slot:slot + QB], st == 0, st == NST - 1)
                        pr.mm(psv(*POB, 0, 65), vq[:, st * 130 + 65: st * 130 + 130],
                              attB[:, slot:slot + QB], st == 0, st == NST - 1,
                              inc="t_avh" if fin else None)

                pend = []
                for half in range(2):
                    for g, glen in enumerate(GROUPS):
                        st0 = half * 16 + sum(GROUPS[:g])
                        if g == 0 and half == 0:
                            pr.wait("tensor", "a_kv", kv_ld)
                            pr.wait("tensor", "a_q", qtb_ld)
                            pr.wait("tensor", "v_no", pr.val("v_no"))
                            pr.wait("tensor", "v_u", pr.val("v_u"))
                        pr.wait("tensor", "s_ex", reg_mark[0])
                        tscA = tscB = None
                        for k in range(glen):
                            st = st0 + k
                            last = (k == glen - 1)
                            if k == 0:
                                pass
                            pr.mm(psv(*SCA)[:, k * QB:(k + 1) * QB],
                                  kt[0:64, st * P:(st + 1) * P],
                                  qtb[0:64, p * QB:(p + 1) * QB],
                                  True, True, inc="t_sc" if last else None,
                                  tile_position=(0, 0))
                            if k == 0:
                                tscA = pr.val("t_sc") + (0 if last else glen - 1)
                                pr.wait("tensor", "s_ex", reg_mark[1])
                            pr.mm(psv(*SCB)[:, k * QB:(k + 1) * QB],
                                  kt[64:128, st * P:(st + 1) * P],
                                  qtb[64:128, p * QB:(p + 1) * QB],
                                  True, True, inc="t_sc" if last else None,
                                  tile_position=(64, 0))
                        tscA = pr.val("t_sc") - 1
                        tscB = pr.val("t_sc")
                        if g == 0:
                            pr.wait("scalar", "t_avh", pr.val("t_avh"))
                        sl0 = (st0 % 16) * QB
                        pr.wait("scalar", "t_sc", tscA)
                        pr.se("activation", attA[:, sl0:sl0 + glen * QB],
                              psv(*SCA)[:, 0:glen * QB], AF.Exp, inc="s_ex")
                        reg_mark[0] = pr.val("s_ex")
                        pr.wait("scalar", "t_sc", tscB)
                        pr.se("activation", attB[:, sl0:sl0 + glen * QB],
                              psv(*SCB)[:, 0:glen * QB], AF.Exp, inc="s_ex")
                        reg_mark[1] = pr.val("s_ex")
                        pend.append((st0, glen, pr.val("s_ex")))
                        if len(pend) >= 2:
                            emit_attv(pend.pop(0))
                    while pend:
                        emit_attv(pend.pop(0))
                pair_end_tav.append(pr.val("t_avh"))

                pr.wait("vector", "t_avh", pair_end_tav[-1])
                if gi >= 1:
                    pr.wait("vector", "g_rs", pr.val("g_rs"))
                    pr.wait("vector", "g_mb", pr.val("g_mb"))
                pr.ve("tensor_copy", O_sbU[0:64, p * QB:(p + 1) * QB],
                      psv(*POA, 0, 65)[0:64, :], inc="v_oe")
                pr.ve("tensor_copy", rs_st[64:65, 0:QB], psv(*POA, 64, 65), inc="v_oe")
                pr.ve("tensor_copy", tmpB[0:64, :], psv(*POB, 0, 65)[0:64, :], inc="v_oe")
                pr.ve("tensor_copy", rs_st[64:65, QB:2 * QB], psv(*POB, 64, 65), inc="v_oe")
                voe = pr.val("v_oe")
                pr.wait("gpsimd", "v_oe", voe)
                pr.dma("gpsimd", coll[2 * p:2 * p + 2, :],
                       rs_st[64:65, :].rearrange("p (a c) -> p a c", c=QB), "g_rs")
                pr.dma("gpsimd", O_sbU[64:128, p * QB:(p + 1) * QB], tmpB[0:64, :], "g_mb")

            pr.wait("vector", "g_rs", pr.val("g_rs"))
            pr.ve("reciprocal", rcoll[0:16, :], coll[0:16, :], inc="v_rc")
            for hf in range(2):
                pr.wait("gpsimd", "v_rc", pr.val("v_rc"))
                pr.wait("gpsimd", "t_bc", pr.val("t_bc"))
                pr.dma("gpsimd",
                       rstbf[0:1, :].rearrange("p (a c) -> p a c", c=QB),
                       rcoll[8 * hf:8 * hf + 8, :], "g_sc")
                gscv = pr.val("g_sc")
                for p4 in range(4):
                    p = hf * 4 + p4
                    pr.wait("tensor", "g_sc", gscv)
                    if p == 0:
                        pr.wait("tensor", "s_ex", pr.val("s_ex"))
                    pr.wait("tensor", "v_no", pr.val("v_no"))
                    r0 = 2 * p4 * QB
                    pr.mm(psv(*BCA, 0, 64), ones1[0:1, 0:64], rstbf[0:1, r0:r0 + QB],
                          True, True, inc="t_bc", tile_position=(0, 0))
                    pr.mm(psv(*BCB, 64, 128), ones1[0:1, 0:64],
                          rstbf[0:1, r0 + QB:r0 + 2 * QB],
                          True, True, inc="t_bc", tile_position=(0, 64))
                    pr.wait("vector", "t_bc", pr.val("t_bc"))
                    pr.wait("vector", "g_mb", pr.val("g_mb"))
                    blk_ = O_sbU[0:64, p * QB:(p + 1) * QB]
                    pr.ve("tensor_mul", blk_, blk_, psv(*BCA, 0, 64), inc="v_no")
                    blk2 = O_sbU[64:128, p * QB:(p + 1) * QB]
                    pr.ve("tensor_mul", blk2, blk2, psv(*BCB, 64, 128), inc="v_no")

            if "dbg_rs" in debug_outs:
                pr.wait("sync", "v_rc", pr.val("v_rc"))
                pr.dma("sync", dbg_rs[qb], coll[0:16, :], "s_dbg")
            if "dbg_o" in debug_outs:
                pr.wait("sync", "v_no", pr.val("v_no"))
                pr.dma("sync", dbg_o[qb], O_sbU, "s_dbg")

            vno = pr.val("v_no")
            for tt in range(4):
                xr = xres[tt % 2]
                if len(stt_marks) >= 2:
                    pr.wait("sync", "v_u", stt_marks[-2])
                pr.dma("sync", xr, xq[qb * QB + tt * P: qb * QB + (tt + 1) * P, :], "s_xr")
                sxr = pr.val("s_xr")
                if tt == 0:
                    pr.wait("tensor", "v_no", vno)
                    if qb == 0:
                        pr.wait("tensor", "g_ld", ldv)
                if stt_marks:
                    pr.wait("tensor", "v_u", stt_marks[-1])
                for nb in range(2):
                    c0 = PU[0] + nb * QB
                    for j in range(8):
                        pr.mm(psv(c0, c0 + QB), O_sbU[:, j * QB + tt * P:j * QB + (tt + 1) * P],
                              wuT[:, j * D + nb * QB:j * D + (nb + 1) * QB],
                              j == 0, False)
                    pr.mm(psv(c0, c0 + QB), ones1[0:1, 0:P], bu_bf[0:1, nb * QB:(nb + 1) * QB],
                          False, True, inc="t_u" if nb == 1 else None)
                s_ = tt % 2
                pr.wait("vector", "t_u", pr.val("t_u"))
                pr.wait("vector", "s_xr", sxr)
                if len(sq_marks) >= 2:
                    pr.wait("vector", "s_u", sq_marks[-2])
                vstt = pr.ve("scalar_tensor_tensor", xfst[s_], psv(PU[0], PU[0] + D), 1.0, xr,
                             OP.mult, OP.add, accum_out=st_tmp[:, s_:s_ + 1], inc="v_u")
                stt_marks.append(vstt)
                if "dbg_u" in debug_outs:
                    pr.wait("sync", "v_u", pr.val("v_u"))
                    pr.dma("sync", dbg_u[qb, tt], xfst[s_], "s_dbg")
                pr.wait("scalar", "v_u", pr.val("v_u"))
                pr.se("activation", sqd, xfst[s_], AF.Square,
                      accum_out=st_sq[:, s_:s_ + 1])
                pr.se("activation", st_negm[:, s_:s_ + 1], st_tmp[:, s_:s_ + 1],
                      AF.Identity, scale=-1.0 / D)
                pr.se("activation", st_tmp[:, 2 + s_:3 + s_], st_negm[:, s_:s_ + 1],
                      AF.Square, inc="s_u")
                pr.wait("vector", "s_u", pr.val("s_u"))
                pr.ve("tensor_copy", st_tmp[:, 6 + s_:7 + s_], st_sq[:, s_:s_ + 1])
                pr.ve("tensor_scalar", st_tmp[:, 4 + s_:5 + s_], st_tmp[:, 2 + s_:3 + s_],
                      -1.0, EPS, OP.mult, OP.add, inc="v_u")
                pr.wait("scalar", "v_u", pr.val("v_u"))
                sq_marks.append(pr.se("activation", st_std[:, s_:s_ + 1],
                                      st_tmp[:, 6 + s_:7 + s_],
                                      AF.Sqrt, bias=st_tmp[:, 4 + s_:5 + s_], scale=1.0 / D,
                                      inc="s_u"))
                pr.wait("vector", "s_u", pr.val("s_u"))
                pr.ve("reciprocal", st_rstd[:, s_:s_ + 1], st_std[:, s_:s_ + 1])
                pr.drain("vector")
                if len(sxf_marks) >= 2:
                    pr.wait("vector", "s_xf", sxf_marks[-2])
                if len(xfb_marks) >= 2:
                    pr.wait("vector", "t_tr", xfb_marks[-2])
                pr.ve("tensor_scalar", xfn[s_], xfst[s_], st_negm[:, s_:s_ + 1],
                      st_rstd[:, s_:s_ + 1], OP.add, OP.mult, inc="v_u")
                pr.ve("tensor_copy", xfbf[s_], xfn[s_], inc="v_u")
                vu = pr.val("v_u")
                pr.wait("sync", "v_u", vu)
                pr.dma("sync", xf_s[qb * QB + tt * P: qb * QB + (tt + 1) * P, :],
                       xfn[s_], "s_xf")
                if "dbg_st" in debug_outs:
                    pr.dma("sync", dbg_st[qb, tt, :, 0:8], st_tmp, "s_dbg")
                    pr.dma("sync", dbg_st[qb, tt, :, 8:12], st_sq, "s_dbg")
                    pr.dma("sync", dbg_st[qb, tt, :, 12:16], st_std, "s_dbg")
                    pr.dma("sync", dbg_st[qb, tt, :, 16:20], st_rstd, "s_dbg")
                    pr.dma("sync", dbg_st[qb, tt, :, 20:24], st_negm, "s_dbg")
                pr.wait("tensor", "v_u", vu)
                srcs = [xfbf[s_][:, j * P:(j + 1) * P] for j in range(8)]
                def evx(pv, tt=tt, qb=qb):
                    xview = xfT.rearrange("p (j c) -> p j c", c=TQ)[
                        :, :, qb * QB + tt * P: qb * QB + (tt + 1) * P]
                    pvv = pv.rearrange("p (j c) -> p j c", c=P)
                    pr.ve("tensor_copy", xview, pvv, inc="v_tr")
                transpose_block(srcs, evx)
                sxf_marks.append(pr.val("s_xf"))
                xfb_marks.append(pr.val("t_tr"))
            qb_end_marks.append({"t_sc": pr.val("t_sc")})

    att_end = {"s_xf": pr.val("s_xf") if "ATT" in phases else 0}
    AR.pop()

    AR.push()
    wbuf = [AR.view(8 * D, BF16) for _ in range(3)]
    g1t = AR.view(D, F32)
    be1t = AR.view(D, F32)
    g2t = AR.view(D, F32)
    hbf = [AR.view(D, BF16) for _ in range(2)]
    hTb = [AR.view(8 * P, BF16) for _ in range(2)]
    acc = AR.view(8 * D, F32)
    xfrd = [AR.view(D, F32)]
    t1b = [AR.view(D, F32) for _ in range(2)]
    u1b = [AR.view(D, F32)]
    outst = [AR.view(D, F32) for _ in range(2)]
    sqd2 = AR.view(D, BF16)
    b1r = [AR.view(D, BF16) for _ in range(2)]
    b2r = [AR.view(D, BF16) for _ in range(2)]
    be2sb = AR.view(D, BF16)
    gsc = AR.view(32, F32)
    gts = AR.view(6 * 8, F32)
    gtmp = AR.view(32, F32)
    st_sq2 = AR.view(8, F32)
    dbgz_sb = outst[0]
    st_s2 = AR.view(8, F32)

    PZ = [(0, 1024), (1024, 2048)]
    PO = (2048, 3072)
    SUMC0 = 3584
    SUMC1 = 3592
    GATC = 3600

    if "FFN" in phases:
        att_done = {n: pr.val(n) for n in ("t_u", "t_tr", "v_u", "s_xf", "v_tr")}
        for q_ in ("gpsimd", "sync"):
            pr.wait(q_, "t_tr", att_done["t_tr"])
            pr.wait(q_, "t_u", att_done["t_u"])
            pr.wait(q_, "s_xf", att_done["s_xf"])

        def load_expert_w(e, which, buf):
            srct = w1 if which == 0 else w2
            return pr.dma("gpsimd", wbuf[buf].rearrange("p (a d) -> p a d", d=D),
                          srct[e].rearrange("(a p) d -> p a d", p=P), "f_w")

        pr.dma("gpsimd", be2sb[0:4, :], be2, "f_w")
        ldw = {}
        z1_end = {}
        z2_end = {}
        ldw[0] = [load_expert_w(0, 0, 0), load_expert_w(0, 1, 1)]

        def load_gb(e, slot):
            m = pr.dma("gpsimd", g1t, g1[e:e + 1, :].partition_broadcast(P), "f_g")
            m = pr.dma("gpsimd", be1t, be1[e:e + 1, :].partition_broadcast(P), "f_g")
            m = pr.dma("gpsimd", g2t, g2[e:e + 1, :].partition_broadcast(P), "f_g")
            return m

        def prologue(ge_p, ch_p):
            e_p = ge_p % NE
            slot = ge_p % 2
            ch0_p = (ge_p // NE) * 8 * P
            wa_p = (2 * ge_p) % 3
            if ge_p - 2 in z2_end:
                pr.wait("gpsimd", "t_z", z2_end[ge_p - 2])
            pr.dma("gpsimd", b1r[slot][0:1, :], b1[e_p:e_p + 1, :], "f_g")
            ldb = pr.dma("gpsimd", b2r[slot][0:1, :], b2[e_p:e_p + 1, :], "f_g")
            pr.wait("scalar", "f_g", ldb)
            pr.se("activation", sqd2[0:1, :], b1r[slot][0:1, :], AF.Identity,
                  accum_out=bsum_f[0:1, e_p:e_p + 1], inc="s_f")
            pr.wait("scalar", "f_w", ldw[ge_p][0])
            for j in range(8):
                pr.se("activation", sqd2, wbuf[wa_p][:, j * D:(j + 1) * D], AF.Identity,
                      accum_out=wsum_f[:, j:j + 1], inc="s_f" if j == 7 else None)
            pr.wait("vector", "s_f", pr.val("s_f"))
            pr.wait("vector", "t_z", pr.val("t_z"))
            pr.ve("tensor_copy", bsum_bf[0:1, e_p:e_p + 1], bsum_f[0:1, e_p:e_p + 1])
            pr.ve("tensor_copy", wsum_bf[:, 0:8], wsum_f[:, 0:8], inc="v_f")
            wsum_ready = pr.val("v_f")
            SUMC = SUMC0 if ge_p % 2 == 0 else SUMC1
            pr.wait("tensor", "s_f", sum_read_mark.get(ge_p % 2, 0))
            pr.wait("tensor", "v_f", wsum_ready)
            for tt in range(8):
                for c in range(8):
                    pr.mm(psv(SUMC + tt, SUMC + tt + 1),
                          xfT[:, c * TQ + ch0_p + tt * P: c * TQ + ch0_p + (tt + 1) * P],
                          wsum_bf[:, c:c + 1], c == 0, False)
                pr.mm(psv(SUMC + tt, SUMC + tt + 1), ones1[0:1, 0:P],
                      bsum_bf[0:1, e_p:e_p + 1], False, True,
                      inc="t_z" if tt == 7 else None)

        stt1_marks = {0: [], 1: []}
        z1sq_mark = {}
        prologue_done = set()
        sum_read_mark = {}
        z2sq_mark = [0]
        fstats_mark = [0]
        o_free_mark = [0]

        def ln_chain(zsrc, k, sumsrc, g_t, be_t, out_apply):
            pr.se("activation", sqd2, zsrc, AF.Square, accum_out=st_sq[:, k:k + 1])
            pr.se("activation", st_negm[:, k:k + 1], sumsrc, AF.Identity, scale=-1.0 / D)
            pr.se("activation", st_tmp[:, k:k + 1], st_negm[:, k:k + 1], AF.Square,
                  inc="s_f")
            m0 = pr.val("s_f")
            pr.wait("vector", "s_f", m0)
            pr.ve("tensor_copy", st_sq2[:, k:k + 1], st_sq[:, k:k + 1])
            pr.ve("tensor_scalar", st_tmp[:, 4 + k:5 + k], st_tmp[:, k:k + 1],
                  -1.0, EPS, OP.mult, OP.add, inc="v_f")
            pr.wait("scalar", "v_f", pr.val("v_f"))
            pr.se("activation", st_std[:, k:k + 1], st_sq2[:, k:k + 1], AF.Sqrt,
                  bias=st_tmp[:, 4 + k:5 + k], scale=1.0 / D, inc="s_f")
            fstats_mark[0] = pr.val("s_f")
            pr.wait("vector", "s_f", pr.val("s_f"))
            pr.ve("reciprocal", st_rstd[:, k:k + 1], st_std[:, k:k + 1])
            pr.drain("vector")
            out_apply(st_negm[:, k:k + 1], st_rstd[:, k:k + 1])
            return m0

        pr.wait("tensor", "v_u", pr.val("v_u"))
        pr.wait("tensor", "v_no", pr.val("v_no"))
        pr.wait("tensor", "v_oe", pr.val("v_oe"))
        pr.wait("tensor", "s_ex", pr.val("s_ex"))

        for ch in range(2):
            ch0 = ch * 8 * P

            pr.wait("tensor", "v_tr", pr.val("v_tr"))
            pr.wait("tensor", "g_ld", pr.val("g_ld"))
            pr.wait("tensor", "v_g", pr.val("v_g"))
            pr.wait("tensor", "s_f", max(sum_read_mark.values()) if sum_read_mark else 0)
            for tt in range(8):
                for c in range(8):
                    pr.mm(psv(GATC + 4 * tt, GATC + 4 * tt + 4),
                          xfT[:, c * TQ + ch0 + tt * P: c * TQ + ch0 + (tt + 1) * P],
                          gw_bf[:, c * NE:(c + 1) * NE], c == 0, False)
                pr.mm(psv(GATC + 4 * tt, GATC + 4 * tt + 4), ones1[0:1, 0:P],
                      gb_bf[0:1, :], False, True, inc="t_g" if tt == 7 else None)
            pr.wait("vector", "t_g", pr.val("t_g"))
            pr.ve("tensor_copy", gsc, psv(GATC, GATC + 32))
            pr.drain("vector")
            g3 = gsc.rearrange("p (a e) -> p a e", e=NE)
            ga, gb_, gc, gd = (gts[:, 0:8], gts[:, 8:16], gts[:, 16:24], gts[:, 24:32])
            gm1v, gm2v = gts[:, 32:40], gts[:, 40:48]
            pr.ve("tensor_max", ga, g3[:, :, 0], g3[:, :, 1])
            pr.ve("tensor_tensor", gb_, g3[:, :, 0], g3[:, :, 1], OP.min)
            pr.ve("tensor_max", gc, g3[:, :, 2], g3[:, :, 3])
            pr.ve("tensor_tensor", gd, g3[:, :, 2], g3[:, :, 3], OP.min)
            pr.drain("vector")
            pr.ve("tensor_max", gm1v, ga, gc)
            pr.ve("tensor_tensor", ga, ga, gc, OP.min)
            pr.ve("tensor_max", gb_, gb_, gd)
            pr.drain("vector")
            pr.ve("tensor_max", gm2v, ga, gb_)
            pr.drain("vector")
            m1b = gm1v.unsqueeze(2).broadcast_to([P, 8, NE])
            m2b = gm2v.unsqueeze(2).broadcast_to([P, 8, NE])
            gt3 = gtmp.rearrange("p (a e) -> p a e", e=NE)
            pr.ve("tensor_sub", gt3, g3, m1b)
            pr.ve("tensor_sub", gd, gm2v, gm1v, inc="v_g")
            pr.drain("vector")
            pr.wait("scalar", "v_g", pr.val("v_g"))
            pr.se("activation", wgate, gtmp, AF.Exp)
            pr.se("activation", gts[:, 0:8], gd, AF.Exp, inc="s_g")
            pr.wait("vector", "s_g", pr.val("s_g"))
            pr.ve("tensor_scalar_add", gts[:, 16:24], gts[:, 0:8], 1.0)
            pr.drain("vector")
            pr.ve("reciprocal", gts[:, 8:16], gts[:, 16:24])
            pr.drain("vector")
            pr.ve("tensor_tensor", gt3, g3, m2b, OP.is_ge)
            pr.drain("vector")
            w3 = wgate.rearrange("p (a e) -> p a e", e=NE)
            pr.ve("tensor_mul", w3, w3, gt3)
            pr.drain("vector")
            rdb = gts[:, 8:16].unsqueeze(2).broadcast_to([P, 8, NE])
            pr.ve("tensor_mul", w3, w3, rdb, inc="v_g")
            pr.drain("vector")
            pr.ve("tensor_copy", wgate_bf, wgate, inc="v_g")
            if "dbg_w" in debug_outs:
                pr.wait("sync", "v_g", pr.val("v_g"))
                pr.dma("sync", dbg_w[ch], wgate, "s_dbg")
            for tt in range(8):
                pr.wait("tensor", "v_g", pr.val("v_g"))
                pr.wait("tensor", "v_tr", pr.val("v_tr"))
                pr.tr(psbf(3072, 3584)[0:4, tt * P:(tt + 1) * P],
                      wgate_bf[:, tt * NE:(tt + 1) * NE], ident,
                      inc="t_tr" if tt == 7 else None)
            pr.wait("vector", "t_tr", pr.val("t_tr"))
            for tt in range(8):
                pr.ve("tensor_copy", wTg[tt][0:4, :],
                      psbf(3072, 3584)[0:4, tt * P:(tt + 1) * P],
                      inc="v_tr" if tt == 7 else None)
            pr.wait("tensor", "f_w", 16)
            for tt in range(8):
                pr.wait("tensor", "v_tr", pr.val("v_tr"))
                if tt > 0:
                    pr.wait("tensor", "v_f", pr.val("v_f"))
                for nb in range(2):
                    pr.mm(psv(PO[0] + nb * QB, PO[0] + (nb + 1) * QB),
                          wTg[tt][0:4, :], be2sb[0:4, nb * QB:(nb + 1) * QB],
                          True, True, inc="t_z" if nb == 1 else None)
                pr.wait("vector", "t_z", pr.val("t_z"))
                pr.ve("tensor_copy", acc[:, tt * D:(tt + 1) * D], psv(*PO), inc="v_f")
            o_free_mark[0] = pr.val("v_f")
            if "dbg_acc" in debug_outs and ch == 0:
                pr.wait("sync", "v_f", pr.val("v_f"))
                pr.dma("sync", dbg_acc[0], acc, "s_dbg")

            for e in range(NE):
                ge = ch * NE + e
                wa = (2 * ge) % 3
                wb = (2 * ge + 1) % 3
                gslot = ge % 2
                if ge >= 1:
                    pr.wait("gpsimd", "v_f", pr.val("v_f"))
                ldg = load_gb(e, gslot)
                if ge not in prologue_done:
                    prologue(ge, ch)
                    prologue_done.add(ge)
                gn = ge + 1
                if gn < 8 and gn not in ldw:
                    if ge - 1 in z2_end:
                        pr.wait("gpsimd", "t_z", z2_end[ge - 1])
                    ldw[gn] = [load_expert_w(gn % NE, 0, (2 * gn) % 3), None]
                SUMC = SUMC0 if ge % 2 == 0 else SUMC1
                pr.wait("tensor", "f_w", ldw[ge][0])
                pr.wait("tensor", "f_g", ldg)
                if "dbg_wb" in debug_outs and ch == 0 and e == 0:
                    pr.wait("sync", "f_w", ldw[ge][0])
                    pr.dma("sync", dbg_wb, wbuf[wa], "s_dbg")

                ht_tr_mark = {0: 0, 1: 0}
                ht_used_mark = {0: 0, 1: 0}
                h_ready = {}

                def z_side(tt, wa=wa, gslot=gslot, ge=ge):
                    zb = PZ[tt % 2]
                    k1 = tt % 2
                    if len(stt1_marks[tt % 2]) >= 1:
                        pr.wait("tensor", "v_f", stt1_marks[tt % 2][-1])
                    pr.wait("tensor", "s_f", z1sq_mark.get(tt % 2, 0))
                    for nb in range(2):
                        c0 = zb[0] + nb * QB
                        for c in range(8):
                            pr.mm(psv(c0, c0 + QB),
                                  xfT[:, c * TQ + ch0 + tt * P: c * TQ + ch0 + (tt + 1) * P],
                                  wbuf[wa][:, c * D + nb * QB:c * D + (nb + 1) * QB],
                                  c == 0, False)
                        pr.mm(psv(c0, c0 + QB), ones1[0:1, 0:P],
                              b1r[gslot][0:1, nb * QB:(nb + 1) * QB], False, True,
                              inc="t_z" if nb == 1 else None)
                    if tt == 7:
                        z1_end[ge] = pr.val("t_z")
                        gn = ge + 1
                        if gn < 8:
                            pr.wait("gpsimd", "t_z", z1_end[ge])
                            ldw[gn][1] = load_expert_w(gn % NE, 1, (2 * gn + 1) % 3)
                    pr.wait("scalar", "t_z", pr.val("t_z"))
                    if len(stt1_marks[tt % 2]) >= 1:
                        pr.wait("scalar", "v_f", stt1_marks[tt % 2][-1])

                    def apply1(negm, rstd, tt=tt, zb=zb):
                        t1 = t1b[tt % 2]
                        pr.ve("scalar_tensor_tensor", t1, psv(*zb), negm, g1t,
                              OP.add, OP.mult, inc="v_f")
                        stt1_marks[tt % 2].append(pr.val("v_f"))
                        pr.ve("scalar_tensor_tensor", t1, t1, rstd, be1t,
                              OP.mult, OP.add)
                        pr.wait("vector", "t_tr", ht_tr_mark[tt % 2])
                        pr.ve("tensor_scalar_max", hbf[tt % 2], t1, 0.0, inc="v_f")

                    z1sq_mark[tt % 2] = ln_chain(psv(*zb), k1,
                                                 psv(SUMC + tt, SUMC + tt + 1),
                                                 g1t, be1t, apply1)
                    sum_read_mark[ge % 2] = z1sq_mark[tt % 2]
                    h_ready[tt] = pr.val("v_f")

                def o_side(tt, wb=wb, gslot=gslot, ge=ge, e=e):
                    k2 = 2 + tt % 2
                    pr.wait("tensor", "v_f", h_ready[tt])
                    pr.wait("tensor", "v_tr", pr.val("v_tr"))
                    for j in range(8):
                        pr.tr(psbf(3072, 3584)[:, j * P:(j + 1) * P],
                              hbf[tt % 2][:, j * P:(j + 1) * P], ident,
                              inc="t_tr" if j == 7 else None)
                    ht_tr_mark[tt % 2] = pr.val("t_tr")
                    pr.wait("vector", "t_tr", pr.val("t_tr"))
                    pr.wait("vector", "t_z", ht_used_mark[tt % 2])
                    pr.ve("tensor_copy", hTb[tt % 2], psbf(3072, 3584), inc="v_tr")
                    ht_ready = pr.val("v_tr")

                    pr.wait("tensor", "v_tr", ht_ready)
                    pr.wait("tensor", "v_f", o_free_mark[0])
                    pr.wait("tensor", "s_f", z2sq_mark[0])
                    pr.wait("tensor", "f_w", ldw[ge][1])
                    for nb in range(2):
                        c0 = PO[0] + nb * QB
                        for j in range(8):
                            pr.mm(psv(c0, c0 + QB),
                                  hTb[tt % 2][:, j * P:(j + 1) * P],
                                  wbuf[wb][:, j * D + nb * QB:j * D + (nb + 1) * QB],
                                  j == 0, False)
                        pr.mm(psv(c0, c0 + QB), ones1[0:1, 0:P],
                              b2r[gslot][0:1, nb * QB:(nb + 1) * QB], False, True,
                              inc="t_z" if nb == 1 else None)
                    ht_used_mark[tt % 2] = pr.val("t_z")
                    pr.wait("scalar", "t_z", pr.val("t_z"))
                    pr.se("activation", sqd2, psv(*PO), AF.Identity,
                          accum_out=st_sq[:, k2:k2 + 1], inc="s_f")
                    pr.wait("vector", "s_f", pr.val("s_f"))
                    pr.ve("tensor_copy", st_s2[:, k2:k2 + 1], st_sq[:, k2:k2 + 1],
                          inc="v_f")
                    pr.wait("scalar", "v_f", pr.val("v_f"))

                    def apply2(negm, rstd, tt=tt, e=e):
                        u1 = u1b[0]
                        pr.ve("scalar_tensor_tensor", u1, psv(*PO), negm, g2t,
                              OP.add, OP.mult, inc="v_f")
                        o_free_mark[0] = pr.val("v_f")
                        pr.ve("tensor_scalar_mul", st_rstd[:, 3:4], rstd,
                              wgate[:, tt * NE + e: tt * NE + e + 1])
                        pr.drain("vector")
                        ab_ = acc[:, tt * D:(tt + 1) * D]
                        pr.ve("scalar_tensor_tensor", ab_, u1, st_rstd[:, 3:4], ab_,
                              OP.mult, OP.add, inc="v_f")

                    z2sq_mark[0] = ln_chain(psv(*PO), k2, st_s2[:, k2:k2 + 1], g2t,
                                            None, apply2)
                    if tt == 7:
                        z2_end[ge] = pr.val("t_z")
                        if "dbg_acc" in debug_outs and ch == 0:
                            pr.wait("sync", "v_f", pr.val("v_f"))
                            pr.dma("sync", dbg_acc[1 + e], acc, "s_dbg")

                for step in range(9):
                    if step < 8:
                        z_side(step)
                    if step == 4 and ge + 1 < 2 * NE and (ge + 1) not in prologue_done:
                        prologue(ge + 1, ch)
                        prologue_done.add(ge + 1)
                    if step >= 1:
                        o_side(step - 1)

            fin_marks = []
            for tt in range(8):
                xr2 = xfrd[0]
                if len(fin_marks) >= 1:
                    pr.wait("sync", "v_f", fin_marks[-1])
                pr.dma("sync", xr2, xf_s[ch0 + tt * P: ch0 + (tt + 1) * P, :], "s_x2")
                sx2 = pr.val("s_x2")
                pr.wait("vector", "s_x2", sx2)
                pr.wait("vector", "s_f", fstats_mark[0])
                k = tt % 2
                fm = pr.ve("scalar_tensor_tensor", t1b[k], xr2, 2.0, acc[:, tt * D:(tt + 1) * D],
                           OP.mult, OP.add, accum_out=st_tmp[:, k:k + 1], inc="v_f")
                fin_marks.append(fm)
                pr.wait("scalar", "v_f", pr.val("v_f"))

                def applyf(negm, rstd, tt=tt, k=k):
                    pr.wait("vector", "s_out", max(0, pr.val("s_out") - 16))
                    pr.ve("tensor_scalar", outst[k], t1b[k], negm, rstd,
                          OP.add, OP.mult, inc="v_f")
                    vo = pr.val("v_f")
                    pr.wait("sync", "v_f", vo)
                    pr.dma("sync", out[ch0 + tt * P: ch0 + (tt + 1) * P, :], outst[k],
                           "s_out")

                ln_chain(t1b[k], k, st_tmp[:, k:k + 1], None, None, applyf)

        pr.wait("sync", "s_out", pr.val("s_out"))

    AR.pop()

    if "FFN" not in phases:
        AR.push()
        ostub = AR.view(D, F32)
        pr.wait("vector", "s_st", phaseA_store)
        pr.ve("memset", ostub, 0.0, inc="v_misc")
        pr.wait("sync", "v_misc", pr.val("v_misc"))
        for i in range(TQ // P):
            pr.dma("sync", out[i * P:(i + 1) * P, :], ostub, "s_out")
        pr.wait("sync", "s_out", pr.val("s_out"))
        AR.pop()

    pr.run()
    return nc


def kernel(L0_attn_x, L0_attn_Wq, L0_attn_Wk, L0_attn_Wv, L0_attn_Wu, L0_attn_bu,
           L0_ffn_gate_W, L0_ffn_gate_b, L0_ffn_W1, L0_ffn_b1, L0_ffn_g1, L0_ffn_be1,
           L0_ffn_W2, L0_ffn_b2, L0_ffn_g2, L0_ffn_be2):
    x = np.ascontiguousarray(np.asarray(L0_attn_x, dtype=np.float32))
    B = x.shape[0]
    nc = build()
    res = run_bass_kernel_spmd(nc, make_in_maps(x, locals()), core_ids=list(range(8)))
    outp = np.empty((B, T, D), dtype=np.float32)
    for c in range(8):
        b, qh = c // 2, c % 2
        outp[b, qh * TQ:(qh + 1) * TQ] = res.results[c]["out"]
    return outp


def make_in_maps(x, env):
    eye = np.eye(P, dtype=np.float32)
    f = np.float32
    maps = []
    for c in range(8):
        b, qh = c // 2, c % 2
        maps.append({
            "xkv": x[b],
            "xq": np.ascontiguousarray(x[b, qh * TQ:(qh + 1) * TQ]),
            "wq": np.asarray(env["L0_attn_Wq"], f),
            "wk": np.asarray(env["L0_attn_Wk"], f),
            "wv": np.asarray(env["L0_attn_Wv"], f),
            "wu": np.asarray(env["L0_attn_Wu"], f),
            "bu": np.asarray(env["L0_attn_bu"], f).reshape(1, D),
            "gw": np.asarray(env["L0_ffn_gate_W"], f),
            "gb": np.asarray(env["L0_ffn_gate_b"], f).reshape(1, NE),
            "w1": np.asarray(env["L0_ffn_W1"], f),
            "b1": np.asarray(env["L0_ffn_b1"], f),
            "g1": np.asarray(env["L0_ffn_g1"], f),
            "be1": np.asarray(env["L0_ffn_be1"], f),
            "w2": np.asarray(env["L0_ffn_W2"], f),
            "b2": np.asarray(env["L0_ffn_b2"], f),
            "g2": np.asarray(env["L0_ffn_g2"], f),
            "be2": np.asarray(env["L0_ffn_be2"], f),
            "ident_in": eye,
        })
    return maps


# version 17
# speedup vs baseline: 1.3247x; 1.0793x over previous
import numpy as np
import concourse.bass as bass
import concourse.mybir as mybir
from concourse.bass_utils import run_bass_kernel_spmd

F32 = mybir.dt.float32
BF16 = mybir.dt.bfloat16
AF = mybir.ActivationFunctionType
OP = mybir.AluOpType
AX = mybir.AxisListType.X

P = 128
D = 1024
T = 4096
TQ = 2048
H = 16
NE = 4
QB = 512
NQB = TQ // QB
NST = T // P
EPS = 1e-5
QSCALE = 1.0 / 32.0

DEBUG_OUTS = ()


class Prog:
    ENGINES = ("sync", "gpsimd", "vector", "scalar", "tensor")

    def __init__(self, nc):
        self.nc = nc
        self.ops = {e: [] for e in self.ENGINES}
        self.sems = {}
        self.cnt = {}

    def sem(self, name):
        if name not in self.sems:
            self.sems[name] = self.nc.alloc_semaphore(name)
            self.cnt[name] = 0
        return self.sems[name]

    def inc(self, name, k=1):
        h = self.sem(name)
        self.cnt[name] += k
        return h, self.cnt[name]

    def val(self, name):
        self.sem(name)
        return self.cnt[name]

    def emit(self, engine, fn):
        self.ops[engine].append(fn)

    def wait(self, engine, name, value):
        if value <= 0:
            return
        h = self.sem(name)
        self.emit(engine, lambda eng, h=h, v=value: eng.wait_ge(h, v))

    def mm(self, out, lhsT, rhs, start, stop, inc=None, tile_position=None):
        if inc is None:
            self.emit("tensor", lambda t, o=out, l=lhsT, r=rhs, s0=start, s1=stop, tp=tile_position:
                      t.matmul(o, l, r, start=s0, stop=s1, tile_position=tp))
            return None
        h, v = self.inc(inc)
        self.emit("tensor", lambda t, o=out, l=lhsT, r=rhs, s0=start, s1=stop, h=h, tp=tile_position:
                  t.matmul(o, l, r, start=s0, stop=s1, tile_position=tp).then_inc(h, 1))
        return v

    def tr(self, out, in_, ident, inc=None):
        if inc is None:
            self.emit("tensor", lambda t, o=out, i=in_, d=ident: t.transpose(o, i, d))
            return None
        h, v = self.inc(inc)
        self.emit("tensor", lambda t, o=out, i=in_, d=ident, h=h: t.transpose(o, i, d).then_inc(h, 1))
        return v

    def ve(self, fname, *args, inc=None, **kwargs):
        if inc is None:
            self.emit("vector", lambda v, f=fname, a=args, k=kwargs: getattr(v, f)(*a, **k))
            return None
        h, val = self.inc(inc)
        self.emit("vector", lambda v, f=fname, a=args, k=kwargs, h=h:
                  getattr(v, f)(*a, **k).then_inc(h, 1))
        return val

    def se(self, fname, *args, inc=None, **kwargs):
        if inc is None:
            self.emit("scalar", lambda s, f=fname, a=args, k=kwargs: getattr(s, f)(*a, **k))
            return None
        h, val = self.inc(inc)
        self.emit("scalar", lambda s, f=fname, a=args, k=kwargs, h=h:
                  getattr(s, f)(*a, **k).then_inc(h, 1))
        return val

    def drain(self, engine):
        self.emit(engine, lambda eng: eng.drain())

    def dma(self, q, dst, src, sem):
        h, v = self.inc(sem, 16)
        self.emit(q, lambda eng, d=dst, s=src, h=h: eng.dma_start(d, s).then_inc(h, 16))
        return v

    def run(self):
        with self.nc.Block() as blk:
            for e in self.ENGINES:
                getattr(blk, e)(self._runner(e))

    def _runner(self, e):
        def f(eng):
            for op in self.ops[e]:
                op(eng)
        return f


def ceil32(x):
    return (x + 31) & ~31


class Arena:
    def __init__(self, nc, name, nbytes):
        self.nbytes = nbytes
        self.t = nc.alloc_sbuf_tensor(name, [P, nbytes // 2], BF16)
        self.ap = self.t.ap()
        self.off = 0
        self.marks = []
        self.hwm = 0

    def view(self, nelem, dtype, shape=None):
        esz = 2 if dtype == BF16 else 4
        off = ceil32(self.off)
        self.off = off + nelem * esz
        self.hwm = max(self.hwm, self.off)
        assert self.off <= self.nbytes, f"arena overflow {self.off} > {self.nbytes}"
        v = self.ap[:, off // 2: (off + nelem * esz) // 2]
        if dtype != BF16:
            v = v.bitcast(dtype)
        if shape is not None:
            assert int(np.prod(shape)) == nelem
            if len(shape) == 2:
                v = v.rearrange("p (a b) -> p a b", b=shape[1])
        return v

    def push(self):
        self.marks.append(self.off)

    def pop(self):
        self.off = self.marks.pop()


def build(debug_outs=DEBUG_OUTS, phases=("A", "ATT", "FFN")):
    nc = bass.Bass("TRN2", target_bir_lowering=False)
    pr = Prog(nc)

    xkv = nc.dram_tensor("xkv", [T, D], F32, kind="ExternalInput").ap()
    xq = nc.dram_tensor("xq", [TQ, D], F32, kind="ExternalInput").ap()
    wmat = {n: nc.dram_tensor(n, [D, D], F32, kind="ExternalInput").ap()
            for n in ("wq", "wk", "wv", "wu")}
    bu = nc.dram_tensor("bu", [1, D], F32, kind="ExternalInput").ap()
    gw = nc.dram_tensor("gw", [D, NE], F32, kind="ExternalInput").ap()
    gb = nc.dram_tensor("gb", [1, NE], F32, kind="ExternalInput").ap()
    w1 = nc.dram_tensor("w1", [NE, D, D], F32, kind="ExternalInput").ap()
    b1 = nc.dram_tensor("b1", [NE, D], F32, kind="ExternalInput").ap()
    g1 = nc.dram_tensor("g1", [NE, D], F32, kind="ExternalInput").ap()
    be1 = nc.dram_tensor("be1", [NE, D], F32, kind="ExternalInput").ap()
    w2 = nc.dram_tensor("w2", [NE, D, D], F32, kind="ExternalInput").ap()
    b2 = nc.dram_tensor("b2", [NE, D], F32, kind="ExternalInput").ap()
    g2 = nc.dram_tensor("g2", [NE, D], F32, kind="ExternalInput").ap()
    be2 = nc.dram_tensor("be2", [NE, D], F32, kind="ExternalInput").ap()
    ident_in = nc.dram_tensor("ident_in", [P, P], F32, kind="ExternalInput").ap()

    out = nc.dram_tensor("out", [TQ, D], F32, kind="ExternalOutput").ap()

    def scratch(name, shape, dtype):
        kind = "ExternalOutput" if name in debug_outs else "Internal"
        return nc.dram_tensor(name, shape, dtype, kind=kind).ap()

    kT_s = scratch("kT_s", [D, T], BF16)
    qT_s = scratch("qT_s", [D, TQ], BF16)
    v_s = scratch("v_s", [H // 2, T, P], BF16)
    xf_s = scratch("xf_s", [TQ, D], F32)
    dbg_rs = scratch("dbg_rs", [NQB, H, QB], F32)
    dbg_o = scratch("dbg_o", [NQB, P, 8 * QB], BF16)
    dbg_u = scratch("dbg_u", [NQB, 4, P, D], F32)
    dbg_st = scratch("dbg_st", [NQB, 4, P, 24], F32)
    dbg_w = scratch("dbg_w", [2, P, 32], F32)
    dbg_h = scratch("dbg_h", [2, P, D], BF16)
    dbg_acc = scratch("dbg_acc", [5, P, 8 * D], F32)
    dbg_z = scratch("dbg_z", [2, P, D], F32)
    dbg_wb = scratch("dbg_wb", [P, 8 * D], BF16)

    ident = nc.alloc_sbuf_tensor("ident", [P, P], BF16).ap()
    ones1 = nc.alloc_sbuf_tensor("ones1", [1, P], BF16).ap()
    wuT = nc.alloc_sbuf_tensor("wuT", [P, 8 * D], BF16).ap()
    bu_bf = nc.alloc_sbuf_tensor("bu_bf", [1, D], BF16).ap()
    xfT = nc.alloc_sbuf_tensor("xfT", [P, 8 * TQ], BF16).ap()
    gw_bf = nc.alloc_sbuf_tensor("gw_bf", [P, 8 * NE], BF16).ap()
    gb_bf = nc.alloc_sbuf_tensor("gb_bf", [1, NE], BF16).ap()
    wsum_f = nc.alloc_sbuf_tensor("wsum_f", [P, 16], F32).ap()
    wsum_bf = nc.alloc_sbuf_tensor("wsum_bf", [P, 16], BF16).ap()
    bsum_bf = nc.alloc_sbuf_tensor("bsum_bf", [1, 16], BF16).ap()
    bsum_f = nc.alloc_sbuf_tensor("bsum_f", [1, 16], F32).ap()
    st_negm = nc.alloc_sbuf_tensor("st_negm", [P, 12], F32).ap()
    st_tmp = nc.alloc_sbuf_tensor("st_tmp", [P, 28], F32).ap()
    st_sq = nc.alloc_sbuf_tensor("st_sq", [P, 12], F32).ap()
    st_std = nc.alloc_sbuf_tensor("st_std", [P, 12], F32).ap()
    st_rstd = nc.alloc_sbuf_tensor("st_rstd", [P, 14], F32).ap()
    wgate = nc.alloc_sbuf_tensor("wgate", [P, 32], F32).ap()
    wgate_bf = nc.alloc_sbuf_tensor("wgate_bf", [P, 32], BF16).ap()
    wTg = [nc.alloc_sbuf_tensor(f"wTg{i}", [NE, P], BF16).ap() for i in range(8)]
    gm1 = nc.alloc_sbuf_tensor("gm1", [P, 8], F32).ap()
    gm2 = nc.alloc_sbuf_tensor("gm2", [P, 8], F32).ap()
    gden = nc.alloc_sbuf_tensor("gden", [P, 8], F32).ap()
    gtmp = nc.alloc_sbuf_tensor("gtmp", [P, 32], F32).ap()

    psum = nc.alloc_psum_tensor("psum", [P, 4096], F32).ap()

    def psv(c0, c1, r0=0, r1=P):
        return psum[r0:r1, c0:c1]

    def psbf(c0, c1):
        return psum[:, c0:c1].bitcast(BF16)

    AR = Arena(nc, "arena", 140 * 1024)

    pr.dma("gpsimd", ident, ident_in, "g_ld")
    pr.ve("memset", ones1, 1.0, inc="v_misc")

    AR.push()
    wnat2 = [AR.view(8 * D, BF16) for _ in range(2)]
    wTq = AR.view(8 * D, BF16)
    wTk = AR.view(8 * D, BF16)
    wTv = AR.view(8 * D, BF16)
    xnat = [AR.view(4 * D, BF16) for _ in range(2)]
    xT = [AR.view(8 * QB, BF16) for _ in range(2)]
    stg = [AR.view(QB, BF16) for _ in range(4)]

    def transpose_block(srcs, dst_fn):
        pr.wait("tensor", "v_tr", pr.val("v_tr"))
        for i, s in enumerate(srcs):
            pr.tr(psbf(3072, 3584)[:, i * P:(i + 1) * P], s, ident,
                  inc="t_tr" if i == 7 else None)
        pr.wait("vector", "t_tr", pr.val("t_tr"))
        dst_fn(psbf(3072, 3584))

    wprep_marks = [0, 0]
    for wi, (wname, dst) in enumerate((("wq", wTq), ("wk", wTk), ("wv", wTv), ("wu", wuT))):
        wnat = wnat2[wi % 2]
        pr.wait("gpsimd", "t_tr", wprep_marks[wi % 2])
        ldv = pr.dma("gpsimd", wnat.rearrange("p (a d) -> p a d", d=D),
                     wmat[wname].rearrange("(a p) d -> p a d", p=P), "g_ld")
        pr.wait("tensor", "g_ld", ldv)
        for j in range(8):
            srcs = [wnat[:, i * D + j * P: i * D + (j + 1) * P] for i in range(8)]
            def ev(pv, j=j, dst=dst):
                pr.ve("tensor_copy", dst[:, j * D:(j + 1) * D], pv, inc="v_tr")
            transpose_block(srcs, ev)
        wprep_marks[wi % 2] = pr.val("t_tr")

    pr.dma("gpsimd", gw_bf.rearrange("p (a e) -> p a e", e=NE),
           gw.rearrange("(a p) e -> p a e", p=P), "g_ld")
    pr.dma("gpsimd", gb_bf, gb, "g_ld")

    projps = [(i * 512, (i + 1) * 512) for i in range(4)]
    pp = {"i": 0}
    evac_cnt = {i: 0 for i in range(4)}
    chunk_end_tpj = []
    stg_store_mark = {i: 0 for i in range(4)}

    def proj_tile(lhs_rhs_fn, stgbuf, dsts, scale=None):
        bank = pp["i"] % 4
        pp["i"] += 1
        c0, c1 = projps[bank]
        pr.wait("tensor", "v_pa", evac_cnt[bank])
        for c in range(8):
            lhsT, rhs = lhs_rhs_fn(c)
            pr.mm(psv(c0, c1), lhsT, rhs, c == 0, c == 7,
                  inc="t_pj" if c == 7 else None)
        pr.wait("vector", "t_pj", pr.val("t_pj"))
        if scale is None:
            v2 = pr.ve("tensor_copy", stgbuf, psv(c0, c1), inc="v_pa")
        else:
            v2 = pr.ve("tensor_scalar_mul", stgbuf, psv(c0, c1), scale, inc="v_pa")
        evac_cnt[bank] = v2
        pr.wait("sync", "v_pa", v2)
        for dst_ap, src_ap in dsts:
            pr.dma("sync", dst_ap, src_ap, "s_st")

    def do_chunk(src_rows, ct0, emit_k, emit_v, emit_q, buf_i):
        ct_cols = slice(ct0, ct0 + 512)
        xb, xtb = xnat[buf_i], xT[buf_i]
        if len(chunk_end_tpj) >= 2:
            pr.wait("gpsimd", "t_pj", chunk_end_tpj[-2])
        ldv = pr.dma("gpsimd", xb.rearrange("p (a d) -> p a d", d=D),
                     src_rows.rearrange("(a p) d -> p a d", p=P), "g_ld")
        pr.wait("tensor", "g_ld", ldv)
        for tt in range(4):
            srcs = [xb[:, tt * D + j * P: tt * D + (j + 1) * P] for j in range(8)]
            def ev(pv, tt=tt, xtb=xtb):
                xv = xtb.rearrange("p (j c) -> p j c", c=512)[:, :, tt * P:(tt + 1) * P]
                pvv = pv.rearrange("p (j c) -> p j c", c=P)
                pr.ve("tensor_copy", xv, pvv, inc="v_tr")
            transpose_block(srcs, ev)
        pr.wait("tensor", "v_tr", pr.val("v_tr"))
        if emit_k:
            for j in range(8):
                si = j % 4
                sb = stg[si]
                pr.wait("vector", "s_st", stg_store_mark[si])
                proj_tile(lambda c, j=j, xtb=xtb: (wTk[:, c * D + j * P:c * D + (j + 1) * P],
                                                   xtb[:, c * 512:(c + 1) * 512]),
                          sb, [(kT_s[j * P:(j + 1) * P, ct_cols], sb)])
                stg_store_mark[si] = pr.val("s_st")
        if emit_v:
            for tt in range(4):
                for nb in range(2):
                    si = (tt * 2 + nb) % 4
                    sb = stg[si]
                    pr.wait("vector", "s_st", stg_store_mark[si])
                    tok0 = ct0 + tt * P
                    dsts = []
                    for pq in range(4):
                        pair = nb * 4 + pq
                        dsts.append((v_s[pair, tok0:tok0 + P, :],
                                     sb.rearrange("p (a c) -> p a c", c=P)[:, pq, :]))
                    proj_tile(lambda c, tt=tt, nb=nb, xtb=xtb: (
                        xtb[:, c * 512 + tt * P:c * 512 + (tt + 1) * P],
                        wTv[:, c * D + nb * 512:c * D + (nb + 1) * 512]),
                        sb, dsts)
                    stg_store_mark[si] = pr.val("s_st")
        if emit_q:
            for j in range(8):
                si = j % 4
                sb = stg[si]
                pr.wait("vector", "s_st", stg_store_mark[si])
                proj_tile(lambda c, j=j, xtb=xtb: (wTq[:, c * D + j * P:c * D + (j + 1) * P],
                                                   xtb[:, c * 512:(c + 1) * 512]),
                          sb, [(qT_s[j * P:(j + 1) * P, ct_cols], sb)],
                          scale=QSCALE)
                stg_store_mark[si] = pr.val("s_st")
        chunk_end_tpj.append(pr.val("t_pj"))

    if "A" in phases:
        for ct in range(8):
            do_chunk(xkv[ct * 512:(ct + 1) * 512, :], ct * 512,
                     emit_k=True, emit_v=True, emit_q=False, buf_i=ct % 2)
        for ct in range(4):
            do_chunk(xq[ct * 512:(ct + 1) * 512, :], ct * 512,
                     emit_k=False, emit_v=False, emit_q=True, buf_i=ct % 2)

    AR.pop()
    phaseA_store = pr.val("s_st")

    AR.push()
    KT = [AR.view(T, BF16) for _ in range(2)]
    Vq = [AR.view(32 * 130, BF16) for _ in range(2)]
    QTb = [AR.view(8 * QB, BF16) for _ in range(2)]
    attA = AR.view(16 * QB, BF16)
    attB = AR.view(16 * QB, BF16)
    O_sbU = AR.view(8 * QB, BF16)
    tmpB = AR.view(QB, BF16)
    rs_st = AR.view(2 * QB, F32)
    coll = AR.view(QB, F32)
    rcoll = AR.view(QB, F32)
    rstbf = AR.view(8 * QB, BF16)
    xres = [AR.view(D, F32) for _ in range(2)]
    xfst = [AR.view(D, F32) for _ in range(2)]
    xfn = [AR.view(D, F32) for _ in range(2)]
    xfbf = [AR.view(D, BF16) for _ in range(2)]
    sqd = AR.view(D, BF16)

    SCA = (0, 1536)
    SCB = (1536, 3072)
    POA = (3072, 3584)
    POB = (3584, 4096)
    BCA = (0, 512)
    BCB = (512, 1024)
    PU = (1536, 2560)

    if "ATT" in phases:
        ldv = pr.dma("gpsimd", bu_bf, bu, "g_ld")
        for i in range(2):
            pr.ve("memset", Vq[i], 1.0, inc="v_misc")
        vq_ready = pr.val("v_misc")

        GROUPS = [3, 3, 3, 3, 2, 2]
        reg_mark = {0: 0, 1: 0}
        pair_end_tav = []
        qb_end_marks = []
        stt_marks = []
        sxf_marks = []
        xfb_marks = []
        sq_marks = []

        for qb in range(NQB):
            qtb = QTb[qb % 2]
            if qb >= 2:
                pr.wait("sync", "t_sc", qb_end_marks[qb - 2]["t_sc"])
            pr.dma("sync",
                   qtb.rearrange("p (j c) -> p j c", c=QB),
                   qT_s.rearrange("(j p) q -> p j q", p=P)[:, :, qb * QB:(qb + 1) * QB],
                   "a_q")
            qtb_ld = pr.val("a_q")

            for p in range(8):
                gi = len(pair_end_tav)
                kt, vq = KT[gi % 2], Vq[gi % 2]
                if gi >= 2:
                    pr.wait("sync", "t_avh", pair_end_tav[gi - 2])
                pr.wait("sync", "v_misc", vq_ready)
                pr.dma("sync", kt, kT_s[p * P:(p + 1) * P, :], "a_kv")
                vsrc = v_s[p].rearrange("(s q) c -> q s c", q=P)
                vdst = vq.rearrange("p (s c) -> p s c", c=130)
                pr.dma("sync", vdst[:, :, 0:64], vsrc[:, :, 0:64], "a_kv")
                pr.dma("sync", vdst[:, :, 65:129], vsrc[:, :, 64:128], "a_kv")
                kv_ld = pr.val("a_kv")

                def emit_attv(item, kt=kt, vq=vq):
                    st0, glen, sexv = item
                    pr.wait("tensor", "s_ex", sexv)
                    for k in range(glen):
                        st = st0 + k
                        slot = (st % 16) * QB
                        if st == 0:
                            pr.wait("tensor", "v_oe", pr.val("v_oe"))
                            pr.wait("tensor", "v_tr", pr.val("v_tr"))
                        fin = (st % 16 == 15) or (st == NST - 1)
                        pr.mm(psv(*POA, 0, 65), vq[:, st * 130: st * 130 + 65],
                              attA[:, slot:slot + QB], st == 0, st == NST - 1)
                        pr.mm(psv(*POB, 0, 65), vq[:, st * 130 + 65: st * 130 + 130],
                              attB[:, slot:slot + QB], st == 0, st == NST - 1,
                              inc="t_avh" if fin else None)

                pend = []
                for half in range(2):
                    for g, glen in enumerate(GROUPS):
                        st0 = half * 16 + sum(GROUPS[:g])
                        if g == 0 and half == 0:
                            pr.wait("tensor", "a_kv", kv_ld)
                            pr.wait("tensor", "a_q", qtb_ld)
                            pr.wait("tensor", "v_no", pr.val("v_no"))
                            pr.wait("tensor", "v_u", pr.val("v_u"))
                        pr.wait("tensor", "s_ex", reg_mark[0])
                        tscA = tscB = None
                        for k in range(glen):
                            st = st0 + k
                            last = (k == glen - 1)
                            if k == 0:
                                pass
                            pr.mm(psv(*SCA)[:, k * QB:(k + 1) * QB],
                                  kt[0:64, st * P:(st + 1) * P],
                                  qtb[0:64, p * QB:(p + 1) * QB],
                                  True, True, inc="t_sc" if last else None,
                                  tile_position=(0, 0))
                            if k == 0:
                                tscA = pr.val("t_sc") + (0 if last else glen - 1)
                                pr.wait("tensor", "s_ex", reg_mark[1])
                            pr.mm(psv(*SCB)[:, k * QB:(k + 1) * QB],
                                  kt[64:128, st * P:(st + 1) * P],
                                  qtb[64:128, p * QB:(p + 1) * QB],
                                  True, True, inc="t_sc" if last else None,
                                  tile_position=(64, 0))
                        tscA = pr.val("t_sc") - 1
                        tscB = pr.val("t_sc")
                        if g == 0:
                            pr.wait("scalar", "t_avh", pr.val("t_avh"))
                        sl0 = (st0 % 16) * QB
                        pr.wait("scalar", "t_sc", tscA)
                        pr.se("activation", attA[:, sl0:sl0 + glen * QB],
                              psv(*SCA)[:, 0:glen * QB], AF.Exp, inc="s_ex")
                        reg_mark[0] = pr.val("s_ex")
                        pr.wait("scalar", "t_sc", tscB)
                        pr.se("activation", attB[:, sl0:sl0 + glen * QB],
                              psv(*SCB)[:, 0:glen * QB], AF.Exp, inc="s_ex")
                        reg_mark[1] = pr.val("s_ex")
                        pend.append((st0, glen, pr.val("s_ex")))
                        if len(pend) >= 2:
                            emit_attv(pend.pop(0))
                    while pend:
                        emit_attv(pend.pop(0))
                pair_end_tav.append(pr.val("t_avh"))

                pr.wait("vector", "t_avh", pair_end_tav[-1])
                if gi >= 1:
                    pr.wait("vector", "g_rs", pr.val("g_rs"))
                    pr.wait("vector", "g_mb", pr.val("g_mb"))
                pr.ve("tensor_copy", O_sbU[0:64, p * QB:(p + 1) * QB],
                      psv(*POA, 0, 65)[0:64, :], inc="v_oe")
                pr.ve("tensor_copy", rs_st[64:65, 0:QB], psv(*POA, 64, 65), inc="v_oe")
                pr.ve("tensor_copy", tmpB[0:64, :], psv(*POB, 0, 65)[0:64, :], inc="v_oe")
                pr.ve("tensor_copy", rs_st[64:65, QB:2 * QB], psv(*POB, 64, 65), inc="v_oe")
                voe = pr.val("v_oe")
                pr.wait("gpsimd", "v_oe", voe)
                pr.dma("gpsimd", coll[2 * p:2 * p + 2, :],
                       rs_st[64:65, :].rearrange("p (a c) -> p a c", c=QB), "g_rs")
                pr.dma("gpsimd", O_sbU[64:128, p * QB:(p + 1) * QB], tmpB[0:64, :], "g_mb")

            pr.wait("vector", "g_rs", pr.val("g_rs"))
            pr.ve("reciprocal", rcoll[0:16, :], coll[0:16, :], inc="v_rc")
            for hf in range(2):
                pr.wait("gpsimd", "v_rc", pr.val("v_rc"))
                pr.wait("gpsimd", "t_bc", pr.val("t_bc"))
                pr.dma("gpsimd",
                       rstbf[0:1, :].rearrange("p (a c) -> p a c", c=QB),
                       rcoll[8 * hf:8 * hf + 8, :], "g_sc")
                gscv = pr.val("g_sc")
                for p4 in range(4):
                    p = hf * 4 + p4
                    pr.wait("tensor", "g_sc", gscv)
                    if p == 0:
                        pr.wait("tensor", "s_ex", pr.val("s_ex"))
                    pr.wait("tensor", "v_no", pr.val("v_no"))
                    r0 = 2 * p4 * QB
                    pr.mm(psv(*BCA, 0, 64), ones1[0:1, 0:64], rstbf[0:1, r0:r0 + QB],
                          True, True, inc="t_bc", tile_position=(0, 0))
                    pr.mm(psv(*BCB, 64, 128), ones1[0:1, 0:64],
                          rstbf[0:1, r0 + QB:r0 + 2 * QB],
                          True, True, inc="t_bc", tile_position=(0, 64))
                    pr.wait("vector", "t_bc", pr.val("t_bc"))
                    pr.wait("vector", "g_mb", pr.val("g_mb"))
                    blk_ = O_sbU[0:64, p * QB:(p + 1) * QB]
                    pr.ve("tensor_mul", blk_, blk_, psv(*BCA, 0, 64), inc="v_no")
                    blk2 = O_sbU[64:128, p * QB:(p + 1) * QB]
                    pr.ve("tensor_mul", blk2, blk2, psv(*BCB, 64, 128), inc="v_no")

            if "dbg_rs" in debug_outs:
                pr.wait("sync", "v_rc", pr.val("v_rc"))
                pr.dma("sync", dbg_rs[qb], coll[0:16, :], "s_dbg")
            if "dbg_o" in debug_outs:
                pr.wait("sync", "v_no", pr.val("v_no"))
                pr.dma("sync", dbg_o[qb], O_sbU, "s_dbg")

            vno = pr.val("v_no")
            for tt in range(4):
                xr = xres[tt % 2]
                if len(stt_marks) >= 2:
                    pr.wait("sync", "v_u", stt_marks[-2])
                pr.dma("sync", xr, xq[qb * QB + tt * P: qb * QB + (tt + 1) * P, :], "s_xr")
                sxr = pr.val("s_xr")
                if tt == 0:
                    pr.wait("tensor", "v_no", vno)
                    if qb == 0:
                        pr.wait("tensor", "g_ld", ldv)
                if stt_marks:
                    pr.wait("tensor", "v_u", stt_marks[-1])
                for nb in range(2):
                    c0 = PU[0] + nb * QB
                    for j in range(8):
                        pr.mm(psv(c0, c0 + QB), O_sbU[:, j * QB + tt * P:j * QB + (tt + 1) * P],
                              wuT[:, j * D + nb * QB:j * D + (nb + 1) * QB],
                              j == 0, False)
                    pr.mm(psv(c0, c0 + QB), ones1[0:1, 0:P], bu_bf[0:1, nb * QB:(nb + 1) * QB],
                          False, True, inc="t_u" if nb == 1 else None)
                s_ = tt % 2
                pr.wait("vector", "t_u", pr.val("t_u"))
                pr.wait("vector", "s_xr", sxr)
                if len(sq_marks) >= 2:
                    pr.wait("vector", "s_u", sq_marks[-2])
                vstt = pr.ve("scalar_tensor_tensor", xfst[s_], psv(PU[0], PU[0] + D), 1.0, xr,
                             OP.mult, OP.add, accum_out=st_tmp[:, s_:s_ + 1], inc="v_u")
                stt_marks.append(vstt)
                if "dbg_u" in debug_outs:
                    pr.wait("sync", "v_u", pr.val("v_u"))
                    pr.dma("sync", dbg_u[qb, tt], xfst[s_], "s_dbg")
                pr.wait("scalar", "v_u", pr.val("v_u"))
                pr.se("activation", sqd, xfst[s_], AF.Square,
                      accum_out=st_sq[:, s_:s_ + 1])
                pr.se("activation", st_negm[:, s_:s_ + 1], st_tmp[:, s_:s_ + 1],
                      AF.Identity, scale=-1.0 / D)
                pr.se("activation", st_tmp[:, 2 + s_:3 + s_], st_negm[:, s_:s_ + 1],
                      AF.Square, inc="s_u")
                pr.wait("vector", "s_u", pr.val("s_u"))
                pr.ve("tensor_copy", st_tmp[:, 6 + s_:7 + s_], st_sq[:, s_:s_ + 1])
                pr.ve("tensor_scalar", st_tmp[:, 4 + s_:5 + s_], st_tmp[:, 2 + s_:3 + s_],
                      -1.0, EPS, OP.mult, OP.add, inc="v_u")
                pr.wait("scalar", "v_u", pr.val("v_u"))
                sq_marks.append(pr.se("activation", st_std[:, s_:s_ + 1],
                                      st_tmp[:, 6 + s_:7 + s_],
                                      AF.Sqrt, bias=st_tmp[:, 4 + s_:5 + s_], scale=1.0 / D,
                                      inc="s_u"))
                pr.wait("vector", "s_u", pr.val("s_u"))
                pr.ve("reciprocal", st_rstd[:, s_:s_ + 1], st_std[:, s_:s_ + 1])
                pr.drain("vector")
                if len(sxf_marks) >= 2:
                    pr.wait("vector", "s_xf", sxf_marks[-2])
                if len(xfb_marks) >= 2:
                    pr.wait("vector", "t_tr", xfb_marks[-2])
                pr.ve("tensor_scalar", xfn[s_], xfst[s_], st_negm[:, s_:s_ + 1],
                      st_rstd[:, s_:s_ + 1], OP.add, OP.mult, inc="v_u")
                pr.ve("tensor_copy", xfbf[s_], xfn[s_], inc="v_u")
                vu = pr.val("v_u")
                pr.wait("sync", "v_u", vu)
                pr.dma("sync", xf_s[qb * QB + tt * P: qb * QB + (tt + 1) * P, :],
                       xfn[s_], "s_xf")
                if "dbg_st" in debug_outs:
                    pr.dma("sync", dbg_st[qb, tt, :, 0:8], st_tmp, "s_dbg")
                    pr.dma("sync", dbg_st[qb, tt, :, 8:12], st_sq, "s_dbg")
                    pr.dma("sync", dbg_st[qb, tt, :, 12:16], st_std, "s_dbg")
                    pr.dma("sync", dbg_st[qb, tt, :, 16:20], st_rstd, "s_dbg")
                    pr.dma("sync", dbg_st[qb, tt, :, 20:24], st_negm, "s_dbg")
                pr.wait("tensor", "v_u", vu)
                srcs = [xfbf[s_][:, j * P:(j + 1) * P] for j in range(8)]
                def evx(pv, tt=tt, qb=qb):
                    xview = xfT.rearrange("p (j c) -> p j c", c=TQ)[
                        :, :, qb * QB + tt * P: qb * QB + (tt + 1) * P]
                    pvv = pv.rearrange("p (j c) -> p j c", c=P)
                    pr.ve("tensor_copy", xview, pvv, inc="v_tr")
                transpose_block(srcs, evx)
                sxf_marks.append(pr.val("s_xf"))
                xfb_marks.append(pr.val("t_tr"))
            qb_end_marks.append({"t_sc": pr.val("t_sc")})

    att_end = {"s_xf": pr.val("s_xf") if "ATT" in phases else 0}
    AR.pop()

    AR.push()
    wbuf = [AR.view(8 * D, BF16) for _ in range(3)]
    g1t = AR.view(D, F32)
    be1t = AR.view(D, F32)
    g2t = AR.view(D, F32)
    hbf = [AR.view(2 * D, BF16) for _ in range(2)]
    hTb = AR.view(2 * 8 * P, BF16)
    acc = AR.view(8 * D, F32)
    xfrd = [AR.view(D, F32)]
    t1b = [AR.view(D, F32) for _ in range(2)]
    u1b = [AR.view(D, F32)]
    outst = [AR.view(D, F32)]
    sqd2 = AR.view(D, BF16)
    b1r = [AR.view(D, BF16) for _ in range(2)]
    b2r = [AR.view(D, BF16) for _ in range(2)]
    be2sb = AR.view(D, BF16)
    gsc = AR.view(32, F32)
    gts = AR.view(6 * 8, F32)
    gtmp = AR.view(32, F32)
    st_sq2 = AR.view(12, F32)
    dbgz_sb = u1b[0]
    st_s2 = AR.view(8, F32)
    zsum_sb = AR.view(16, F32)

    ZR = 0
    OR_ = 2048

    if "FFN" in phases:
        att_done = {n: pr.val(n) for n in ("t_u", "t_tr", "v_u", "s_xf", "v_tr")}
        for q_ in ("gpsimd", "sync"):
            pr.wait(q_, "t_tr", att_done["t_tr"])
            pr.wait(q_, "t_u", att_done["t_u"])
            pr.wait(q_, "s_xf", att_done["s_xf"])
            pr.wait(q_, "v_u", att_done["v_u"])
            pr.wait(q_, "v_tr", att_done["v_tr"])

        def load_expert_w(e, which, buf):
            srct = w1 if which == 0 else w2
            return pr.dma("gpsimd", wbuf[buf].rearrange("p (a d) -> p a d", d=D),
                          srct[e].rearrange("(a p) d -> p a d", p=P), "f_w")

        pr.dma("gpsimd", be2sb[0:4, :], be2, "f_w")
        ldw = {}
        ldw[0] = [load_expert_w(0, 0, 0), load_expert_w(0, 1, 1)]
        ldw[1] = [load_expert_w(1, 0, 2), None]

        def ldg1(ge_l):
            e_l = ge_l % NE
            pr.dma("gpsimd", g1t, g1[e_l:e_l + 1, :].partition_broadcast(P), "f_g")
            return pr.dma("gpsimd", be1t, be1[e_l:e_l + 1, :].partition_broadcast(P), "f_g")

        def ldg2(ge_l):
            e_l = ge_l % NE
            return pr.dma("gpsimd", g2t, g2[e_l:e_l + 1, :].partition_broadcast(P), "f_g")

        z1_end = {}
        z2_end = {}
        stt1_marks = {0: [], 1: []}
        zsq_done = {0: 0, 1: 0}
        osq_r = {0: 0, 1: 0, 2: 0, 3: 0}
        o_free_r = {0: 0, 1: 0, 2: 0, 3: 0}
        o_apply_last = [0]
        pro_marks = {}
        h_ready = {}
        tr_done = {}
        ht_evac = {0: 0, 1: 0}
        ldg_marks = {}
        prologue_done = set()
        fin_sq_mark = [0]
        hbt = [hbf[k // 2][:, (k % 2) * 1024:(k % 2 + 1) * 1024] for k in range(4)]

        def prologue(ge_p):
            e_p = ge_p % NE
            slot = ge_p % 2
            if ge_p - 2 in z2_end:
                pr.wait("gpsimd", "t_z", z2_end[ge_p - 2])
            pr.dma("gpsimd", b1r[slot][0:1, :], b1[e_p:e_p + 1, :], "f_g")
            pro_marks[ge_p] = pr.dma("gpsimd", b2r[slot][0:1, :], b2[e_p:e_p + 1, :],
                                     "f_g")

        def z_mms(ge, tt, ch0_):
            wa = (2 * ge) % 3
            gslot = ge % 2
            par = ge % 2
            ut = 8 * ge + tt
            r = ut % 4
            R0 = 1024 * r
            pr.wait("tensor", "v_f", o_free_r[r])
            pr.wait("tensor", "s_f", osq_r[r])
            if tt == 0:
                pr.wait("tensor", "f_w", ldw[ge][0])
                pr.wait("tensor", "f_g", pro_marks[ge])
            for nb in range(2):
                c0 = R0 + nb * QB
                for c in range(8):
                    pr.mm(psv(c0, c0 + QB),
                          xfT[:, c * TQ + ch0_ + tt * P: c * TQ + ch0_ + (tt + 1) * P],
                          wbuf[wa][:, c * D + nb * QB:c * D + (nb + 1) * QB],
                          c == 0, False)
                pr.mm(psv(c0, c0 + QB), ones1[0:1, 0:P],
                      b1r[gslot][0:1, nb * QB:(nb + 1) * QB], False, True,
                      inc="t_z" if nb == 1 else None)
            if tt == 7:
                z1_end[ge] = pr.val("t_z")
                gn = ge + 1
                if gn < 8 and ldw[gn][1] is None:
                    pr.wait("gpsimd", "t_z", z1_end[ge])
                    ldw[gn][1] = load_expert_w(gn % NE, 1, (2 * gn + 1) % 3)

        def z_chain(ge, tt, ch0_):
            par = ge % 2
            ut = 8 * ge + tt
            r = ut % 4
            R0 = 1024 * r
            zs = ut % 2
            pr.wait("scalar", "t_z", pr.val("t_z"))
            if stt1_marks[zs]:
                pr.wait("scalar", "v_f", stt1_marks[zs][-1])
            pr.se("activation", sqd2, psv(R0, R0 + D), AF.Identity,
                  accum_out=st_s2[:, 2 + zs:3 + zs])
            pr.se("activation", sqd2, psv(R0, R0 + D), AF.Square,
                  accum_out=st_sq[:, zs:zs + 1], inc="s_f")
            pr.wait("vector", "s_f", pr.val("s_f"))
            pr.ve("tensor_copy", st_tmp[:, 2 + zs:3 + zs], st_s2[:, 2 + zs:3 + zs])
            pr.ve("tensor_copy", st_sq2[:, zs:zs + 1], st_sq[:, zs:zs + 1], inc="v_f")
            pr.wait("scalar", "v_f", pr.val("v_f"))
            pr.se("activation", st_negm[:, zs:zs + 1], st_tmp[:, 2 + zs:3 + zs],
                  AF.Identity, scale=-1.0 / D)
            pr.se("activation", st_tmp[:, zs:zs + 1], st_negm[:, zs:zs + 1], AF.Square,
                  inc="s_f")
            pr.wait("vector", "s_f", pr.val("s_f"))
            pr.ve("tensor_scalar", st_tmp[:, 4 + zs:5 + zs], st_tmp[:, zs:zs + 1],
                  -1.0, EPS, OP.mult, OP.add, inc="v_f")
            pr.wait("scalar", "v_f", pr.val("v_f"))
            pr.se("activation", st_std[:, zs:zs + 1], st_sq2[:, zs:zs + 1], AF.Sqrt,
                  bias=st_tmp[:, 4 + zs:5 + zs], scale=1.0 / D, inc="s_f")
            zsq_done[zs] = pr.val("s_f")
            pr.wait("vector", "s_f", pr.val("s_f"))
            pr.ve("reciprocal", st_rstd[:, zs:zs + 1], st_std[:, zs:zs + 1])
            pr.drain("vector")
            hp = hbt[ut % 4]
            t1 = t1b[ut % 2]
            pr.ve("scalar_tensor_tensor", t1, psv(R0, R0 + D),
                  st_negm[:, zs:zs + 1], g1t, OP.add, OP.mult, inc="v_f")
            pr.ve("scalar_tensor_tensor", t1, t1, st_rstd[:, zs:zs + 1],
                  be1t, OP.mult, OP.add)
            pr.wait("vector", "t_tr", tr_done.get(ut - 4, 0))
            pr.ve("tensor_scalar_max", hp, t1, 0.0, inc="v_f")
            stt1_marks[zs].append(pr.val("v_f"))
            h_ready[(ge, tt)] = pr.val("v_f")

        def o_front(ge, tt, ch0_):
            gslot = ge % 2
            wb = (2 * ge + 1) % 3
            ut = 8 * ge + tt
            r = ut % 4
            R0 = 1024 * r
            hh = ut % 2
            pr.wait("tensor", "v_f", h_ready[(ge, tt)])
            pr.wait("tensor", "s_f", zsq_done[ut % 2])
            hp = hbt[ut % 4]
            for j in range(8):
                pr.tr(psbf(R0, R0 + 512)[:, j * P:(j + 1) * P],
                      hp[:, j * P:(j + 1) * P], ident,
                      inc="t_tr" if j == 7 else None)
            tr_done[ut] = pr.val("t_tr")
            pr.wait("scalar", "t_tr", tr_done[ut])
            pr.se("activation", hTb[:, hh * 1024:(hh + 1) * 1024], psbf(R0, R0 + 512),
                  AF.Identity, inc="s_h")
            ht_evac[hh] = pr.val("s_h")
            pr.wait("tensor", "s_h", ht_evac[hh])
            if tt == 0:
                pr.wait("tensor", "f_g", ldg_marks[ge][1])
                pr.wait("tensor", "f_w", ldw[ge][1])
            if ge == 0 and tt == 0 and "dbg_h" in debug_outs:
                pr.wait("sync", "v_f", h_ready[(ge, tt)])
                pr.wait("sync", "s_h", ht_evac[hh])
                pr.dma("sync", dbg_h[0], hTb[:, hh * 1024:(hh + 1) * 1024], "s_dbg")
                pr.dma("sync", dbg_h[1], hp, "s_dbg")
            for nb in range(2):
                c0 = R0 + nb * QB
                for j in range(8):
                    pr.mm(psv(c0, c0 + QB),
                          hTb[:, hh * 1024 + j * P: hh * 1024 + (j + 1) * P],
                          wbuf[wb][:, j * D + nb * QB:j * D + (nb + 1) * QB],
                          j == 0, False)
                pr.mm(psv(c0, c0 + QB), ones1[0:1, 0:P],
                      b2r[gslot][0:1, nb * QB:(nb + 1) * QB], False, True,
                      inc="t_z" if nb == 1 else None)
            if tt == 7:
                z2_end[ge] = pr.val("t_z")

        def o_back(ge, tt, ch0_):
            e_o = ge % NE
            ut = 8 * ge + tt
            r = ut % 4
            R0 = 1024 * r
            os_ = 2 + ut % 2
            pr.wait("scalar", "t_z", pr.val("t_z"))
            pr.wait("scalar", "v_f", o_free_r[(r + 2) % 4])
            pr.se("activation", sqd2, psv(R0, R0 + D), AF.Identity,
                  accum_out=st_sq[:, 4 + ut % 2:5 + ut % 2])
            pr.se("activation", sqd2, psv(R0, R0 + D), AF.Square,
                  accum_out=st_s2[:, ut % 2:1 + ut % 2], inc="s_f")
            pr.wait("vector", "s_f", pr.val("s_f"))
            pr.ve("tensor_copy", st_tmp[:, 8 + ut % 2:9 + ut % 2],
                  st_sq[:, 4 + ut % 2:5 + ut % 2])
            pr.ve("tensor_copy", st_sq2[:, os_:os_ + 1], st_s2[:, ut % 2:1 + ut % 2],
                  inc="v_f")
            pr.wait("scalar", "v_f", pr.val("v_f"))
            pr.se("activation", st_negm[:, os_:os_ + 1], st_tmp[:, 8 + ut % 2:9 + ut % 2],
                  AF.Identity, scale=-1.0 / D)
            pr.se("activation", st_tmp[:, 12 + ut % 2:13 + ut % 2],
                  st_negm[:, os_:os_ + 1], AF.Square, inc="s_f")
            osq_r[r] = pr.val("s_f")
            pr.wait("vector", "s_f", pr.val("s_f"))
            pr.ve("tensor_scalar", st_tmp[:, 16 + ut % 2:17 + ut % 2],
                  st_tmp[:, 12 + ut % 2:13 + ut % 2], -1.0, EPS, OP.mult, OP.add,
                  inc="v_f")
            pr.wait("scalar", "v_f", pr.val("v_f"))
            pr.se("activation", st_std[:, os_:os_ + 1], st_sq2[:, os_:os_ + 1], AF.Sqrt,
                  bias=st_tmp[:, 16 + ut % 2:17 + ut % 2], scale=1.0 / D, inc="s_f")
            osq_r[r] = pr.val("s_f")
            pr.wait("vector", "s_f", pr.val("s_f"))
            pr.ve("reciprocal", st_rstd[:, os_:os_ + 1], st_std[:, os_:os_ + 1])
            pr.drain("vector")
            u1 = u1b[0]
            pr.ve("scalar_tensor_tensor", u1, psv(R0, R0 + D),
                  st_negm[:, os_:os_ + 1], g2t, OP.add, OP.mult, inc="v_f")
            o_free_r[r] = pr.val("v_f")
            pr.ve("tensor_scalar_mul", st_rstd[:, 6 + ut % 2:7 + ut % 2],
                  st_rstd[:, os_:os_ + 1],
                  wgate[:, tt * NE + e_o: tt * NE + e_o + 1])
            pr.drain("vector")
            ab_ = acc[:, tt * D:(tt + 1) * D]
            pr.ve("scalar_tensor_tensor", ab_, u1, st_rstd[:, 6 + ut % 2:7 + ut % 2],
                  ab_, OP.mult, OP.add, inc="v_f")
            o_apply_last[0] = pr.val("v_f")
            if tt == 7 and "dbg_acc" in debug_outs and ge < NE:
                pr.wait("sync", "v_f", pr.val("v_f"))
                pr.dma("sync", dbg_acc[1 + e_o], acc, "s_dbg")

        pr.wait("tensor", "v_u", pr.val("v_u"))
        pr.wait("tensor", "v_no", pr.val("v_no"))
        pr.wait("tensor", "v_oe", pr.val("v_oe"))
        pr.wait("tensor", "s_ex", pr.val("s_ex"))

        def gate_scores(ch):
            ch0 = ch * 8 * P
            pr.wait("tensor", "v_tr", pr.val("v_tr"))
            pr.wait("tensor", "g_ld", pr.val("g_ld"))
            pr.wait("tensor", "v_g", pr.val("v_g"))
            pr.wait("tensor", "v_f", o_free_r[0])
            pr.wait("tensor", "s_f", osq_r[0])
            for tt in range(8):
                for c in range(8):
                    pr.mm(psv(ZR + 4 * tt, ZR + 4 * tt + 4),
                          xfT[:, c * TQ + ch0 + tt * P: c * TQ + ch0 + (tt + 1) * P],
                          gw_bf[:, c * NE:(c + 1) * NE], c == 0, False)
                pr.mm(psv(ZR + 4 * tt, ZR + 4 * tt + 4), ones1[0:1, 0:P],
                      gb_bf[0:1, :], False, True, inc="t_g" if tt == 7 else None)
            pr.wait("vector", "t_g", pr.val("t_g"))
            pr.ve("tensor_copy", gsc, psv(ZR, ZR + 32), inc="v_g")
            pr.drain("vector")
            g3 = gsc.rearrange("p (a e) -> p a e", e=NE)
            ga, gb_, gc, gd = (gts[:, 0:8], gts[:, 8:16], gts[:, 16:24], gts[:, 24:32])
            gm1v, gm2v = gts[:, 32:40], gts[:, 40:48]
            pr.ve("tensor_max", ga, g3[:, :, 0], g3[:, :, 1])
            pr.ve("tensor_tensor", gb_, g3[:, :, 0], g3[:, :, 1], OP.min)
            pr.ve("tensor_max", gc, g3[:, :, 2], g3[:, :, 3])
            pr.ve("tensor_tensor", gd, g3[:, :, 2], g3[:, :, 3], OP.min)
            pr.drain("vector")
            pr.ve("tensor_max", gm1v, ga, gc)
            pr.ve("tensor_tensor", ga, ga, gc, OP.min)
            pr.ve("tensor_max", gb_, gb_, gd)
            pr.drain("vector")
            pr.ve("tensor_max", gm2v, ga, gb_)
            pr.drain("vector")
            m1b = gm1v.unsqueeze(2).broadcast_to([P, 8, NE])
            m2b = gm2v.unsqueeze(2).broadcast_to([P, 8, NE])
            gt3 = gtmp.rearrange("p (a e) -> p a e", e=NE)
            pr.ve("tensor_sub", gt3, g3, m1b)
            pr.ve("tensor_sub", gd, gm2v, gm1v, inc="v_g")
            pr.drain("vector")
            pr.wait("scalar", "v_g", pr.val("v_g"))
            pr.se("activation", wgate, gtmp, AF.Exp)
            pr.se("activation", gts[:, 0:8], gd, AF.Exp, inc="s_g")
            pr.wait("vector", "s_g", pr.val("s_g"))
            pr.ve("tensor_scalar_add", gts[:, 16:24], gts[:, 0:8], 1.0)
            pr.drain("vector")
            pr.ve("reciprocal", gts[:, 8:16], gts[:, 16:24])
            pr.drain("vector")
            pr.ve("tensor_tensor", gt3, g3, m2b, OP.is_ge)
            pr.drain("vector")
            w3 = wgate.rearrange("p (a e) -> p a e", e=NE)
            pr.ve("tensor_mul", w3, w3, gt3)
            pr.drain("vector")
            rdb = gts[:, 8:16].unsqueeze(2).broadcast_to([P, 8, NE])
            pr.ve("tensor_mul", w3, w3, rdb, inc="v_g")
            pr.drain("vector")
            pr.ve("tensor_copy", wgate_bf, wgate, inc="v_g")
            if "dbg_w" in debug_outs:
                pr.wait("sync", "v_g", pr.val("v_g"))
                pr.dma("sync", dbg_w[ch], wgate, "s_dbg")
            for tt in range(8):
                pr.wait("tensor", "v_g", pr.val("v_g"))
                pr.wait("tensor", "v_tr", pr.val("v_tr"))
                if ch == 1 and tt == 0:
                    pr.wait("tensor", "v_f", o_free_r[2])
                    pr.wait("tensor", "s_f", osq_r[2])
                pr.tr(psbf(OR_, OR_ + 1024)[0:4, tt * P:(tt + 1) * P],
                      wgate_bf[:, tt * NE:(tt + 1) * NE], ident,
                      inc="t_tr" if tt == 7 else None)
            pr.wait("vector", "t_tr", pr.val("t_tr"))
            for tt in range(8):
                pr.ve("tensor_copy", wTg[tt][0:4, :],
                      psbf(OR_, OR_ + 1024)[0:4, tt * P:(tt + 1) * P],
                      inc="v_tr" if tt == 7 else None)

        def gate_seeds(ch):
            pr.wait("tensor", "f_w", 16)
            for tt in range(8):
                pr.wait("tensor", "v_tr", pr.val("v_tr"))
                if tt > 0:
                    pr.wait("tensor", "v_f", pr.val("v_f"))
                for nb in range(2):
                    pr.mm(psv(OR_ + nb * QB, OR_ + (nb + 1) * QB),
                          wTg[tt][0:4, :], be2sb[0:4, nb * QB:(nb + 1) * QB],
                          True, True, inc="t_z" if nb == 1 else None)
                pr.wait("vector", "t_z", pr.val("t_z"))
                pr.ve("tensor_copy", acc[:, tt * D:(tt + 1) * D], psv(OR_, OR_ + D),
                      inc="v_f")
            o_free_r[2] = pr.val("v_f")
            o_apply_last[0] = pr.val("v_f")

        def experts_chunk(ch):
            ch0 = ch * 8 * P
            punits = [(ch * NE + e, t) for e in range(NE) for t in range(8)]
            LAGT = 2
            for i in range(len(punits) + LAGT):
                if i < len(punits):
                    ge_i, tt_i = punits[i]
                    if tt_i == 0:
                        if ge_i >= 1:
                            pr.wait("gpsimd", "v_f", h_ready.get((ge_i - 1, 7), 0))
                        ldg_marks[ge_i] = [ldg1(ge_i), None]
                        if ge_i not in prologue_done:
                            prologue(ge_i)
                            prologue_done.add(ge_i)
                    if tt_i == 4 and ge_i + 1 < 2 * NE and (ge_i + 1) not in prologue_done:
                        prologue(ge_i + 1)
                        prologue_done.add(ge_i + 1)
                    z_mms(ge_i, tt_i, ch0)
                if i >= LAGT:
                    ge_o, tt_o = punits[i - LAGT]
                    if tt_o == 0:
                        if ge_o >= 1:
                            pr.wait("gpsimd", "v_f", o_apply_last[0])
                        ldg_marks[ge_o][1] = ldg2(ge_o)
                    o_front(ge_o, tt_o, ch0)
                if i < len(punits):
                    z_chain(ge_i, tt_i, ch0)
                if i >= LAGT:
                    o_back(ge_o, tt_o, ch0)
                    if tt_o == 7:
                        gn = ge_o + 2
                        if gn < 8 and gn not in ldw:
                            pr.wait("gpsimd", "t_z", z2_end[ge_o])
                            ldw[gn] = [load_expert_w(gn % NE, 0, (2 * gn) % 3), None]

        def final_chunk(ch):
            ch0 = ch * 8 * P
            if "dbg_acc" in debug_outs and ch == 0:
                pr.wait("sync", "v_f", pr.val("v_f"))
                pr.wait("sync", "s_f", pr.val("s_f"))
                pr.dma("sync", dbg_acc[0], acc, "s_dbg")
            fin_marks = []
            for tt in range(8):
                xr2 = xfrd[0]
                if len(fin_marks) >= 1:
                    pr.wait("sync", "v_f", fin_marks[-1])
                pr.dma("sync", xr2, xf_s[ch0 + tt * P: ch0 + (tt + 1) * P, :], "s_x2")
                sx2 = pr.val("s_x2")
                pr.wait("vector", "s_x2", sx2)
                pr.wait("vector", "s_f", fin_sq_mark[0])
                k = tt % 2
                fm = pr.ve("scalar_tensor_tensor", t1b[k], xr2, 2.0,
                           acc[:, tt * D:(tt + 1) * D],
                           OP.mult, OP.add, accum_out=st_tmp[:, 20 + k:21 + k], inc="v_f")
                fin_marks.append(fm)
                pr.wait("scalar", "v_f", pr.val("v_f"))
                pr.se("activation", sqd2, t1b[k], AF.Square,
                      accum_out=st_sq[:, 8 + k:9 + k])
                pr.se("activation", st_negm[:, 8 + k:9 + k], st_tmp[:, 20 + k:21 + k],
                      AF.Identity, scale=-1.0 / D)
                pr.se("activation", st_tmp[:, 22 + k:23 + k], st_negm[:, 8 + k:9 + k],
                      AF.Square, inc="s_f")
                pr.wait("vector", "s_f", pr.val("s_f"))
                pr.ve("tensor_copy", st_sq2[:, 8 + k:9 + k], st_sq[:, 8 + k:9 + k])
                pr.ve("tensor_scalar", st_tmp[:, 24 + k:25 + k], st_tmp[:, 22 + k:23 + k],
                      -1.0, EPS, OP.mult, OP.add, inc="v_f")
                pr.wait("scalar", "v_f", pr.val("v_f"))
                pr.se("activation", st_std[:, 8 + k:9 + k], st_sq2[:, 8 + k:9 + k],
                      AF.Sqrt, bias=st_tmp[:, 24 + k:25 + k], scale=1.0 / D, inc="s_f")
                fin_sq_mark[0] = pr.val("s_f")
                pr.wait("vector", "s_f", pr.val("s_f"))
                pr.ve("reciprocal", st_rstd[:, 10 + k:11 + k], st_std[:, 8 + k:9 + k])
                pr.drain("vector")
                pr.wait("vector", "s_out", pr.val("s_out"))
                pr.ve("tensor_scalar", outst[0], t1b[k], st_negm[:, 8 + k:9 + k],
                      st_rstd[:, 10 + k:11 + k], OP.add, OP.mult, inc="v_f")
                vo = pr.val("v_f")
                pr.wait("sync", "v_f", vo)
                pr.dma("sync", out[ch0 + tt * P: ch0 + (tt + 1) * P, :], outst[0],
                       "s_out")

        gate_scores(0)
        gate_seeds(0)
        experts_chunk(0)
        gate_scores(1)
        final_chunk(0)
        gate_seeds(1)
        experts_chunk(1)
        final_chunk(1)

        pr.wait("sync", "s_out", pr.val("s_out"))

    AR.pop()

    if "FFN" not in phases:
        AR.push()
        ostub = AR.view(D, F32)
        pr.wait("vector", "s_st", phaseA_store)
        pr.ve("memset", ostub, 0.0, inc="v_misc")
        pr.wait("sync", "v_misc", pr.val("v_misc"))
        for i in range(TQ // P):
            pr.dma("sync", out[i * P:(i + 1) * P, :], ostub, "s_out")
        pr.wait("sync", "s_out", pr.val("s_out"))
        AR.pop()

    pr.run()
    return nc


def kernel(L0_attn_x, L0_attn_Wq, L0_attn_Wk, L0_attn_Wv, L0_attn_Wu, L0_attn_bu,
           L0_ffn_gate_W, L0_ffn_gate_b, L0_ffn_W1, L0_ffn_b1, L0_ffn_g1, L0_ffn_be1,
           L0_ffn_W2, L0_ffn_b2, L0_ffn_g2, L0_ffn_be2):
    x = np.ascontiguousarray(np.asarray(L0_attn_x, dtype=np.float32))
    B = x.shape[0]
    nc = build()
    res = run_bass_kernel_spmd(nc, make_in_maps(x, locals()), core_ids=list(range(8)))
    outp = np.empty((B, T, D), dtype=np.float32)
    for c in range(8):
        b, qh = c // 2, c % 2
        outp[b, qh * TQ:(qh + 1) * TQ] = res.results[c]["out"]
    return outp


def make_in_maps(x, env):
    eye = np.eye(P, dtype=np.float32)
    f = np.float32
    maps = []
    for c in range(8):
        b, qh = c // 2, c % 2
        maps.append({
            "xkv": x[b],
            "xq": np.ascontiguousarray(x[b, qh * TQ:(qh + 1) * TQ]),
            "wq": np.asarray(env["L0_attn_Wq"], f),
            "wk": np.asarray(env["L0_attn_Wk"], f),
            "wv": np.asarray(env["L0_attn_Wv"], f),
            "wu": np.asarray(env["L0_attn_Wu"], f),
            "bu": np.asarray(env["L0_attn_bu"], f).reshape(1, D),
            "gw": np.asarray(env["L0_ffn_gate_W"], f),
            "gb": np.asarray(env["L0_ffn_gate_b"], f).reshape(1, NE),
            "w1": np.asarray(env["L0_ffn_W1"], f),
            "b1": np.asarray(env["L0_ffn_b1"], f),
            "g1": np.asarray(env["L0_ffn_g1"], f),
            "be1": np.asarray(env["L0_ffn_be1"], f),
            "w2": np.asarray(env["L0_ffn_W2"], f),
            "b2": np.asarray(env["L0_ffn_b2"], f),
            "g2": np.asarray(env["L0_ffn_g2"], f),
            "be2": np.asarray(env["L0_ffn_be2"], f),
            "ident_in": eye,
        })
    return maps
